# version 11
# baseline (speedup 1.0000x reference)
import math
import numpy as np
import ml_dtypes
from contextlib import ExitStack

import concourse.bass as bass
import concourse.mybir as mybir
from concourse.bass_utils import run_bass_kernel_spmd
from concourse.aot_env import allow_ici_all_to_all

F32 = mybir.dt.float32
BF16 = mybir.dt.bfloat16
AF = mybir.ActivationFunctionType
ALU = mybir.AluOpType
AXX = mybir.AxisListType.X

B = 2
SEQ = 4096
H = 2048
NH = 16
HD = 128
TOK = 1024
NTC = 8
NFC = 16
LAT = 512
EPS = 1e-5
SCALE_A = 1.0 / HD
SCALE_B = 1.0 / math.sqrt(HD)
bfnp = ml_dtypes.bfloat16


def _alibi_slopes(n):
    start = 2.0 ** (-2.0 ** (-(math.log2(n) - 3)))
    return np.array([start * start ** i for i in range(n)], np.float64)


def _rope_cos_sin(seq_len, dim, base=10000.0):
    inv_freq = 1.0 / base ** (np.arange(0, dim, 2, dtype=np.float64) / dim)
    t = np.arange(seq_len, dtype=np.float64)
    freqs = np.outer(t, inv_freq)
    emb = np.concatenate([freqs, freqs], -1)
    return np.cos(emb), np.sin(emb)


class Sem:
    __slots__ = ("h", "n")

    def __init__(self, handle):
        self.h = handle
        self.n = 0


class Plan:
    def __init__(self):
        self.q = {e: [] for e in ("t", "v", "s", "g", "y")}

    def add(self, eng, fn, real=False):
        fn.real = real
        self.q[eng].append(fn)

    def wait(self, eng, sem, val):
        if val > 0:
            w = lambda e, _h=sem.h, _v=val: e.wait_ge(_h, _v)
            w.real = False
            self.q[eng].append(w)

    def inc_last(self, eng, sem, k=1):
        if self.q[eng] and getattr(self.q[eng][-1], "real", False):
            prev = self.q[eng].pop()
            def wrapped(e, _p=prev, _h=sem.h, _k=k):
                inst = _p(e)
                inst.then_inc(_h, _k)
            wrapped.real = False
            self.q[eng].append(wrapped)
        else:
            f = lambda e, _h=sem.h, _k=k: e.drain().then_inc(_h, _k)
            f.real = False
            self.q[eng].append(f)
        sem.n += k
        return sem.n


def build_program(upto=99):
    nc = bass.Bass("TRN2", target_bir_lowering=False, debug=False)

    def din(name, shape, dt=F32):
        return nc.dram_tensor(name, shape, dt, kind="ExternalInput").ap()

    x_in = din("x_slice", [TOK, H])
    wqkv = din("wqkv_p", [H, 3 * H])
    wproj = din("wproj_p", [H, H])
    wo = din("wo_p", [H, H])
    wqd = din("wqd_t", [H, LAT])
    wkd = din("wkd_t", [H, LAT])
    sgw = din("shared_gw_ti", [H, 5632])
    sdw = din("shared_dw_t", [2816, H])
    upw = din("up_w_ti", [8, H, 2816])
    dnw = din("down_w_t", [8, 1408, H])
    gwt = din("gate_w_t", [H, 8])
    wqu_s = din("wqu_s", [LAT, 256])
    wku_s = din("wku_s", [LAT, 256])
    wvu_s = din("wvu_s", [LAT, 256])
    bias_misc = din("bias_misc", [128, 80])
    gbias_in = din("gate_bias_p", [8, 1])
    ident_in = din("ident_in", [128, 128])
    onesr_in = din("ones_row", [1, 128])
    onesc_in = din("ones_colb", [128, 1], BF16)
    onehot_in = din("onehot8", [8, 1024], BF16)
    slope_in = din("slope_row2", [1, 256])
    kposc_in = din("kposc", [128, 64])
    qpos_in = din("qpos_bf", [1, SEQ], BF16)
    diag_in = din("diag_masks", [128, 2048], BF16)
    pa_in = din("ropePA", [128, SEQ])
    pb_in = din("ropePB", [128, SEQ])

    out_t = nc.dram_tensor("out_slice", [TOK, H], F32, kind="ExternalOutput").ap()

    cc_ak_in = nc.dram_tensor("cc_ak_in", [2048, TOK], BF16)
    cc_ak_out = nc.dram_tensor("cc_ak_out", [2048, TOK], BF16)
    cc_aq_in = [nc.dram_tensor(f"cc_aq_in{i}", [1024, TOK], BF16) for i in range(2)]
    cc_aq_out = [nc.dram_tensor(f"cc_aq_out{i}", [1024, TOK], BF16) for i in range(2)]
    cc_av_in = nc.dram_tensor("cc_av_in", [8 * TOK, 256], BF16)
    cc_av_out = nc.dram_tensor("cc_av_out", [8 * TOK, 256], BF16)
    cc_ao_in = [nc.dram_tensor(f"cc_ao_in{i}", [8 * 128, TOK], BF16) for i in range(2)]
    cc_ao_out = [nc.dram_tensor(f"cc_ao_out{i}", [8 * 128, TOK], BF16) for i in range(2)]
    cc_bq_in = nc.dram_tensor("cc_bq_in", [512, TOK], BF16)
    cc_bq_out = nc.dram_tensor("cc_bq_out", [8 * 512, TOK], BF16)
    cc_bkv_in = nc.dram_tensor("cc_bkv_in", [512, TOK], BF16)
    cc_bkv_out = nc.dram_tensor("cc_bkv_out", [8 * 512, TOK], BF16)
    cc_bo_in = [nc.dram_tensor(f"cc_bo_in{i}", [8 * 128, TOK], BF16) for i in range(2)]
    cc_bo_out = [nc.dram_tensor(f"cc_bo_out{i}", [8 * 128, TOK], BF16) for i in range(2)]

    MC_BPROJ = 0
    MC_BO = 16
    MC_BVA = 32
    MC_QKB = 48
    MC_BQD = 34
    MC_BKD = 38
    MC_BQU = 42
    MC_BKU = 44
    MC_BVU = 46

    with ExitStack() as stack:
        E = stack.enter_context

        af = E(nc.sbuf_tensor("arena_f32", [128, 22016], F32))
        X_ = af[:, 0:16384]
        dscr = af[:, 16384:17408]
        weB = dscr
        mBh = af[:, 17408:17920]
        rBh = af[:, 17920:18432]
        sumac = af[:, 18432:19456]
        recB = af[:, 19456:19968]
        otmp = af[:, 19968:20992]
        sum2 = af[:, 20992:22016]
        xstgA = af[:, 16384:17408]
        xstgB = af[:, 17408:18432]

        ab = E(nc.sbuf_tensor("arena_b16", [128, 51200], BF16))
        XN = ab[:, 0:16384]
        WS = ab[:, 16384:40960]
        WT = ab[:, 40960:49152]
        PT = ab[:, 49152:51200]

        misc = E(nc.sbuf_tensor("misc_f32", [128, 16], F32))
        MC_ONES = 0
        MC_BEFFQ = 2
        MC_BEFFK = 4
        MC_BEFFV = 6
        gate = E(nc.sbuf_tensor("gate_f32", [8, 2064], F32))
        G_S, G_BS, G_GB = 0, 1024, 2048
        srowA = af[0:1, 16384:16896]
        srowB = af[0:1, 16896:17408]
        srowC = af[0:1, 19456:19968]
        topk = E(nc.sbuf_tensor("topk_f32", [128, 64], F32))
        w8sb = E(nc.sbuf_tensor("w8sb_b16", [8, 1024], BF16))
        onehot = E(nc.sbuf_tensor("onehot_b16", [8, 1024], BF16))
        ident = E(nc.sbuf_tensor("ident_f32", [128, 128], F32))
        onesr = E(nc.sbuf_tensor("onesr_f32", [1, 128], F32))
        onescb = E(nc.sbuf_tensor("onescb_b16", [128, 1], BF16))
        onesrb = E(nc.sbuf_tensor("onesrb_b16", [1, 128], BF16))
        slopes = E(nc.sbuf_tensor("slopes_f32", [1, 256], F32))
        upslc = E(nc.sbuf_tensor("upslc_b16", [128, 3072], BF16))
        gwt_s = E(nc.sbuf_tensor("gwt_b16", [128, 128], BF16))
        bmisc = E(nc.sbuf_tensor("bmisc_f32", [128, 80], F32))
        bqdb = E(nc.sbuf_tensor("bqdb_b16", [128, 8], BF16))
        kposc = E(nc.sbuf_tensor("kposc_f32", [128, 64], F32))

        PP = [E(nc.psum_tensor(f"pp{i}", [128, 1024], F32)) for i in range(4)]

        S = {nm: Sem(E(nc.semaphore(nm))) for nm in
             ["ld", "lx0", "lx1", "w0", "w1", "wf", "mm", "ev", "es", "cc", "st",
              "px", "sum", "gp", "pb0", "pb1", "pb2", "pb3", "pb4", "pb5", "pb6",
              "pb7"]}
        pb = [S[f"pb{i}"] for i in range(8)]
        pbu = [0] * 8

        block = E(nc.Block())
        P = Plan()

        def ph(i):
            return PP[i // 2][:, (i % 2) * 512:(i % 2) * 512 + 512]

        def pe_acq(i):
            if pbu[i] > 0:
                P.wait("t", pb[i], pbu[i])
            pbu[i] += 1

        def REL(eng, i):
            P.inc_last(eng, pb[i], 1)

        def MM(out, lhsT, rhs, start, stop):
            P.add("t", lambda t, o=out, l=lhsT, r=rhs, st=start, sp=stop:
                  t.matmul(o, l, r, start=st, stop=sp), real=True)

        def TTR(out, in_, np_=128):
            P.add("t", lambda t, o=out, i=in_, n=np_: t.transpose(o, i, ident[0:n, 0:n]),
                  real=True)

        def TINC(sem=None, k=1):
            return P.inc_last("t", sem or S["mm"], k)

        def VINC(sem=None, k=1):
            return P.inc_last("v", sem or S["ev"], k)

        def SINC(sem=None, k=1):
            return P.inc_last("s", sem or S["es"], k)

        def V(meth, *args):
            def f(v, m=meth, a=args):
                v.drain()
                return getattr(v, m)(*a)
            P.add("v", f, real=True)

        VD = V

        def SDR():
            P.add("s", lambda s: s.drain())

        def SC(meth, *args, **kw):
            P.add("s", lambda s, m=meth, a=args, k=kw: getattr(s, m)(*a, **k))

        def SACT(out, in_, func, bias=0.0, scale=1.0, accum=None):
            if accum is None:
                def f1(s, o=out, i=in_, fn=func, b=bias, sc=scale):
                    s.drain()
                    return s.activation(out=o, in_=i, func=fn, bias=b, scale=sc)
                P.add("s", f1, real=True)
            else:
                def f2(s, o=out, i=in_, fn=func, b=bias, sc=scale, ac=accum):
                    s.drain()
                    return s.activation(out=o, in_=i, func=fn, bias=b, scale=sc,
                                        accum_out=ac)
                P.add("s", f2, real=True)

        def SACT_ND(out, in_, func, bias=0.0, scale=1.0):
            P.add("s", lambda s, o=out, i=in_, fn=func, b=bias, sc=scale:
                  s.activation(out=o, in_=i, func=fn, bias=b, scale=sc), real=True)

        def SYD(out, in_, transpose=False, sem=None):
            sm = sem or S["ld"]
            P.add("y", lambda y, o=out, i=in_, tr=transpose, _h=sm.h:
                  y.dma_start(out=o, in_=i, transpose=tr).then_inc(_h, 16))
            sm.n += 16
            return sm.n

        def STD(out, in_, chain=0):
            def _f(y, o=out, i=in_, ch=chain):
                if ch > 0:
                    y.wait_ge(S["st"].h, ch)
                y.dma_start(out=o, in_=i).then_inc(S["st"].h, 16)
            P.add("y", _f)
            S["st"].n += 16
            return S["st"].n

        def GPD(out, in_, sem=None):
            sm = sem or S["gp"]
            P.add("g", lambda g, o=out, i=in_, _h=sm.h:
                  g.dma_start(out=o, in_=i).then_inc(_h, 16))
            sm.n += 16
            return sm.n

        def COLL(kind, ins, outs):
            def _do(g, k=kind, i=ins, o=outs):
                with allow_ici_all_to_all():
                    g.collective_compute(k, ALU.bypass,
                                         replica_groups=[list(range(8))],
                                         ins=[i.ap().opt()], outs=[o.ap().opt()]
                                         ).then_inc(S["cc"].h, 1)
            P.add("g", _do)
            S["cc"].n += 1
            return S["cc"].n

        wstate = {"idx": 0}
        slab_free_mm = {}

        def SLAB(ap2d, nf, osz):
            si = wstate["idx"]
            wstate["idx"] += 1
            par = si % 2
            wsem = S[f"w{par}"]
            sv = WT[:, par * 4096:par * 4096 + nf * osz]
            prev = si - 2
            def _dma(g, dr=ap2d, s2=sv, oz=osz, pv=prev, wh=wsem.h):
                if pv >= 0:
                    g.wait_ge(S["mm"].h, slab_free_mm[pv])
                g.dma_start(out=s2.rearrange("p (c o) -> p c o", o=oz),
                            in_=dr.rearrange("(c p) o -> p c o", p=128)
                            ).then_inc(wh, 16)
            P.add("g", _dma)
            wsem.n += 16
            return sv, (wsem, wsem.n)

        def SLAB_DONE():
            slab_free_mm[len(slab_free_mm)] = S["mm"].n

        GPD(ident[:, :], ident_in)
        GPD(onesr[:, :], onesr_in)
        GPD(onescb[:, :], onesc_in)
        GPD(slopes[:, :], slope_in)
        GPD(onehot[:, :], onehot_in)
        GPD(bmisc[:, :], bias_misc)
        GPD(gate[0:8, G_GB:G_GB + 1], gbias_in)
        GPD(WS[0:1, 8192:12288], qpos_in)
        GPD(WS[:, 12288:14336], diag_in)
        GPD(gwt_s[:, :].rearrange("p (c o) -> p c o", o=8),
            gwt.rearrange("(c p) o -> p c o", p=128))
        GPD(upslc[:, 0:1024].rearrange("p (c o) -> p c o", o=256),
            wqu_s.rearrange("(c p) o -> p c o", p=128))
        GPD(upslc[:, 1024:2048].rearrange("p (c o) -> p c o", o=256),
            wku_s.rearrange("(c p) o -> p c o", p=128))
        GPD(upslc[:, 2048:3072].rearrange("p (c o) -> p c o", o=256),
            wvu_s.rearrange("(c p) o -> p c o", p=128))
        GPD(bqdb[:, 0:4], bias_misc[:, MC_BQD:MC_BQD + 4])
        GPD(bqdb[:, 4:8], bias_misc[:, MC_BKD:MC_BKD + 4])
        CONST = GPD(kposc[:, :], kposc_in)

        P.wait("v", S["gp"], CONST)
        V("memset", misc[:, MC_ONES:MC_ONES + 1], 1.0)
        V("memset", onesrb[:, :], 1.0)
        V("tensor_scalar", bmisc[:, MC_QKB:MC_QKB + 16],
          bmisc[:, MC_QKB:MC_QKB + 16], SCALE_A, None, ALU.mult)

        P.wait("t", S["gp"], CONST)
        xmv = {}
        xfc1 = []
        for i in range(NTC * 2):
            hf, tc = i // 8, i % 8
            stg = xstgA if i % 2 == 0 else xstgB
            if i >= 2:
                P.wait("y", S["mm"], xmv[i - 2])
            lv = SYD(stg, x_in[tc * 128:(tc + 1) * 128, hf * 1024:(hf + 1) * 1024],
                     sem=S[f"lx{i % 2}"])
            P.wait("t", S[f"lx{i % 2}"], lv)
            for f8 in range(8):
                fc = hf * 8 + f8
                hp = 4 + (f8 % 2)
                pe_acq(hp)
                TTR(ph(hp)[:, 0:128], stg[:, f8 * 128:f8 * 128 + 128])
                mv = TINC()
                P.wait("v", S["mm"], mv)
                V("tensor_copy",
                  X_[:, fc * 1024 + tc * 128: fc * 1024 + tc * 128 + 128],
                  ph(hp)[:, 0:128])
                REL("v", hp)
            xmv[i] = mv
            if tc == 7:
                hev = VINC()
                xfc1 += [hev] * 8
        X_EV = VINC()

        for woff, bcol, ocol, scl in ((0, MC_BQU, MC_BEFFQ, SCALE_B),
                                      (1024, MC_BKU, MC_BEFFK, 1.0),
                                      (2048, MC_BVU, MC_BEFFV, 1.0)):
            bsrc = 0 if woff == 0 else (4 if woff == 1024 else 4)
            for blk in range(2):
                pe_acq(6)
                P.wait("t", S["gp"], CONST)
                for l in range(4):
                    MM(ph(6)[:, 0:1],
                       upslc[:, woff + l * 256 + blk * 128: woff + l * 256 + blk * 128 + 128],
                       bqdb[:, bsrc + l: bsrc + l + 1], l == 0, l == 3)
                mv = TINC()
                P.wait("v", S["mm"], mv)
                V("scalar_tensor_tensor", misc[:, ocol + blk:ocol + blk + 1],
                  ph(6)[:, 0:1], 1.0, bmisc[:, bcol + blk:bcol + blk + 1],
                  ALU.mult, ALU.add)
                if scl != 1.0:
                    VD("tensor_scalar", misc[:, ocol + blk:ocol + blk + 1],
                       misc[:, ocol + blk:ocol + blk + 1], scl, None, ALU.mult)
                REL("v", 6)
        BEFF_EV = VINC()

        def emit_ln(with_gate, xfc_ev=None, xn_guard_mm=0, sq_guard_mm=0):
            entry_ev = S["ev"].n
            if sq_guard_mm:
                P.wait("s", S["mm"], sq_guard_mm)
            for fc in range(NFC):
                P.wait("s", S["ev"], xfc_ev[fc] if xfc_ev else entry_ev)
                SACT_ND(WS[:, fc * 512:(fc + 1) * 512],
                        X_[:, fc * 1024:fc * 1024 + 512], AF.Square)
                SACT_ND(WS[:, 14336 + fc * 512:14336 + (fc + 1) * 512],
                        X_[:, fc * 1024 + 512:(fc + 1) * 1024], AF.Square)
            cmk = {}
            if xn_guard_mm:
                P.wait("s", S["mm"], xn_guard_mm)
            for fc in range(NFC):
                SACT_ND(XN[:, fc * 1024:(fc + 1) * 1024],
                        X_[:, fc * 1024:(fc + 1) * 1024], AF.Copy)
                cmk[fc] = SINC()
            for th in range(2):
                sqb = 0 if th == 0 else 14336
                pe_acq(6)
                pe_acq(7)
                for fc in range(NFC):
                    P.wait("t", S["es"], cmk[fc])
                    MM(PP[3][0:1, 0:512], onescb[:, :],
                       XN[:, fc * 1024 + th * 512: fc * 1024 + th * 512 + 512],
                       fc == 0, fc == NFC - 1)
                    MM(PP[3][0:1, 512:1024], onescb[:, :],
                       WS[:, sqb + fc * 512:sqb + (fc + 1) * 512],
                       fc == 0, fc == NFC - 1)
                mv = TINC()
                P.wait("v", S["mm"], mv)
                V("tensor_scalar", srowA, PP[3][0:1, 0:512], 1.0 / H, None, ALU.mult)
                REL("v", 6)
                VD("tensor_mul", srowC, srowA, srowA)
                VD("scalar_tensor_tensor", srowB, PP[3][0:1, 512:1024],
                   1.0 / H, srowC, ALU.mult, ALU.subtract)
                REL("v", 7)
                VD("tensor_scalar", srowB, srowB, EPS, None, ALU.add)
                var_ev = VINC()
                P.wait("s", S["ev"], var_ev)
                SACT(srowB, srowB, AF.Ln)
                SDR()
                SACT(srowB, srowB, AF.Exp, scale=-0.5)
                rstd_ev = SINC()
                pe_acq(6)
                pe_acq(7)
                P.wait("t", S["es"], rstd_ev)
                MM(ph(6), onesr[:, :], srowA, True, True)
                MM(ph(7), onesr[:, :], srowB, True, True)
                mv = TINC()
                P.wait("v", S["mm"], mv)
                V("tensor_copy", mBh[:, :], ph(6))
                REL("v", 6)
                V("tensor_copy", rBh[:, :], ph(7))
                REL("v", 7)
                scrf = af[:, 18432:21504]
                def _sub(fc, th=th):
                    sl0 = fc * 1024 + th * 512
                    (VD if fc == 0 else V)("tensor_sub",
                        scrf[:, (fc % 6) * 512:(fc % 6) * 512 + 512],
                        X_[:, sl0:sl0 + 512], mBh[:, :])
                def _mul(fc, th=th):
                    sl0 = fc * 1024 + th * 512
                    (VD if fc == 0 else V)("tensor_mul", XN[:, sl0:sl0 + 512],
                        scrf[:, (fc % 6) * 512:(fc % 6) * 512 + 512], rBh[:, :])
                for fc in range(NFC):
                    _sub(fc)
                    if fc >= 3:
                        _mul(fc - 3)
                for fc in range(NFC - 3, NFC):
                    _mul(fc)
            apply_ev = VINC()
            if with_gate:
                pe_acq(6)
                pe_acq(7)
                P.wait("t", S["ev"], apply_ev)
                for fc in range(NFC):
                    for th in range(2):
                        MM(PP[3][0:8, th * 512:th * 512 + 512],
                           gwt_s[:, fc * 8:(fc + 1) * 8],
                           XN[:, fc * 1024 + th * 512: fc * 1024 + th * 512 + 512],
                           fc == 0, fc == NFC - 1)
                mv = TINC()
                P.wait("s", S["mm"], mv)
                SACT(gate[0:8, G_S:G_S + 512], PP[3][0:8, 0:512], AF.Sigmoid)
                REL("s", 6)
                SACT(gate[0:8, G_S + 512:G_S + 1024], PP[3][0:8, 512:1024], AF.Sigmoid)
                REL("s", 7)
                gate_pb = (pb[6].n, pb[7].n)
                return apply_ev, gate_pb
            return apply_ev, None

        def emit_topk(gate_pb):
            P.wait("v", pb[6], gate_pb[0])
            P.wait("v", pb[7], gate_pb[1])
            V("tensor_scalar", gate[0:8, G_BS:G_BS + 1024], gate[0:8, G_S:G_S + 1024],
              gate[0:8, G_GB:G_GB + 1], None, ALU.add)
            bs_ev = VINC()
            for tc in range(NTC):
                pe_acq(6)
                P.wait("t", S["ev"], bs_ev)
                TTR(ph(6)[:, 0:8], gate[0:8, G_BS + tc * 128:G_BS + tc * 128 + 128], np_=8)
                TTR(ph(6)[:, 8:16], gate[0:8, G_S + tc * 128:G_S + tc * 128 + 128], np_=8)
                mv = TINC()
                P.wait("v", S["mm"], mv)
                V("tensor_copy", topk[:, 0:8], ph(6)[:, 0:8])
                V("tensor_copy", topk[:, 8:16], ph(6)[:, 8:16])
                REL("v", 6)
                VD("reduce_max", topk[:, 32:33], topk[:, 0:8], AXX)
                VD("tensor_single_scalar", topk[:, 16:24], topk[:, 0:8],
                   topk[:, 32:33], ALU.is_equal)
                VD("scalar_tensor_tensor", topk[:, 24:32], topk[:, 16:24],
                   -1e30, topk[:, 0:8], ALU.mult, ALU.add)
                VD("reduce_max", topk[:, 33:34], topk[:, 24:32], AXX)
                VD("tensor_single_scalar", topk[:, 24:32], topk[:, 24:32],
                   topk[:, 33:34], ALU.is_equal)
                VD("tensor_add", topk[:, 16:24], topk[:, 16:24], topk[:, 24:32])
                VD("tensor_mul", topk[:, 8:16], topk[:, 8:16], topk[:, 16:24])
                VD("reduce_sum", topk[:, 34:35], topk[:, 8:16], AXX)
                VD("reciprocal", topk[:, 35:36], topk[:, 34:35])
                VD("tensor_single_scalar", topk[:, 40:48], topk[:, 8:16],
                   topk[:, 35:36], ALU.mult)
                we_ev = VINC()
                pe_acq(7)
                P.wait("t", S["ev"], we_ev)
                TTR(ph(7)[0:8, 0:128], topk[:, 40:48])
                mv2 = TINC()
                P.wait("v", S["mm"], mv2)
                V("tensor_copy", w8sb[0:8, tc * 128:(tc + 1) * 128], ph(7)[0:8, 0:128])
                REL("v", 7)
            return VINC()

        def emit_proj_residual2(wdram, rhs16, bias_base, rhs_ev, rhs_ev_b=None):
            xev = {}
            for sl in range(8):
                sv, wv = SLAB(wdram[:, sl * 256:(sl + 1) * 256], 16, 256)
                P.wait("t", wv[0], wv[1])
                P.wait("t", S["ev"], rhs_ev)
                for oc2 in range(2):
                    oc = sl * 2 + oc2
                    for th in range(2):
                        hp = (oc2 * 2 + th) % 4
                        pe_acq(hp)
                        for fc in range(NFC):
                            if fc == 8 and rhs_ev_b is not None and sl == 0:
                                P.wait("t", S["ev"], rhs_ev_b)
                            MM(ph(hp),
                               sv[:, fc * 256 + oc2 * 128: fc * 256 + oc2 * 128 + 128],
                               rhs16[:, fc * 1024 + th * 512: fc * 1024 + th * 512 + 512],
                               fc == 0, fc == NFC - 1)
                        mv = TINC()
                        P.wait("v", S["mm"], mv)
                        xsl = oc * 1024 + th * 512
                        op = ((oc * 2 + th) % 2) * 512
                        V("tensor_scalar", otmp[:, op:op + 512], ph(hp),
                          bmisc[:, bias_base + oc:bias_base + oc + 1], None, ALU.add)
                        REL("v", hp)
                        VD("tensor_add", X_[:, xsl:xsl + 512], X_[:, xsl:xsl + 512],
                           otmp[:, op:op + 512])
                        if th == 1:
                            xev[oc] = VINC()
                SLAB_DONE()
            return [xev[oc] for oc in range(16)], S["mm"].n

        def emit_attn_tail(variant, qt, Ostag, obias_ap, sumv_last):
            oh = 4 + (qt % 2)
            pe_acq(6)
            P.wait("t", S["sum"], sumv_last)
            sq = (qt % 2) * 512
            MM(ph(6)[0:1, 0:512], misc[:, MC_ONES:MC_ONES + 1],
               sumac[:, sq:sq + 512], True, False)
            MM(ph(6)[0:1, 0:512], misc[:, MC_ONES:MC_ONES + 1],
               sum2[:, sq:sq + 512], False, True)
            mv = TINC()
            P.wait("v", S["mm"], mv)
            V("reciprocal", srowA, ph(6)[0:1, 0:512])
            REL("v", 6)
            rec_pb6 = pb[6].n
            pe_acq(7)
            P.wait("t", pb[6], rec_pb6)
            MM(ph(7), onesr[:, :], srowA, True, True)
            mv2 = TINC()
            P.wait("v", S["mm"], mv2)
            V("tensor_copy", recB[:, :], ph(7))
            REL("v", 7)
            ot = (qt % 2) * 512
            VD("tensor_mul", otmp[:, ot:ot + 512], ph(oh), recB[:, :])
            REL("v", oh)
            VD("tensor_scalar", Ostag[:, qt * 512:(qt + 1) * 512],
               otmp[:, ot:ot + 512], obias_ap, None, ALU.add)
            return VINC()

        def emit_attn_a_unit(Qv, Kv, Vtile, Ostag, obias_ap, slot, rrow,
                             load_val, load_sem):
            sched = []
            for qt in range(8):
                nkc = 4 * (qt + 1)
                for kc in range(nkc):
                    sched.append((qt, kc, nkc))
            njj = len(sched)
            expv = {}
            avv = {}
            sumv = {}
            tail_cast = {}
            LAG = 3

            def emit_S(j):
                qt, kc, nkc = sched[j]
                sb = j % 4
                if pbu[sb] > 0:
                    P.wait("t", pb[sb], pbu[sb])
                pbu[sb] += 1
                if j == 0 and load_val is not None:
                    P.wait("t", load_sem, load_val)
                MM(ph(sb), Kv[:, kc * 128:(kc + 1) * 128],
                   Qv[:, qt * 512:(qt + 1) * 512], True, False)
                MM(ph(sb), onesrb[:, :], rrow[0:1, qt * 512:(qt + 1) * 512],
                   False, True)
                smm = TINC()
                if kc >= 4 * qt:
                    dvar = kc - 4 * qt
                    P.wait("v", S["mm"], smm)
                    VD("tensor_add", ph(sb), ph(sb),
                       WS[:, 12288 + dvar * 512:12288 + (dvar + 1) * 512])
                    sev = VINC()
                    P.wait("s", S["ev"], sev)
                else:
                    P.wait("s", S["mm"], smm)
                p_i = j % 4
                if j >= 4:
                    P.wait("s", S["px"], avv[j - 4])
                    P.wait("s", S["sum"], sumv[j - 4])
                SACT_ND(PT[:, p_i * 512:(p_i + 1) * 512], ph(sb), AF.Exp,
                        bias=kposc[:, kc * 2 + slot:kc * 2 + slot + 1])
                expv[j] = P.inc_last("s", pb[sb], 1)
                P.wait("v", pb[sb], expv[j])
                sbuf_acc = sumac if kc % 2 == 0 else sum2
                sq = (qt % 2) * 512
                if kc < 2:
                    V("tensor_copy", sbuf_acc[:, sq:sq + 512],
                      PT[:, p_i * 512:(p_i + 1) * 512])
                else:
                    V("tensor_add", sbuf_acc[:, sq:sq + 512], sbuf_acc[:, sq:sq + 512],
                      PT[:, p_i * 512:(p_i + 1) * 512])
                sumv[j] = P.inc_last("v", S["sum"], 1)

            def emit_AV(j):
                qt, kc, nkc = sched[j]
                p_i = j % 4
                oh = 4 + (qt % 2)
                if kc == 0:
                    pe_acq(oh)
                P.wait("t", pb[j % 4], expv[j])
                MM(ph(oh), Vtile(kc), PT[:, p_i * 512:(p_i + 1) * 512],
                   kc == 0, kc == nkc - 1)
                avv[j] = P.inc_last("t", S["px"], 1)
                if kc == nkc - 1:
                    tail_cast[qt] = emit_attn_tail("a", qt, Ostag, obias_ap, sumv[j])

            for j in range(njj):
                emit_S(j)
                if j >= LAG:
                    emit_AV(j - LAG)
            for j in range(njj - LAG, njj):
                emit_AV(j)
            return tail_cast[7]

        def emit_attn_b_unit(Qv, Kv, Vtile, Ostag, obias_ap, load_val, load_sem):
            pairs = []
            for qt in range(8):
                for kp in range(16):
                    pairs.append((qt, 2 * kp, 32))
            npj = len(pairs)
            pexp = {}
            avv = {}
            sumv = {}
            tail_cast = {}
            PLAG = 1

            def emit_Spair(jp):
                qt, kc0, nkc = pairs[jp]
                pp = jp % 2
                g = pb[2 * pp]
                if pbu[2 * pp] > 0:
                    P.wait("t", g, pbu[2 * pp])
                pbu[2 * pp] += 1
                pbu[2 * pp + 1] += 0
                if jp == 0 and load_val is not None:
                    P.wait("t", load_sem, load_val)
                for h in range(2):
                    kc = kc0 + h
                    MM(ph(2 * pp + h), Kv[:, kc * 128:(kc + 1) * 128],
                       Qv[:, qt * 512:(qt + 1) * 512], True, True)
                smm = TINC()
                P.wait("s", S["mm"], smm)
                if jp >= 2:
                    jlast_old = chunk_index(jp - 2) + 1
                    P.wait("s", S["px"], avv[jlast_old])
                    P.wait("s", S["sum"], sumv[jlast_old])
                SACT_ND(PT[:, (jp % 2) * 1024:(jp % 2) * 1024 + 1024],
                        PP[pp][:, 0:1024], AF.Exp)
                pexp[jp] = P.inc_last("s", g, 1)
                P.wait("v", g, pexp[jp])
                for h in range(2):
                    kc = kc0 + h
                    j = chunk_index(jp) + h
                    sbuf_acc = sumac if kc % 2 == 0 else sum2
                    sq = (qt % 2) * 512
                    pt_off = (jp % 2) * 1024 + h * 512
                    if kc < 2:
                        V("tensor_copy", sbuf_acc[:, sq:sq + 512],
                          PT[:, pt_off:pt_off + 512])
                    else:
                        V("tensor_add", sbuf_acc[:, sq:sq + 512],
                          sbuf_acc[:, sq:sq + 512], PT[:, pt_off:pt_off + 512])
                    sumv[j] = P.inc_last("v", S["sum"], 1)

            def chunk_index(jp):
                return 2 * jp

            def emit_AVpair(jp):
                qt, kc0, nkc = pairs[jp]
                pp = jp % 2
                oh = 4 + (qt % 2)
                for h in range(2):
                    kc = kc0 + h
                    j = chunk_index(jp) + h
                    if kc == 0:
                        pe_acq(oh)
                    P.wait("t", pb[2 * pp], pexp[jp])
                    pt_off = (jp % 2) * 1024 + h * 512
                    MM(ph(oh), Vtile(kc), PT[:, pt_off:pt_off + 512],
                       kc == 0, kc == nkc - 1)
                    avv[j] = P.inc_last("t", S["px"], 1)
                    if kc == nkc - 1:
                        tail_cast[qt] = emit_attn_tail("b", qt, Ostag, obias_ap,
                                                       sumv[j])

            for jp in range(npj):
                emit_Spair(jp)
                if jp >= PLAG:
                    emit_AVpair(jp - PLAG)
            for jp in range(npj - PLAG, npj):
                emit_AVpair(jp)
            return tail_cast[7]

        P.wait("v", S["ev"], X_EV)
        ln1_ev, _ = emit_ln(with_gate=False, xfc_ev=xfc1)

        piece_st = []
        SLORD = [(d, p) for p in (1, 2, 0) for d in range(8)]
        for sl in range(24):
            d, part = SLORD[sl]
            sv, wv = SLAB(wqkv[:, (d * 3 + part) * 256:(d * 3 + part) * 256 + 256],
                          16, 256)
            pbv = WS[:, (sl % 2) * 2048:(sl % 2) * 2048 + 2048]
            if sl >= 2:
                P.wait("v", S["st"], piece_st[sl - 2])
            P.wait("t", wv[0], wv[1])
            P.wait("t", S["ev"], ln1_ev)
            if part != 2:
                mcb = MC_QKB + (16 if part == 1 else 0) + d * 2
                sc = SCALE_A if part == 0 else 1.0
                for half in range(2):
                    for tc2 in range(2):
                        hp = (half * 2 + tc2) % 4
                        pe_acq(hp)
                        for fc in range(NFC):
                            MM(ph(hp),
                               sv[:, fc * 256 + half * 128: fc * 256 + half * 128 + 128],
                               XN[:, fc * 1024 + tc2 * 512: fc * 1024 + tc2 * 512 + 512],
                               fc == 0, fc == NFC - 1)
                        mv = TINC()
                        P.wait("v", S["mm"], mv)
                        V("tensor_scalar",
                          pbv[:, half * 1024 + tc2 * 512: half * 1024 + tc2 * 512 + 512],
                          ph(hp), sc, bmisc[:, mcb + half:mcb + half + 1],
                          ALU.mult, ALU.add)
                        REL("v", hp)
                pev = VINC()
                SLAB_DONE()
                P.wait("y", S["ev"], pev)
                stv = piece_st[-1] if piece_st else 0
                for half in range(2):
                    if part == 1:
                        dst = cc_ak_in[d * 256 + half * 128:d * 256 + half * 128 + 128, :]
                    else:
                        dst = cc_aq_in[half][d * 128:(d + 1) * 128, :]
                    stv = STD(dst, pbv[:, half * 1024:half * 1024 + 1024], chain=stv)
                piece_st.append(stv)
            else:
                for tc in range(NTC):
                    hp = tc % 4
                    pe_acq(hp)
                    for fc in range(NFC):
                        MM(ph(hp)[:, 0:256],
                           XN[:, fc * 1024 + tc * 128: fc * 1024 + tc * 128 + 128],
                           sv[:, fc * 256:(fc + 1) * 256], fc == 0, fc == NFC - 1)
                    mv = TINC()
                    P.wait("v", S["mm"], mv)
                    V("tensor_copy", pbv[:, tc * 256:(tc + 1) * 256], ph(hp)[:, 0:256])
                    REL("v", hp)
                pev = VINC()
                SLAB_DONE()
                P.wait("y", S["ev"], pev)
                stv = STD(cc_av_in[d * 1024:(d + 1) * 1024, 0:256].rearrange(
                              "(tc p) o -> p tc o", p=128),
                          pbv.rearrange("p (tc o) -> p tc o", o=256),
                          chain=piece_st[-1] if piece_st else 0)
                piece_st.append(stv)
            if sl == 9:
                P.add("g", lambda g, v=piece_st[7]: g.wait_ge(S["st"].h, v))
                cc1k = COLL("AllToAll", cc_ak_in, cc_ak_out)
            if sl == 17:
                P.add("g", lambda g, v=piece_st[15]: g.wait_ge(S["st"].h, v))
                cc1v = COLL("AllToAll", cc_av_in, cc_av_out)
        P.add("g", lambda g: g.wait_ge(S["st"].h, piece_st[-1]))
        cc1q = [COLL("AllToAll", cc_aq_in[i], cc_aq_out[i]) for i in range(2)]
        qkv_mm_end = S["mm"].n

        P.wait("y", S["mm"], qkv_mm_end)
        ostore = []
        px_end = []

        def emit_a_loads(u):
            beta, slot = u % 2, u // 2
            buf = XN if u % 2 == 0 else WS
            vb = 8192 if u % 2 == 0 else 20480
            lsem = S[f"lx{u % 2}"]
            if u == 0:
                P.wait("y", S["cc"], cc1k)
            for j in range(4):
                r = (beta * 4 + j) * 256 + slot * 128
                SYD(buf[:, 4096 + j * 1024:4096 + (j + 1) * 1024],
                    cc_ak_out[r:r + 128, :], sem=lsem)
            if u == 0:
                P.wait("y", S["cc"], cc1v)
            SYD(buf[:, vb:vb + 4096].rearrange("p (kc h) -> p kc h", h=128),
                cc_av_out[beta * 4096:(beta + 1) * 4096,
                          slot * 128:slot * 128 + 128].rearrange(
                              "(kc p) h -> p kc h", p=128), sem=lsem)
            P.wait("y", S["cc"], cc1q[slot])
            lv = None
            for j in range(4):
                r = (beta * 4 + j) * 128
                lv = SYD(buf[:, j * 1024:(j + 1) * 1024],
                         cc_aq_out[slot][r:r + 128, :], sem=lsem)
            return lv

        lv_next = emit_a_loads(0)
        for u in range(4):
            beta, slot = u % 2, u // 2
            lv = lv_next
            buf = XN if u % 2 == 0 else WS
            vb = 8192 if u % 2 == 0 else 20480
            obias = bmisc[:, MC_BVA + slot:MC_BVA + slot + 1]
            rrow = WS[0:1, 16384:20480]
            P.wait("v", S["px"], S["px"].n)
            V("tensor_scalar", rrow, WS[0:1, 8192:12288],
              slopes[0:1, slot * 128:slot * 128 + 1], -1.0, ALU.mult, ALU.mult)
            rr_ev = VINC()
            P.wait("t", S["ev"], rr_ev)
            cast_last = emit_attn_a_unit(
                buf[:, 0:4096], buf[:, 4096:8192],
                lambda kc, b=buf, vo=vb: b[:, vo + kc * 128:vo + (kc + 1) * 128],
                XN[:, 12288:16384], obias, slot, rrow, lv, S[f"lx{u % 2}"])
            px_end.append(S["px"].n)
            if u < 3:
                if u >= 1:
                    P.wait("y", S["px"], px_end[u - 1])
                lv_next = emit_a_loads(u + 1)
            P.wait("y", S["ev"], cast_last)
            for d2 in range(4):
                dd = beta * 4 + d2
                stv = STD(cc_ao_in[slot][dd * 128:(dd + 1) * 128, :],
                          XN[:, 12288 + d2 * 1024:12288 + (d2 + 1) * 1024])
                ostore.append(stv)
            P.wait("v", S["st"], ostore[-1])
            if u == 1:
                P.add("g", lambda g, v=ostore[-1]: g.wait_ge(S["st"].h, v))
                cc2a = COLL("AllToAll", cc_ao_in[0], cc_ao_out[0])
        P.add("g", lambda g: g.wait_ge(S["st"].h, ostore[-1]))
        cc2b = COLL("AllToAll", cc_ao_in[1], cc_ao_out[1])

        P.wait("y", S["cc"], cc2a)
        olv = None
        oev = {}
        P.wait("y", S["px"], px_end[2])
        for fc in range(NFC):
            if fc == 8:
                P.wait("y", S["ld"], olv)
                P.wait("v", S["ld"], olv)
                oev[0] = VINC()
                P.wait("y", S["cc"], cc2b)
            olv = SYD(XN[:, fc * 1024:(fc + 1) * 1024],
                      cc_ao_out[fc // 8][(fc % 8) * 128:(fc % 8 + 1) * 128, :])
        P.wait("v", S["ld"], olv)
        oev[1] = VINC()
        xev_pa, pa_mm = emit_proj_residual2(wproj, XN, MC_BPROJ, oev[0], oev[1])

        ln2_ev, gate_pb2 = emit_ln(with_gate=True, xfc_ev=xev_pa,
                                   xn_guard_mm=pa_mm)
        topk_ev = emit_topk(gate_pb2)

        def emit_ffn(ln_ev, tk_ev, out_cb=None):
            hmul_ev = {0: 0, 1: 0}
            for j in range(22):
                sv, wv = SLAB(sgw[:, j * 256:(j + 1) * 256], 16, 256)
                P.wait("t", wv[0], wv[1])
                P.wait("t", S["ev"], ln_ev)
                pp = j % 2
                for half, pr in ((0, 0), (1, 1)):
                    pe_acq(pr * 2)
                    pe_acq(pr * 2 + 1)
                    for th in range(2):
                        for fc in range(NFC):
                            MM(PP[pr][:, th * 512:th * 512 + 512],
                               sv[:, fc * 256 + half * 128: fc * 256 + half * 128 + 128],
                               XN[:, fc * 1024 + th * 512: fc * 1024 + th * 512 + 512],
                               fc == 0, fc == NFC - 1)
                    TINC()
                mvy = S["mm"].n - 1
                mvg = S["mm"].n
                P.wait("v", S["mm"], mvy)
                V("tensor_copy", WS[:, j * 1024:j * 1024 + 512], PP[0][:, 0:512])
                REL("v", 0)
                V("tensor_copy", WS[:, j * 1024 + 512:(j + 1) * 1024], PP[0][:, 512:1024])
                REL("v", 1)
                P.wait("s", S["mm"], mvg)
                if hmul_ev[pp] > 0:
                    P.wait("s", S["ev"], hmul_ev[pp])
                SACT(WS[:, 22528 + pp * 1024:22528 + pp * 1024 + 512],
                     PP[1][:, 0:512], AF.Silu)
                REL("s", 2)
                SACT(WS[:, 22528 + pp * 1024 + 512:22528 + (pp + 1) * 1024],
                     PP[1][:, 512:1024], AF.Silu)
                REL("s", 3)
                gse_pb = pb[3].n
                P.wait("v", pb[3], gse_pb)
                VD("tensor_mul", WS[:, j * 1024:(j + 1) * 1024],
                   WS[:, j * 1024:(j + 1) * 1024],
                   WS[:, 22528 + pp * 1024:22528 + (pp + 1) * 1024])
                hmul_ev[pp] = VINC()
                SLAB_DONE()
            hsh_ev = VINC()
            for cs in range(8):
                svs = []
                for part in range(2):
                    sv, wv = SLAB(sdw[part * 1408:(part + 1) * 1408,
                                      cs * 256:(cs + 1) * 256], 11, 256)
                    svs.append((sv, wv))
                P.wait("t", svs[0][1][0], svs[0][1][1])
                P.wait("t", svs[1][1][0], svs[1][1][1])
                P.wait("t", S["ev"], hsh_ev)
                for oc2 in range(2):
                    oc = cs * 2 + oc2
                    for th in range(2):
                        hp = (oc2 * 2 + th) % 4
                        pe_acq(hp)
                        for part in range(2):
                            for l in range(11):
                                j = part * 11 + l
                                MM(ph(hp),
                                   svs[part][0][:, l * 256 + oc2 * 128: l * 256 + oc2 * 128 + 128],
                                   WS[:, j * 1024 + th * 512: j * 1024 + th * 512 + 512],
                                   part == 0 and l == 0, part == 1 and l == 10)
                        mv = TINC()
                        P.wait("v", S["mm"], mv)
                        xsl = oc * 1024 + th * 512
                        V("tensor_add", X_[:, xsl:xsl + 512], X_[:, xsl:xsl + 512], ph(hp))
                        REL("v", hp)
                SLAB_DONE()
                SLAB_DONE()
            for e in range(8):
                pe_acq(6)
                pe_acq(7)
                P.wait("t", S["ev"], tk_ev)
                MM(ph(6), onehot[0:8, e * 128:(e + 1) * 128], w8sb[0:8, 0:512], True, True)
                MM(ph(7), onehot[0:8, e * 128:(e + 1) * 128], w8sb[0:8, 512:1024], True, True)
                mv = TINC()
                P.wait("v", S["mm"], mv)
                V("tensor_copy", weB[:, 0:512], ph(6))
                REL("v", 6)
                V("tensor_copy", weB[:, 512:1024], ph(7))
                REL("v", 7)
                hmul_ev = {0: 0, 1: 0}
                xev = {}
                for j in range(11):
                    sv, wv = SLAB(upw[e, :, j * 256:(j + 1) * 256], 16, 256)
                    P.wait("t", wv[0], wv[1])
                    pp = j % 2
                    for half, pr in ((0, 0), (1, 1)):
                        pe_acq(pr * 2)
                        pe_acq(pr * 2 + 1)
                        for th in range(2):
                            for fc in range(NFC):
                                MM(PP[pr][:, th * 512:th * 512 + 512],
                                   sv[:, fc * 256 + half * 128: fc * 256 + half * 128 + 128],
                                   XN[:, fc * 1024 + th * 512: fc * 1024 + th * 512 + 512],
                                   fc == 0, fc == NFC - 1)
                        TINC()
                    mvg = S["mm"].n - 1
                    mvu = S["mm"].n
                    P.wait("s", S["mm"], mvg)
                    if hmul_ev[pp] > 0:
                        P.wait("s", S["ev"], hmul_ev[pp])
                    SACT(WS[:, 22528 + pp * 1024:22528 + pp * 1024 + 512],
                         PP[0][:, 0:512], AF.Silu)
                    REL("s", 0)
                    SACT(WS[:, 22528 + pp * 1024 + 512:22528 + (pp + 1) * 1024],
                         PP[0][:, 512:1024], AF.Silu)
                    REL("s", 1)
                    gse_pb = pb[1].n
                    P.wait("v", S["mm"], mvu)
                    VD("tensor_mul", WS[:, j * 1024:j * 1024 + 512],
                       PP[1][:, 0:512], weB[:, 0:512])
                    REL("v", 2)
                    V("tensor_mul", WS[:, j * 1024 + 512:(j + 1) * 1024],
                      PP[1][:, 512:1024], weB[:, 512:1024])
                    REL("v", 3)
                    P.wait("v", pb[1], gse_pb)
                    VD("tensor_mul", WS[:, j * 1024:(j + 1) * 1024],
                       WS[:, j * 1024:(j + 1) * 1024],
                       WS[:, 22528 + pp * 1024:22528 + (pp + 1) * 1024])
                    hmul_ev[pp] = VINC()
                    SLAB_DONE()
                he_ev = VINC()
                xn_free_mm = S["mm"].n
                for cs in range(8):
                    sv, wv = SLAB(dnw[e, :, cs * 256:(cs + 1) * 256], 11, 256)
                    P.wait("t", wv[0], wv[1])
                    P.wait("t", S["ev"], he_ev)
                    for oc2 in range(2):
                        oc = cs * 2 + oc2
                        for th in range(2):
                            hp = (oc2 * 2 + th) % 4
                            pe_acq(hp)
                            for l in range(11):
                                MM(ph(hp),
                                   sv[:, l * 256 + oc2 * 128: l * 256 + oc2 * 128 + 128],
                                   WS[:, l * 1024 + th * 512: l * 1024 + th * 512 + 512],
                                   l == 0, l == 10)
                            mv = TINC()
                            P.wait("v", S["mm"], mv)
                            xsl = oc * 1024 + th * 512
                            V("tensor_add", X_[:, xsl:xsl + 512],
                              X_[:, xsl:xsl + 512], ph(hp))
                            REL("v", hp)
                            if e == 7 and th == 1:
                                xev[oc] = VINC()
                    SLAB_DONE()
                    if e == 7 and out_cb is not None and cs in (3, 7):
                        out_cb(cs // 4, xev[cs * 2 + 1])
            return (VINC(), [xev[oc] for oc in range(16)], xn_free_mm,
                    S["mm"].n)

        ffn1_ev, xev_f1, f1_xn_mm, f1_all_mm = emit_ffn(ln2_ev, topk_ev)

        ln3_ev, _ = emit_ln(with_gate=False, xfc_ev=xev_f1,
                            xn_guard_mm=f1_xn_mm, sq_guard_mm=f1_all_mm)
        lat_st = []
        slab_st = {}
        for sl in range(4):
            src = wkd if sl < 2 else wqd
            c0 = (sl % 2) * 256
            sv, wv = SLAB(src[:, c0:c0 + 256], 16, 256)
            pbv = WS[:, (sl % 2) * 2048:(sl % 2) * 2048 + 2048]
            if sl >= 2:
                P.wait("v", S["st"], slab_st[sl - 2])
            P.wait("t", wv[0], wv[1])
            P.wait("t", S["ev"], ln3_ev)
            for half in range(2):
                for tc2 in range(2):
                    hp = (half * 2 + tc2) % 4
                    pe_acq(hp)
                    for fc in range(NFC):
                        MM(ph(hp),
                           sv[:, fc * 256 + half * 128: fc * 256 + half * 128 + 128],
                           XN[:, fc * 1024 + tc2 * 512: fc * 1024 + tc2 * 512 + 512],
                           fc == 0, fc == NFC - 1)
                    mv = TINC()
                    P.wait("v", S["mm"], mv)
                    V("tensor_copy",
                      pbv[:, half * 1024 + tc2 * 512: half * 1024 + tc2 * 512 + 512],
                      ph(hp))
                    REL("v", hp)
            pev = VINC()
            SLAB_DONE()
            P.wait("y", S["ev"], pev)
            dst_t = cc_bkv_in if sl < 2 else cc_bq_in
            stv = None
            for half in range(2):
                r0 = c0 + half * 128
                stv = STD(dst_t[r0:r0 + 128, :],
                          pbv[:, half * 1024:half * 1024 + 1024],
                          chain=lat_st[-1] if lat_st else 0)
                lat_st.append(stv)
            slab_st[sl] = stv
            if sl == 2:
                P.add("g", lambda g, v=slab_st[1]: g.wait_ge(S["st"].h, v))
                cc3 = COLL("AllGather", cc_bkv_in, cc_bkv_out)
        P.add("g", lambda g: g.wait_ge(S["st"].h, lat_st[-1]))
        cc3q = COLL("AllGather", cc_bq_in, cc_bq_out)
        lat_mm_end = S["mm"].n

        P.wait("y", S["cc"], cc3)
        P.wait("y", S["mm"], lat_mm_end)
        P.wait("v", S["st"], lat_st[-1])
        bostore = []
        LATB = [XN[:, 0:8192], WS[:, 16384:24576]]
        grp_mm = {}
        grp_ev = {}
        grp_ld = {}
        GROUPS = []
        for u in range(4):
            if u // 2 == 0:
                GROUPS += [(u, side, lh) for side in ("kv", "q") for lh in range(2)]
            else:
                GROUPS += [(u, side, lh) for lh in range(2) for side in ("kv", "q")]

        def emit_b_loads(gi):
            u, side, lh = GROUPS[gi]
            beta = u % 2
            latb = LATB[gi % 2]
            if gi >= 2:
                P.wait("y", S["mm"], grp_mm[gi - 2])
            src_t = cc_bkv_out if side == "kv" else cc_bq_out
            if side == "q" and gi == 2:
                P.wait("y", S["cc"], cc3q)
            lvl = None
            for l in range(4):
                for j in range(2):
                    r = (beta * 4 + lh * 2 + j) * 512 + l * 128
                    lvl = SYD(latb[:, l * 2048 + j * 1024: l * 2048 + j * 1024 + 1024],
                              src_t[r:r + 128, :], sem=S[f"lx{gi % 2}"])
            grp_ld[gi] = lvl

        emit_b_loads(0)
        emit_b_loads(1)
        for u in range(4):
            beta, slot = u % 2, u // 2
            for k in range(4):
                gi = u * 4 + k
                side, lh = GROUPS[gi][1], GROUPS[gi][2]
                latb = LATB[gi % 2]
                if slot == 1 and side == "kv":
                    if gi >= 1:
                        P.wait("g", S["ev"], grp_ev[gi - 1])
                    lvp = GPD(WS[:, 12288:14336], pa_in[:, lh * 2048:(lh + 1) * 2048])
                    lvp = GPD(WS[:, 14336:16384], pb_in[:, lh * 2048:(lh + 1) * 2048])
                else:
                    lvp = 0
                P.wait("t", S[f"lx{gi % 2}"], grp_ld[gi])
                if lvp:
                    P.wait("t", S["gp"], lvp)
                wof = {"q": 0, "kv": 1024}[side]
                dst = 8192 if side == "q" else 0
                beffc = MC_BEFFQ if side == "q" else MC_BEFFK
                bia = misc[:, beffc:beffc + 1]
                bia1 = misc[:, beffc + 1:beffc + 2]
                ncomp = 1 if slot == 0 else 2
                for ci in range(ncomp):
                    outv = (WS[:, dst + lh * 2048:dst + (lh + 1) * 2048] if slot == 0
                            else XN[:, 8192 + ci * 2048:8192 + (ci + 1) * 2048])
                    for tt in range(4):
                        hp = tt % 2
                        pe_acq(hp)
                        for l in range(4):
                            MM(ph(hp),
                               upslc[:, wof + l * 256 + ci * 128: wof + l * 256 + ci * 128 + 128],
                               latb[:, l * 2048 + tt * 512: l * 2048 + tt * 512 + 512],
                               l == 0, l == 3)
                        mv = TINC()
                        P.wait("v", S["mm"], mv)
                        bb = bia if ci == 0 else bia1
                        sca = SCALE_B if side == "q" else 1.0
                        V("tensor_scalar", outv[:, tt * 512:(tt + 1) * 512],
                          ph(hp), sca, bb, ALU.mult, ALU.add)
                        REL("v", hp)
                if slot == 1:
                    VD("tensor_mul", WS[:, dst + lh * 2048:dst + (lh + 1) * 2048],
                       XN[:, 8192:10240], WS[:, 12288:14336])
                    VD("tensor_mul", XN[:, 8192:10240],
                       XN[:, 10240:12288], WS[:, 14336:16384])
                    VD("tensor_add", WS[:, dst + lh * 2048:dst + (lh + 1) * 2048],
                       WS[:, dst + lh * 2048:dst + (lh + 1) * 2048],
                       XN[:, 8192:10240])
                if side == "kv":
                    for tt in range(16):
                        kc = lh * 16 + tt
                        hp = 2 + (tt % 2)
                        pe_acq(hp)
                        for l in range(4):
                            MM(ph(hp)[:, 0:128],
                               latb[:, l * 2048 + tt * 128: l * 2048 + tt * 128 + 128],
                               upslc[:, 2048 + l * 256 + slot * 128:
                                     2048 + l * 256 + slot * 128 + 128],
                               l == 0, l == 3)
                        mv = TINC()
                        P.wait("v", S["mm"], mv)
                        V("tensor_copy", WS[:, 4096 + kc * 128:4096 + (kc + 1) * 128],
                          ph(hp)[:, 0:128])
                        REL("v", hp)
                grp_mm[gi] = S["mm"].n
                grp_ev[gi] = VINC()
                if gi + 2 < 16:
                    emit_b_loads(gi + 2)
            bqk_ev = VINC()
            last_bqk = bqk_ev
            P.wait("t", S["ev"], bqk_ev)
            if u > 0:
                P.wait("v", S["st"], bostore[-1])
            obias = misc[:, MC_BEFFV + slot:MC_BEFFV + slot + 1]
            cast_last = emit_attn_b_unit(
                WS[:, 8192:12288], WS[:, 0:4096],
                lambda kc: WS[:, 4096 + kc * 128:4096 + (kc + 1) * 128],
                XN[:, 12288:16384], obias, None, None)
            P.wait("y", S["ev"], cast_last)
            for d2 in range(4):
                dd = beta * 4 + d2
                stv = STD(cc_bo_in[slot][dd * 128:(dd + 1) * 128, :],
                          XN[:, 12288 + d2 * 1024:12288 + (d2 + 1) * 1024])
                bostore.append(stv)
            if u == 1:
                P.add("g", lambda g, v=bostore[-1]: g.wait_ge(S["st"].h, v))
                cc4a = COLL("AllToAll", cc_bo_in[0], cc_bo_out[0])
        P.add("g", lambda g: g.wait_ge(S["st"].h, bostore[-1]))
        cc4b = COLL("AllToAll", cc_bo_in[1], cc_bo_out[1])

        P.wait("y", S["cc"], cc4a)
        olv = None
        oevb = {}
        P.wait("y", S["mm"], grp_mm[14])
        P.wait("y", S["ev"], last_bqk)
        for fc in range(NFC):
            if fc == 8:
                P.wait("y", S["ld"], olv)
                P.wait("v", S["ld"], olv)
                oevb[0] = VINC()
                P.wait("y", S["cc"], cc4b)
            olv = SYD(XN[:, fc * 1024:(fc + 1) * 1024],
                      cc_bo_out[fc // 8][(fc % 8) * 128:(fc % 8 + 1) * 128, :])
        P.wait("v", S["ld"], olv)
        oevb[1] = VINC()
        xev_pb, pb_mm = emit_proj_residual2(wo, XN, MC_BO, oevb[0], oevb[1])

        ln4_ev, gate_pb4 = emit_ln(with_gate=True, xfc_ev=xev_pb,
                                   xn_guard_mm=pb_mm)
        topk4_ev = emit_topk(gate_pb4)

        out_st = []

        def emit_out_half(hf, gate_ev):
            for tc in range(NTC):
                stg = xstgA if len(out_st) % 2 == 0 else xstgB
                if len(out_st) >= 2:
                    P.wait("v", S["st"], out_st[-2])
                for f8 in range(8):
                    fc = hf * 8 + f8
                    hp = 4 + (f8 % 2)
                    pe_acq(hp)
                    if tc == 0 and f8 == 0:
                        P.wait("t", S["ev"], gate_ev)
                    TTR(ph(hp)[:, 0:128], X_[:, fc * 1024 + tc * 128: fc * 1024 + tc * 128 + 128])
                    mv = TINC()
                    P.wait("v", S["mm"], mv)
                    V("tensor_copy", stg[:, f8 * 128:(f8 + 1) * 128], ph(hp)[:, 0:128])
                    REL("v", hp)
                sev = VINC()
                P.wait("y", S["ev"], sev)
                stv = STD(out_t[tc * 128:(tc + 1) * 128, hf * 1024:(hf + 1) * 1024],
                          stg, chain=out_st[-1] if out_st else 0)
                out_st.append(stv)

        ffn2_ev, _, _, _ = emit_ffn(ln4_ev, topk4_ev, out_cb=emit_out_half)
        P.add("y", lambda y: y.wait_ge(S["st"].h, out_st[-1]))

        block.tensor(lambda e: [f(e) for f in P.q["t"]])
        block.vector(lambda e: [f(e) for f in P.q["v"]])
        block.scalar(lambda e: [f(e) for f in P.q["s"]])
        block.gpsimd(lambda e: [f(e) for f in P.q["g"]])
        block.sync(lambda e: [f(e) for f in P.q["y"]])

    return nc


def _prep_inputs(ins):
    f32 = np.float32
    xfull = np.ascontiguousarray(ins["L0_attn_a_hidden_states"], dtype=f32)

    perm = []
    for d in range(8):
        for part in range(3):
            for hh in (d, d + 8):
                base = part * H + hh * 128
                perm.extend(range(base, base + 128))
    perm = np.array(perm)
    wqkv_p = np.ascontiguousarray(ins["L0_attn_a_w_qkv"][:, perm], dtype=f32)
    bqkv_p = np.ascontiguousarray(ins["L0_attn_a_b_qkv"][perm], dtype=f32)

    operm = np.arange(2048)
    wproj_p = np.ascontiguousarray(ins["L0_attn_a_w_proj"][operm, :], f32)
    wo_p = np.ascontiguousarray(ins["L0_attn_b_Wo"].T[operm, :], f32)

    wqd_t = np.ascontiguousarray(ins["L0_attn_b_Wq_down"].T, f32)
    wkd_t = np.ascontiguousarray(ins["L0_attn_b_Wkv_down"].T, f32)

    sperm = []
    for j in range(22):
        sperm.extend(range(j * 128, (j + 1) * 128))
        sperm.extend(range(2816 + j * 128, 2816 + (j + 1) * 128))
    sperm = np.array(sperm)
    sgw_ti = np.ascontiguousarray(ins["L0_ffn_shared_gw"].T[:, sperm], f32)
    sdw_t = np.ascontiguousarray(ins["L0_ffn_shared_dw"].T, f32)
    uperm = []
    for j in range(11):
        uperm.extend(range(j * 128, (j + 1) * 128))
        uperm.extend(range(1408 + j * 128, 1408 + (j + 1) * 128))
    uperm = np.array(uperm)
    upw_ti = np.ascontiguousarray(ins["L0_ffn_up_w"].transpose(0, 2, 1)[:, :, uperm], f32)
    dnw_t = np.ascontiguousarray(ins["L0_ffn_down_w"].transpose(0, 2, 1), f32)
    gwt_t = np.ascontiguousarray(ins["L0_ffn_gate_w"].T, f32)

    ident = np.eye(128, dtype=f32)
    ones_row = np.ones((1, 128), f32)
    ones_colb = np.ones((128, 1), bfnp)
    onehot8 = np.zeros((8, 1024), bfnp)
    for e in range(8):
        onehot8[e, e * 128:(e + 1) * 128] = 1
    qpos_bf = np.arange(SEQ, dtype=f32)[None, :].astype(bfnp)
    slopes = _alibi_slopes(NH)
    cos64, sin64 = _rope_cos_sin(SEQ, H // 4)

    dm = np.zeros((128, 2048), np.float64)
    for dvar in range(4):
        pcol = dvar * 128 + np.arange(128)[:, None]
        fcol = np.arange(512)[None, :]
        dm[:, dvar * 512:(dvar + 1) * 512] = np.where(pcol > fcol, -30000.0, 0.0)
    diag_masks = dm.astype(bfnp)

    gate_bias_p = np.asarray(ins["L0_ffn_gate_bias"], f32).reshape(8, 1)

    base = dict(
        wqkv_p=wqkv_p, wproj_p=wproj_p, wo_p=wo_p, wqd_t=wqd_t, wkd_t=wkd_t,
        shared_gw_ti=sgw_ti, shared_dw_t=sdw_t, up_w_ti=upw_ti, down_w_t=dnw_t,
        gate_w_t=gwt_t, gate_bias_p=gate_bias_p,
        ident_in=ident, ones_row=ones_row, ones_colb=ones_colb, onehot8=onehot8,
        qpos_bf=qpos_bf, diag_masks=diag_masks,
    )

    in_maps = []
    for c in range(8):
        m = dict(base)
        beta, r = c // 4, c % 4
        m["x_slice"] = np.ascontiguousarray(xfull[beta, r * TOK:(r + 1) * TOK, :], f32)
        hA, hB = c, c + 8
        mm = c % 4
        bm = np.zeros((128, 80), f32)
        bm[:, MCB("BPROJ"):MCB("BPROJ") + 16] = np.asarray(
            ins["L0_attn_a_b_proj"], f32).reshape(16, 128).T
        bm[:, MCB("BO"):MCB("BO") + 16] = np.asarray(
            ins["L0_attn_b_bo"], f32).reshape(16, 128).T
        bq = np.asarray(ins["L0_attn_a_b_qkv"], f32)
        bm[:, MCB("BVA") + 0] = bq[2 * H + hA * 128: 2 * H + (hA + 1) * 128]
        bm[:, MCB("BVA") + 1] = bq[2 * H + hB * 128: 2 * H + (hB + 1) * 128]
        for d in range(8):
            for half in range(2):
                hh = d + 8 * half
                bm[:, 48 + d * 2 + half] = bq[hh * 128:(hh + 1) * 128]
                bm[:, 64 + d * 2 + half] = bq[2048 + hh * 128: 2048 + (hh + 1) * 128]
        bm[:, MCB("BQD"):MCB("BQD") + 4] = np.asarray(
            ins["L0_attn_b_bq_down"], f32).reshape(4, 128).T
        bm[:, MCB("BKD"):MCB("BKD") + 4] = np.asarray(
            ins["L0_attn_b_bkv_down"], f32).reshape(4, 128).T
        blk_c = mm if c < 4 else mm + 4
        blk_o = mm + 4 if c < 4 else mm
        qcols = list(range(blk_c * 128, (blk_c + 1) * 128)) + \
            list(range(blk_o * 128, (blk_o + 1) * 128))
        m["wqu_s"] = np.ascontiguousarray(ins["L0_attn_b_Wq_up"].T[:, qcols], f32)
        m["wku_s"] = np.ascontiguousarray(ins["L0_attn_b_Wk_up"].T[:, qcols], f32)
        vcols = list(range(hA * 128, (hA + 1) * 128)) + \
            list(range(hB * 128, (hB + 1) * 128))
        m["wvu_s"] = np.ascontiguousarray(ins["L0_attn_b_Wv_up"].T[:, vcols], f32)
        bqu = np.asarray(ins["L0_attn_b_bq_up"], f32)
        bku = np.asarray(ins["L0_attn_b_bk_up"], f32)
        bvu = np.asarray(ins["L0_attn_b_bv_up"], f32)
        bm[:, MCB("BQU") + 0] = bqu[blk_c * 128:(blk_c + 1) * 128]
        bm[:, MCB("BQU") + 1] = bqu[blk_o * 128:(blk_o + 1) * 128]
        bm[:, MCB("BKU") + 0] = bku[blk_c * 128:(blk_c + 1) * 128]
        bm[:, MCB("BKU") + 1] = bku[blk_o * 128:(blk_o + 1) * 128]
        bm[:, MCB("BVU") + 0] = bvu[hA * 128:(hA + 1) * 128]
        bm[:, MCB("BVU") + 1] = bvu[hB * 128:(hB + 1) * 128]
        m["bias_misc"] = bm
        sl2 = np.zeros((1, 256), np.float64)
        sl2[0, 0:128] = slopes[hA]
        sl2[0, 128:256] = slopes[hB]
        m["slope_row2"] = sl2.astype(f32)
        kpc = np.zeros((128, 64), np.float64)
        for kc in range(32):
            pos = kc * 128 + np.arange(128)
            kpc[:, kc * 2 + 0] = slopes[hA] * pos
            kpc[:, kc * 2 + 1] = slopes[hB] * pos
        m["kposc"] = kpc.astype(f32)
        pa = cos64[:, mm * 128:(mm + 1) * 128].T
        sgn = -1.0 if c < 4 else 1.0
        pbp = sgn * sin64[:, mm * 128:(mm + 1) * 128].T
        m["ropePA"] = np.ascontiguousarray(pa, f32)
        m["ropePB"] = np.ascontiguousarray(pbp, f32)
        in_maps.append(m)
    return in_maps


def MCB(name):
    return {"BPROJ": 0, "BO": 16, "BVA": 32, "BQD": 34, "BKD": 38,
            "BQU": 42, "BKU": 44, "BVU": 46}[name]


_PROG_CACHE = {}


def kernel(**inputs):
    in_maps = _prep_inputs(inputs)
    if "nc" not in _PROG_CACHE:
        _PROG_CACHE["nc"] = build_program()
    nc = _PROG_CACHE["nc"]
    with allow_ici_all_to_all():
        res = run_bass_kernel_spmd(nc, in_maps, core_ids=list(range(8)))
    out = np.zeros((B, SEQ, H), np.float32)
    for c in range(8):
        beta, r = c // 4, c % 4
        out[beta, r * TOK:(r + 1) * TOK, :] = np.asarray(res.results[c]["out_slice"])
    return out


# version 12
# speedup vs baseline: 1.0073x; 1.0073x over previous
import math
import numpy as np
import ml_dtypes
from contextlib import ExitStack

import concourse.bass as bass
import concourse.mybir as mybir
from concourse.bass_utils import run_bass_kernel_spmd
from concourse.aot_env import allow_ici_all_to_all

F32 = mybir.dt.float32
BF16 = mybir.dt.bfloat16
AF = mybir.ActivationFunctionType
ALU = mybir.AluOpType
AXX = mybir.AxisListType.X

B = 2
SEQ = 4096
H = 2048
NH = 16
HD = 128
TOK = 1024
NTC = 8
NFC = 16
LAT = 512
EPS = 1e-5
SCALE_A = 1.0 / HD
SCALE_B = 1.0 / math.sqrt(HD)
bfnp = ml_dtypes.bfloat16


def _alibi_slopes(n):
    start = 2.0 ** (-2.0 ** (-(math.log2(n) - 3)))
    return np.array([start * start ** i for i in range(n)], np.float64)


def _rope_cos_sin(seq_len, dim, base=10000.0):
    inv_freq = 1.0 / base ** (np.arange(0, dim, 2, dtype=np.float64) / dim)
    t = np.arange(seq_len, dtype=np.float64)
    freqs = np.outer(t, inv_freq)
    emb = np.concatenate([freqs, freqs], -1)
    return np.cos(emb), np.sin(emb)


class Sem:
    __slots__ = ("h", "n")

    def __init__(self, handle):
        self.h = handle
        self.n = 0


class Plan:
    def __init__(self):
        self.q = {e: [] for e in ("t", "v", "s", "g", "y")}

    def add(self, eng, fn, real=False):
        fn.real = real
        self.q[eng].append(fn)

    def wait(self, eng, sem, val):
        if val > 0:
            w = lambda e, _h=sem.h, _v=val: e.wait_ge(_h, _v)
            w.real = False
            self.q[eng].append(w)

    def inc_last(self, eng, sem, k=1):
        if self.q[eng] and getattr(self.q[eng][-1], "real", False):
            prev = self.q[eng].pop()
            def wrapped(e, _p=prev, _h=sem.h, _k=k):
                inst = _p(e)
                inst.then_inc(_h, _k)
            wrapped.real = False
            self.q[eng].append(wrapped)
        else:
            f = lambda e, _h=sem.h, _k=k: e.drain().then_inc(_h, _k)
            f.real = False
            self.q[eng].append(f)
        sem.n += k
        return sem.n


def build_program(upto=99):
    nc = bass.Bass("TRN2", target_bir_lowering=False, debug=False)

    def din(name, shape, dt=F32):
        return nc.dram_tensor(name, shape, dt, kind="ExternalInput").ap()

    x_in = din("x_slice", [TOK, H])
    wqkv = din("wqkv_p", [H, 3 * H])
    wproj = din("wproj_p", [H, H])
    wo = din("wo_p", [H, H])
    wqd = din("wqd_t", [H, LAT])
    wkd = din("wkd_t", [H, LAT])
    sgw = din("shared_gw_ti", [H, 5632])
    sdw = din("shared_dw_t", [2816, H])
    upw = din("up_w_ti", [8, H, 2816])
    dnw = din("down_w_t", [8, 1408, H])
    gwt = din("gate_w_t", [H, 8])
    wqu_s = din("wqu_s", [LAT, 256])
    wku_s = din("wku_s", [LAT, 256])
    wvu_s = din("wvu_s", [LAT, 256])
    bias_misc = din("bias_misc", [128, 80])
    gbias_in = din("gate_bias_p", [8, 1])
    ident_in = din("ident_in", [128, 128])
    onesr_in = din("ones_row", [1, 128])
    onesc_in = din("ones_colb", [128, 1], BF16)
    onehot_in = din("onehot8", [8, 1024], BF16)
    slope_in = din("slope_row2", [1, 256])
    kposc_in = din("kposc", [128, 64])
    qpos_in = din("qpos_bf", [1, SEQ], BF16)
    diag_in = din("diag_masks", [128, 2048], BF16)
    pa_in = din("ropePA", [128, SEQ])
    pb_in = din("ropePB", [128, SEQ])

    out_t = nc.dram_tensor("out_slice", [TOK, H], F32, kind="ExternalOutput").ap()

    cc_ak_in = nc.dram_tensor("cc_ak_in", [2048, TOK], BF16)
    cc_ak_out = nc.dram_tensor("cc_ak_out", [2048, TOK], BF16)
    cc_aq_in = [nc.dram_tensor(f"cc_aq_in{i}", [1024, TOK], BF16) for i in range(2)]
    cc_aq_out = [nc.dram_tensor(f"cc_aq_out{i}", [1024, TOK], BF16) for i in range(2)]
    cc_av_in = nc.dram_tensor("cc_av_in", [8 * TOK, 256], BF16)
    cc_av_out = nc.dram_tensor("cc_av_out", [8 * TOK, 256], BF16)
    cc_ao_in = [nc.dram_tensor(f"cc_ao_in{i}", [8 * 128, TOK], BF16) for i in range(2)]
    cc_ao_out = [nc.dram_tensor(f"cc_ao_out{i}", [8 * 128, TOK], BF16) for i in range(2)]
    cc_bq_in = nc.dram_tensor("cc_bq_in", [512, TOK], BF16)
    cc_bq_out = nc.dram_tensor("cc_bq_out", [8 * 512, TOK], BF16)
    cc_bkv_in = nc.dram_tensor("cc_bkv_in", [512, TOK], BF16)
    cc_bkv_out = nc.dram_tensor("cc_bkv_out", [8 * 512, TOK], BF16)
    cc_bo_in = [nc.dram_tensor(f"cc_bo_in{i}", [8 * 128, TOK], BF16) for i in range(2)]
    cc_bo_out = [nc.dram_tensor(f"cc_bo_out{i}", [8 * 128, TOK], BF16) for i in range(2)]

    MC_BPROJ = 0
    MC_BO = 16
    MC_BVA = 32
    MC_QKB = 48
    MC_BQD = 34
    MC_BKD = 38
    MC_BQU = 42
    MC_BKU = 44
    MC_BVU = 46

    with ExitStack() as stack:
        E = stack.enter_context

        af = E(nc.sbuf_tensor("arena_f32", [128, 22016], F32))
        X_ = af[:, 0:16384]
        dscr = af[:, 16384:17408]
        weB = dscr
        mBh = af[:, 17408:17920]
        rBh = af[:, 17920:18432]
        sumac = af[:, 18432:19456]
        recB = af[:, 19456:19968]
        otmp = af[:, 19968:20992]
        sum2 = af[:, 20992:22016]
        xstgA = af[:, 16384:17408]
        xstgB = af[:, 17408:18432]

        ab = E(nc.sbuf_tensor("arena_b16", [128, 51200], BF16))
        XN = ab[:, 0:16384]
        WS = ab[:, 16384:40960]
        WT = ab[:, 40960:49152]
        PT = ab[:, 49152:51200]

        misc = E(nc.sbuf_tensor("misc_f32", [128, 16], F32))
        MC_ONES = 0
        MC_BEFFQ = 2
        MC_BEFFK = 4
        MC_BEFFV = 6
        gate = E(nc.sbuf_tensor("gate_f32", [8, 2064], F32))
        G_S, G_BS, G_GB = 0, 1024, 2048
        srowA = af[0:1, 16384:16896]
        srowB = af[0:1, 16896:17408]
        srowC = af[0:1, 19456:19968]
        topk = E(nc.sbuf_tensor("topk_f32", [128, 64], F32))
        w8sb = E(nc.sbuf_tensor("w8sb_b16", [8, 1024], BF16))
        onehot = E(nc.sbuf_tensor("onehot_b16", [8, 1024], BF16))
        ident = E(nc.sbuf_tensor("ident_f32", [128, 128], F32))
        onesr = E(nc.sbuf_tensor("onesr_f32", [1, 128], F32))
        onescb = E(nc.sbuf_tensor("onescb_b16", [128, 1], BF16))
        onesrb = E(nc.sbuf_tensor("onesrb_b16", [1, 128], BF16))
        slopes = E(nc.sbuf_tensor("slopes_f32", [1, 256], F32))
        upslc = E(nc.sbuf_tensor("upslc_b16", [128, 3072], BF16))
        gwt_s = E(nc.sbuf_tensor("gwt_b16", [128, 128], BF16))
        bmisc = E(nc.sbuf_tensor("bmisc_f32", [128, 80], F32))
        bqdb = E(nc.sbuf_tensor("bqdb_b16", [128, 8], BF16))
        kposc = E(nc.sbuf_tensor("kposc_f32", [128, 64], F32))

        PP = [E(nc.psum_tensor(f"pp{i}", [128, 1024], F32)) for i in range(4)]

        S = {nm: Sem(E(nc.semaphore(nm))) for nm in
             ["ld", "lx0", "lx1", "w0", "w1", "wf", "mm", "ev", "es", "cc", "st",
              "px", "sum", "gp", "pb0", "pb1", "pb2", "pb3", "pb4", "pb5", "pb6",
              "pb7"]}
        pb = [S[f"pb{i}"] for i in range(8)]
        pbu = [0] * 8

        block = E(nc.Block())
        P = Plan()

        def ph(i):
            return PP[i // 2][:, (i % 2) * 512:(i % 2) * 512 + 512]

        def pe_acq(i):
            if pbu[i] > 0:
                P.wait("t", pb[i], pbu[i])
            pbu[i] += 1

        def REL(eng, i):
            P.inc_last(eng, pb[i], 1)

        def MM(out, lhsT, rhs, start, stop):
            P.add("t", lambda t, o=out, l=lhsT, r=rhs, st=start, sp=stop:
                  t.matmul(o, l, r, start=st, stop=sp), real=True)

        def TTR(out, in_, np_=128):
            P.add("t", lambda t, o=out, i=in_, n=np_: t.transpose(o, i, ident[0:n, 0:n]),
                  real=True)

        def TINC(sem=None, k=1):
            return P.inc_last("t", sem or S["mm"], k)

        def VINC(sem=None, k=1):
            return P.inc_last("v", sem or S["ev"], k)

        def SINC(sem=None, k=1):
            return P.inc_last("s", sem or S["es"], k)

        def V(meth, *args):
            def f(v, m=meth, a=args):
                v.drain()
                return getattr(v, m)(*a)
            P.add("v", f, real=True)

        VD = V

        def SDR():
            P.add("s", lambda s: s.drain())

        def SC(meth, *args, **kw):
            P.add("s", lambda s, m=meth, a=args, k=kw: getattr(s, m)(*a, **k))

        def SACT(out, in_, func, bias=0.0, scale=1.0, accum=None):
            if accum is None:
                def f1(s, o=out, i=in_, fn=func, b=bias, sc=scale):
                    s.drain()
                    return s.activation(out=o, in_=i, func=fn, bias=b, scale=sc)
                P.add("s", f1, real=True)
            else:
                def f2(s, o=out, i=in_, fn=func, b=bias, sc=scale, ac=accum):
                    s.drain()
                    return s.activation(out=o, in_=i, func=fn, bias=b, scale=sc,
                                        accum_out=ac)
                P.add("s", f2, real=True)

        def SACT_ND(out, in_, func, bias=0.0, scale=1.0):
            P.add("s", lambda s, o=out, i=in_, fn=func, b=bias, sc=scale:
                  s.activation(out=o, in_=i, func=fn, bias=b, scale=sc), real=True)

        def SYD(out, in_, transpose=False, sem=None):
            sm = sem or S["ld"]
            P.add("y", lambda y, o=out, i=in_, tr=transpose, _h=sm.h:
                  y.dma_start(out=o, in_=i, transpose=tr).then_inc(_h, 16))
            sm.n += 16
            return sm.n

        def STD(out, in_, chain=0):
            def _f(y, o=out, i=in_, ch=chain):
                if ch > 0:
                    y.wait_ge(S["st"].h, ch)
                y.dma_start(out=o, in_=i).then_inc(S["st"].h, 16)
            P.add("y", _f)
            S["st"].n += 16
            return S["st"].n

        def GPD(out, in_, sem=None):
            sm = sem or S["gp"]
            P.add("g", lambda g, o=out, i=in_, _h=sm.h:
                  g.dma_start(out=o, in_=i).then_inc(_h, 16))
            sm.n += 16
            return sm.n

        def COLL(kind, ins, outs):
            def _do(g, k=kind, i=ins, o=outs):
                with allow_ici_all_to_all():
                    g.collective_compute(k, ALU.bypass,
                                         replica_groups=[list(range(8))],
                                         ins=[i.ap().opt()], outs=[o.ap().opt()]
                                         ).then_inc(S["cc"].h, 1)
            P.add("g", _do)
            S["cc"].n += 1
            return S["cc"].n

        wstate = {"idx": 0}
        slab_free_mm = {}

        def SLAB(ap2d, nf, osz):
            si = wstate["idx"]
            wstate["idx"] += 1
            par = si % 2
            wsem = S[f"w{par}"]
            sv = WT[:, par * 4096:par * 4096 + nf * osz]
            prev = si - 2
            def _dma(g, dr=ap2d, s2=sv, oz=osz, pv=prev, wh=wsem.h):
                if pv >= 0:
                    g.wait_ge(S["mm"].h, slab_free_mm[pv])
                g.dma_start(out=s2.rearrange("p (c o) -> p c o", o=oz),
                            in_=dr.rearrange("(c p) o -> p c o", p=128)
                            ).then_inc(wh, 16)
            P.add("g", _dma)
            wsem.n += 16
            return sv, (wsem, wsem.n)

        def SLAB_DONE():
            slab_free_mm[len(slab_free_mm)] = S["mm"].n

        GPD(ident[:, :], ident_in)
        GPD(onesr[:, :], onesr_in)
        GPD(onescb[:, :], onesc_in)
        GPD(slopes[:, :], slope_in)
        GPD(onehot[:, :], onehot_in)
        GPD(bmisc[:, :], bias_misc)
        GPD(gate[0:8, G_GB:G_GB + 1], gbias_in)
        GPD(WS[0:1, 8192:12288], qpos_in)
        GPD(WS[:, 12288:14336], diag_in)
        GPD(gwt_s[:, :].rearrange("p (c o) -> p c o", o=8),
            gwt.rearrange("(c p) o -> p c o", p=128))
        GPD(upslc[:, 0:1024].rearrange("p (c o) -> p c o", o=256),
            wqu_s.rearrange("(c p) o -> p c o", p=128))
        GPD(upslc[:, 1024:2048].rearrange("p (c o) -> p c o", o=256),
            wku_s.rearrange("(c p) o -> p c o", p=128))
        GPD(upslc[:, 2048:3072].rearrange("p (c o) -> p c o", o=256),
            wvu_s.rearrange("(c p) o -> p c o", p=128))
        GPD(bqdb[:, 0:4], bias_misc[:, MC_BQD:MC_BQD + 4])
        GPD(bqdb[:, 4:8], bias_misc[:, MC_BKD:MC_BKD + 4])
        CONST = GPD(kposc[:, :], kposc_in)

        P.wait("v", S["gp"], CONST)
        V("memset", misc[:, MC_ONES:MC_ONES + 1], 1.0)
        V("memset", onesrb[:, :], 1.0)
        V("tensor_scalar", bmisc[:, MC_QKB:MC_QKB + 16],
          bmisc[:, MC_QKB:MC_QKB + 16], SCALE_A, None, ALU.mult)

        P.wait("t", S["gp"], CONST)
        xmv = {}
        xfc1 = []
        for i in range(NTC * 2):
            hf, tc = i // 8, i % 8
            stg = xstgA if i % 2 == 0 else xstgB
            if i >= 2:
                P.wait("y", S["mm"], xmv[i - 2])
            lv = SYD(stg, x_in[tc * 128:(tc + 1) * 128, hf * 1024:(hf + 1) * 1024],
                     sem=S[f"lx{i % 2}"])
            P.wait("t", S[f"lx{i % 2}"], lv)
            for fh in range(2):
                hp = 4 + fh
                pe_acq(hp)
                for k in range(4):
                    f8 = fh * 4 + k
                    TTR(ph(hp)[:, k * 128:(k + 1) * 128],
                        stg[:, f8 * 128:f8 * 128 + 128])
                mv = TINC()
                P.wait("v", S["mm"], mv)
                V("tensor_copy",
                  X_[:, hf * 8192:(hf + 1) * 8192].rearrange(
                      "p (f o) -> p f o", o=1024)[:, fh * 4:fh * 4 + 4,
                                                  tc * 128:(tc + 1) * 128],
                  ph(hp)[:, 0:512].rearrange("p (f o) -> p f o", o=128))
                REL("v", hp)
            xmv[i] = mv
            if tc == 7:
                hev = VINC()
                xfc1 += [hev] * 8
        X_EV = VINC()

        for woff, bcol, ocol, scl in ((0, MC_BQU, MC_BEFFQ, SCALE_B),
                                      (1024, MC_BKU, MC_BEFFK, 1.0),
                                      (2048, MC_BVU, MC_BEFFV, 1.0)):
            bsrc = 0 if woff == 0 else (4 if woff == 1024 else 4)
            for blk in range(2):
                pe_acq(6)
                P.wait("t", S["gp"], CONST)
                for l in range(4):
                    MM(ph(6)[:, 0:1],
                       upslc[:, woff + l * 256 + blk * 128: woff + l * 256 + blk * 128 + 128],
                       bqdb[:, bsrc + l: bsrc + l + 1], l == 0, l == 3)
                mv = TINC()
                P.wait("v", S["mm"], mv)
                V("scalar_tensor_tensor", misc[:, ocol + blk:ocol + blk + 1],
                  ph(6)[:, 0:1], 1.0, bmisc[:, bcol + blk:bcol + blk + 1],
                  ALU.mult, ALU.add)
                if scl != 1.0:
                    VD("tensor_scalar", misc[:, ocol + blk:ocol + blk + 1],
                       misc[:, ocol + blk:ocol + blk + 1], scl, None, ALU.mult)
                REL("v", 6)
        BEFF_EV = VINC()

        def emit_ln(with_gate, xfc_ev=None, xn_guard_mm=0, sq_guard_mm=0):
            entry_ev = S["ev"].n
            if sq_guard_mm:
                P.wait("s", S["mm"], sq_guard_mm)
            for fc in range(NFC):
                P.wait("s", S["ev"], xfc_ev[fc] if xfc_ev else entry_ev)
                SACT_ND(WS[:, fc * 512:(fc + 1) * 512],
                        X_[:, fc * 1024:fc * 1024 + 512], AF.Square)
                SACT_ND(WS[:, 14336 + fc * 512:14336 + (fc + 1) * 512],
                        X_[:, fc * 1024 + 512:(fc + 1) * 1024], AF.Square)
            cmk = {}
            if xn_guard_mm:
                P.wait("s", S["mm"], xn_guard_mm)
            for fc in range(NFC):
                SACT_ND(XN[:, fc * 1024:(fc + 1) * 1024],
                        X_[:, fc * 1024:(fc + 1) * 1024], AF.Copy)
                cmk[fc] = SINC()
            for th in range(2):
                sqb = 0 if th == 0 else 14336
                pe_acq(6)
                pe_acq(7)
                for fc in range(NFC):
                    P.wait("t", S["es"], cmk[fc])
                    MM(PP[3][0:1, 0:512], onescb[:, :],
                       XN[:, fc * 1024 + th * 512: fc * 1024 + th * 512 + 512],
                       fc == 0, fc == NFC - 1)
                    MM(PP[3][0:1, 512:1024], onescb[:, :],
                       WS[:, sqb + fc * 512:sqb + (fc + 1) * 512],
                       fc == 0, fc == NFC - 1)
                mv = TINC()
                P.wait("v", S["mm"], mv)
                V("tensor_scalar", srowA, PP[3][0:1, 0:512], 1.0 / H, None, ALU.mult)
                REL("v", 6)
                VD("tensor_mul", srowC, srowA, srowA)
                VD("scalar_tensor_tensor", srowB, PP[3][0:1, 512:1024],
                   1.0 / H, srowC, ALU.mult, ALU.subtract)
                REL("v", 7)
                VD("tensor_scalar", srowB, srowB, EPS, None, ALU.add)
                var_ev = VINC()
                P.wait("s", S["ev"], var_ev)
                SACT(srowB, srowB, AF.Ln)
                SDR()
                SACT(srowB, srowB, AF.Exp, scale=-0.5)
                rstd_ev = SINC()
                pe_acq(6)
                pe_acq(7)
                P.wait("t", S["es"], rstd_ev)
                MM(ph(6), onesr[:, :], srowA, True, True)
                MM(ph(7), onesr[:, :], srowB, True, True)
                mv = TINC()
                P.wait("v", S["mm"], mv)
                V("tensor_copy", mBh[:, :], ph(6))
                REL("v", 6)
                V("tensor_copy", rBh[:, :], ph(7))
                REL("v", 7)
                scrf = af[:, 18432:21504]
                def _sub(fc, th=th):
                    sl0 = fc * 1024 + th * 512
                    (VD if fc == 0 else V)("tensor_sub",
                        scrf[:, (fc % 6) * 512:(fc % 6) * 512 + 512],
                        X_[:, sl0:sl0 + 512], mBh[:, :])
                def _mul(fc, th=th):
                    sl0 = fc * 1024 + th * 512
                    (VD if fc == 0 else V)("tensor_mul", XN[:, sl0:sl0 + 512],
                        scrf[:, (fc % 6) * 512:(fc % 6) * 512 + 512], rBh[:, :])
                for fc in range(NFC):
                    _sub(fc)
                    if fc >= 3:
                        _mul(fc - 3)
                for fc in range(NFC - 3, NFC):
                    _mul(fc)
            apply_ev = VINC()
            if with_gate:
                pe_acq(6)
                pe_acq(7)
                P.wait("t", S["ev"], apply_ev)
                for fc in range(NFC):
                    for th in range(2):
                        MM(PP[3][0:8, th * 512:th * 512 + 512],
                           gwt_s[:, fc * 8:(fc + 1) * 8],
                           XN[:, fc * 1024 + th * 512: fc * 1024 + th * 512 + 512],
                           fc == 0, fc == NFC - 1)
                mv = TINC()
                P.wait("s", S["mm"], mv)
                SACT(gate[0:8, G_S:G_S + 512], PP[3][0:8, 0:512], AF.Sigmoid)
                REL("s", 6)
                SACT(gate[0:8, G_S + 512:G_S + 1024], PP[3][0:8, 512:1024], AF.Sigmoid)
                REL("s", 7)
                gate_pb = (pb[6].n, pb[7].n)
                return apply_ev, gate_pb
            return apply_ev, None

        def emit_topk(gate_pb):
            P.wait("v", pb[6], gate_pb[0])
            P.wait("v", pb[7], gate_pb[1])
            V("tensor_scalar", gate[0:8, G_BS:G_BS + 1024], gate[0:8, G_S:G_S + 1024],
              gate[0:8, G_GB:G_GB + 1], None, ALU.add)
            bs_ev = VINC()
            for tc in range(NTC):
                pe_acq(6)
                P.wait("t", S["ev"], bs_ev)
                TTR(ph(6)[:, 0:8], gate[0:8, G_BS + tc * 128:G_BS + tc * 128 + 128], np_=8)
                TTR(ph(6)[:, 8:16], gate[0:8, G_S + tc * 128:G_S + tc * 128 + 128], np_=8)
                mv = TINC()
                P.wait("v", S["mm"], mv)
                V("tensor_copy", topk[:, 0:8], ph(6)[:, 0:8])
                V("tensor_copy", topk[:, 8:16], ph(6)[:, 8:16])
                REL("v", 6)
                VD("reduce_max", topk[:, 32:33], topk[:, 0:8], AXX)
                VD("tensor_single_scalar", topk[:, 16:24], topk[:, 0:8],
                   topk[:, 32:33], ALU.is_equal)
                VD("scalar_tensor_tensor", topk[:, 24:32], topk[:, 16:24],
                   -1e30, topk[:, 0:8], ALU.mult, ALU.add)
                VD("reduce_max", topk[:, 33:34], topk[:, 24:32], AXX)
                VD("tensor_single_scalar", topk[:, 24:32], topk[:, 24:32],
                   topk[:, 33:34], ALU.is_equal)
                VD("tensor_add", topk[:, 16:24], topk[:, 16:24], topk[:, 24:32])
                VD("tensor_mul", topk[:, 8:16], topk[:, 8:16], topk[:, 16:24])
                VD("reduce_sum", topk[:, 34:35], topk[:, 8:16], AXX)
                VD("reciprocal", topk[:, 35:36], topk[:, 34:35])
                VD("tensor_single_scalar", topk[:, 40:48], topk[:, 8:16],
                   topk[:, 35:36], ALU.mult)
                we_ev = VINC()
                pe_acq(7)
                P.wait("t", S["ev"], we_ev)
                TTR(ph(7)[0:8, 0:128], topk[:, 40:48])
                mv2 = TINC()
                P.wait("v", S["mm"], mv2)
                V("tensor_copy", w8sb[0:8, tc * 128:(tc + 1) * 128], ph(7)[0:8, 0:128])
                REL("v", 7)
            return VINC()

        def emit_proj_residual2(wdram, rhs16, bias_base, rhs_ev, rhs_ev_b=None):
            xev = {}
            for sl in range(8):
                sv, wv = SLAB(wdram[:, sl * 256:(sl + 1) * 256], 16, 256)
                P.wait("t", wv[0], wv[1])
                P.wait("t", S["ev"], rhs_ev)
                for oc2 in range(2):
                    oc = sl * 2 + oc2
                    for th in range(2):
                        hp = (oc2 * 2 + th) % 4
                        pe_acq(hp)
                        for fc in range(NFC):
                            if fc == 8 and rhs_ev_b is not None and sl == 0:
                                P.wait("t", S["ev"], rhs_ev_b)
                            MM(ph(hp),
                               sv[:, fc * 256 + oc2 * 128: fc * 256 + oc2 * 128 + 128],
                               rhs16[:, fc * 1024 + th * 512: fc * 1024 + th * 512 + 512],
                               fc == 0, fc == NFC - 1)
                        mv = TINC()
                        P.wait("v", S["mm"], mv)
                        xsl = oc * 1024 + th * 512
                        op = ((oc * 2 + th) % 2) * 512
                        V("tensor_scalar", otmp[:, op:op + 512], ph(hp),
                          bmisc[:, bias_base + oc:bias_base + oc + 1], None, ALU.add)
                        REL("v", hp)
                        VD("tensor_add", X_[:, xsl:xsl + 512], X_[:, xsl:xsl + 512],
                           otmp[:, op:op + 512])
                        if th == 1:
                            xev[oc] = VINC()
                SLAB_DONE()
            return [xev[oc] for oc in range(16)], S["mm"].n

        def emit_attn_tail(variant, qt, Ostag, obias_ap, sumv_last):
            oh = 4 + (qt % 2)
            pe_acq(6)
            P.wait("t", S["sum"], sumv_last)
            sq = (qt % 2) * 512
            MM(ph(6)[0:1, 0:512], misc[:, MC_ONES:MC_ONES + 1],
               sumac[:, sq:sq + 512], True, False)
            MM(ph(6)[0:1, 0:512], misc[:, MC_ONES:MC_ONES + 1],
               sum2[:, sq:sq + 512], False, True)
            mv = TINC()
            P.wait("v", S["mm"], mv)
            V("reciprocal", srowA, ph(6)[0:1, 0:512])
            REL("v", 6)
            rec_pb6 = pb[6].n
            pe_acq(7)
            P.wait("t", pb[6], rec_pb6)
            MM(ph(7), onesr[:, :], srowA, True, True)
            mv2 = TINC()
            P.wait("v", S["mm"], mv2)
            V("tensor_copy", recB[:, :], ph(7))
            REL("v", 7)
            ot = (qt % 2) * 512
            VD("tensor_mul", otmp[:, ot:ot + 512], ph(oh), recB[:, :])
            REL("v", oh)
            VD("tensor_scalar", Ostag[:, qt * 512:(qt + 1) * 512],
               otmp[:, ot:ot + 512], obias_ap, None, ALU.add)
            return VINC()

        def emit_attn_a_unit(Qv, Kv, Vtile, Ostag, obias_ap, slot, rrow,
                             load_val, load_sem):
            sched = []
            for qt in range(8):
                nkc = 4 * (qt + 1)
                for kc in range(nkc):
                    sched.append((qt, kc, nkc))
            njj = len(sched)
            expv = {}
            avv = {}
            sumv = {}
            tail_cast = {}
            LAG = 3

            def emit_S(j):
                qt, kc, nkc = sched[j]
                sb = j % 4
                if pbu[sb] > 0:
                    P.wait("t", pb[sb], pbu[sb])
                pbu[sb] += 1
                if j == 0 and load_val is not None:
                    P.wait("t", load_sem, load_val)
                MM(ph(sb), Kv[:, kc * 128:(kc + 1) * 128],
                   Qv[:, qt * 512:(qt + 1) * 512], True, False)
                MM(ph(sb), onesrb[:, :], rrow[0:1, qt * 512:(qt + 1) * 512],
                   False, True)
                smm = TINC()
                if kc >= 4 * qt:
                    dvar = kc - 4 * qt
                    P.wait("v", S["mm"], smm)
                    VD("tensor_add", ph(sb), ph(sb),
                       WS[:, 12288 + dvar * 512:12288 + (dvar + 1) * 512])
                    sev = VINC()
                    P.wait("s", S["ev"], sev)
                else:
                    P.wait("s", S["mm"], smm)
                p_i = j % 4
                if j >= 4:
                    P.wait("s", S["px"], avv[j - 4])
                    P.wait("s", S["sum"], sumv[j - 4])
                SACT_ND(PT[:, p_i * 512:(p_i + 1) * 512], ph(sb), AF.Exp,
                        bias=kposc[:, kc * 2 + slot:kc * 2 + slot + 1])
                expv[j] = P.inc_last("s", pb[sb], 1)
                P.wait("v", pb[sb], expv[j])
                sbuf_acc = sumac if kc % 2 == 0 else sum2
                sq = (qt % 2) * 512
                if kc < 2:
                    V("tensor_copy", sbuf_acc[:, sq:sq + 512],
                      PT[:, p_i * 512:(p_i + 1) * 512])
                else:
                    V("tensor_add", sbuf_acc[:, sq:sq + 512], sbuf_acc[:, sq:sq + 512],
                      PT[:, p_i * 512:(p_i + 1) * 512])
                sumv[j] = P.inc_last("v", S["sum"], 1)

            def emit_AV(j):
                qt, kc, nkc = sched[j]
                p_i = j % 4
                oh = 4 + (qt % 2)
                if kc == 0:
                    pe_acq(oh)
                P.wait("t", pb[j % 4], expv[j])
                MM(ph(oh), Vtile(kc), PT[:, p_i * 512:(p_i + 1) * 512],
                   kc == 0, kc == nkc - 1)
                avv[j] = P.inc_last("t", S["px"], 1)
                if kc == nkc - 1:
                    tail_cast[qt] = emit_attn_tail("a", qt, Ostag, obias_ap, sumv[j])

            for j in range(njj):
                emit_S(j)
                if j >= LAG:
                    emit_AV(j - LAG)
            for j in range(njj - LAG, njj):
                emit_AV(j)
            return tail_cast[7]

        def emit_attn_b_unit(Qv, Kv, Vtile, Ostag, obias_ap, load_val, load_sem):
            pairs = []
            for qt in range(8):
                for kp in range(16):
                    pairs.append((qt, 2 * kp, 32))
            npj = len(pairs)
            pexp = {}
            avv = {}
            sumv = {}
            tail_cast = {}
            PLAG = 1

            def emit_Spair(jp):
                qt, kc0, nkc = pairs[jp]
                pp = jp % 2
                g = pb[2 * pp]
                if pbu[2 * pp] > 0:
                    P.wait("t", g, pbu[2 * pp])
                pbu[2 * pp] += 1
                pbu[2 * pp + 1] += 0
                if jp == 0 and load_val is not None:
                    P.wait("t", load_sem, load_val)
                for h in range(2):
                    kc = kc0 + h
                    MM(ph(2 * pp + h), Kv[:, kc * 128:(kc + 1) * 128],
                       Qv[:, qt * 512:(qt + 1) * 512], True, True)
                smm = TINC()
                P.wait("s", S["mm"], smm)
                if jp >= 2:
                    jlast_old = chunk_index(jp - 2) + 1
                    P.wait("s", S["px"], avv[jlast_old])
                    P.wait("s", S["sum"], sumv[jlast_old])
                SACT_ND(PT[:, (jp % 2) * 1024:(jp % 2) * 1024 + 1024],
                        PP[pp][:, 0:1024], AF.Exp)
                pexp[jp] = P.inc_last("s", g, 1)
                P.wait("v", g, pexp[jp])
                for h in range(2):
                    kc = kc0 + h
                    j = chunk_index(jp) + h
                    sbuf_acc = sumac if kc % 2 == 0 else sum2
                    sq = (qt % 2) * 512
                    pt_off = (jp % 2) * 1024 + h * 512
                    if kc < 2:
                        V("tensor_copy", sbuf_acc[:, sq:sq + 512],
                          PT[:, pt_off:pt_off + 512])
                    else:
                        V("tensor_add", sbuf_acc[:, sq:sq + 512],
                          sbuf_acc[:, sq:sq + 512], PT[:, pt_off:pt_off + 512])
                    sumv[j] = P.inc_last("v", S["sum"], 1)

            def chunk_index(jp):
                return 2 * jp

            def emit_AVpair(jp):
                qt, kc0, nkc = pairs[jp]
                pp = jp % 2
                oh = 4 + (qt % 2)
                for h in range(2):
                    kc = kc0 + h
                    j = chunk_index(jp) + h
                    if kc == 0:
                        pe_acq(oh)
                    P.wait("t", pb[2 * pp], pexp[jp])
                    pt_off = (jp % 2) * 1024 + h * 512
                    MM(ph(oh), Vtile(kc), PT[:, pt_off:pt_off + 512],
                       kc == 0, kc == nkc - 1)
                    avv[j] = P.inc_last("t", S["px"], 1)
                    if kc == nkc - 1:
                        tail_cast[qt] = emit_attn_tail("b", qt, Ostag, obias_ap,
                                                       sumv[j])

            for jp in range(npj):
                emit_Spair(jp)
                if jp >= PLAG:
                    emit_AVpair(jp - PLAG)
            for jp in range(npj - PLAG, npj):
                emit_AVpair(jp)
            return tail_cast[7]

        P.wait("v", S["ev"], X_EV)
        ln1_ev, _ = emit_ln(with_gate=False, xfc_ev=xfc1)

        piece_st = []
        SLORD = [(d, p) for p in (1, 2, 0) for d in range(8)]
        for sl in range(24):
            d, part = SLORD[sl]
            sv, wv = SLAB(wqkv[:, (d * 3 + part) * 256:(d * 3 + part) * 256 + 256],
                          16, 256)
            pbv = WS[:, (sl % 2) * 2048:(sl % 2) * 2048 + 2048]
            if sl >= 2:
                P.wait("v", S["st"], piece_st[sl - 2])
            P.wait("t", wv[0], wv[1])
            P.wait("t", S["ev"], ln1_ev)
            if part != 2:
                mcb = MC_QKB + (16 if part == 1 else 0) + d * 2
                sc = SCALE_A if part == 0 else 1.0
                for half in range(2):
                    for tc2 in range(2):
                        hp = (half * 2 + tc2) % 4
                        pe_acq(hp)
                        for fc in range(NFC):
                            MM(ph(hp),
                               sv[:, fc * 256 + half * 128: fc * 256 + half * 128 + 128],
                               XN[:, fc * 1024 + tc2 * 512: fc * 1024 + tc2 * 512 + 512],
                               fc == 0, fc == NFC - 1)
                        mv = TINC()
                        P.wait("v", S["mm"], mv)
                        V("tensor_scalar",
                          pbv[:, half * 1024 + tc2 * 512: half * 1024 + tc2 * 512 + 512],
                          ph(hp), sc, bmisc[:, mcb + half:mcb + half + 1],
                          ALU.mult, ALU.add)
                        REL("v", hp)
                pev = VINC()
                SLAB_DONE()
                P.wait("y", S["ev"], pev)
                stv = piece_st[-1] if piece_st else 0
                for half in range(2):
                    if part == 1:
                        dst = cc_ak_in[d * 256 + half * 128:d * 256 + half * 128 + 128, :]
                    else:
                        dst = cc_aq_in[half][d * 128:(d + 1) * 128, :]
                    stv = STD(dst, pbv[:, half * 1024:half * 1024 + 1024], chain=stv)
                piece_st.append(stv)
            else:
                for tc in range(NTC):
                    hp = tc % 4
                    pe_acq(hp)
                    for fc in range(NFC):
                        MM(ph(hp)[:, 0:256],
                           XN[:, fc * 1024 + tc * 128: fc * 1024 + tc * 128 + 128],
                           sv[:, fc * 256:(fc + 1) * 256], fc == 0, fc == NFC - 1)
                    mv = TINC()
                    P.wait("v", S["mm"], mv)
                    V("tensor_copy", pbv[:, tc * 256:(tc + 1) * 256], ph(hp)[:, 0:256])
                    REL("v", hp)
                pev = VINC()
                SLAB_DONE()
                P.wait("y", S["ev"], pev)
                stv = STD(cc_av_in[d * 1024:(d + 1) * 1024, 0:256].rearrange(
                              "(tc p) o -> p tc o", p=128),
                          pbv.rearrange("p (tc o) -> p tc o", o=256),
                          chain=piece_st[-1] if piece_st else 0)
                piece_st.append(stv)
            if sl == 9:
                P.add("g", lambda g, v=piece_st[7]: g.wait_ge(S["st"].h, v))
                cc1k = COLL("AllToAll", cc_ak_in, cc_ak_out)
            if sl == 17:
                P.add("g", lambda g, v=piece_st[15]: g.wait_ge(S["st"].h, v))
                cc1v = COLL("AllToAll", cc_av_in, cc_av_out)
        P.add("g", lambda g: g.wait_ge(S["st"].h, piece_st[-1]))
        cc1q = [COLL("AllToAll", cc_aq_in[i], cc_aq_out[i]) for i in range(2)]
        qkv_mm_end = S["mm"].n

        P.wait("y", S["mm"], qkv_mm_end)
        ostore = []
        px_end = []

        def emit_a_loads(u):
            beta, slot = u % 2, u // 2
            buf = XN if u % 2 == 0 else WS
            vb = 8192 if u % 2 == 0 else 20480
            lsem = S[f"lx{u % 2}"]
            if u == 0:
                P.wait("y", S["cc"], cc1k)
            for j in range(4):
                r = (beta * 4 + j) * 256 + slot * 128
                SYD(buf[:, 4096 + j * 1024:4096 + (j + 1) * 1024],
                    cc_ak_out[r:r + 128, :], sem=lsem)
            if u == 0:
                P.wait("y", S["cc"], cc1v)
            SYD(buf[:, vb:vb + 4096].rearrange("p (kc h) -> p kc h", h=128),
                cc_av_out[beta * 4096:(beta + 1) * 4096,
                          slot * 128:slot * 128 + 128].rearrange(
                              "(kc p) h -> p kc h", p=128), sem=lsem)
            P.wait("y", S["cc"], cc1q[slot])
            lv = None
            for j in range(4):
                r = (beta * 4 + j) * 128
                lv = SYD(buf[:, j * 1024:(j + 1) * 1024],
                         cc_aq_out[slot][r:r + 128, :], sem=lsem)
            return lv

        lv_next = emit_a_loads(0)
        for u in range(4):
            beta, slot = u % 2, u // 2
            lv = lv_next
            buf = XN if u % 2 == 0 else WS
            vb = 8192 if u % 2 == 0 else 20480
            obias = bmisc[:, MC_BVA + slot:MC_BVA + slot + 1]
            rrow = WS[0:1, 16384:20480]
            P.wait("v", S["px"], S["px"].n)
            V("tensor_scalar", rrow, WS[0:1, 8192:12288],
              slopes[0:1, slot * 128:slot * 128 + 1], -1.0, ALU.mult, ALU.mult)
            rr_ev = VINC()
            P.wait("t", S["ev"], rr_ev)
            cast_last = emit_attn_a_unit(
                buf[:, 0:4096], buf[:, 4096:8192],
                lambda kc, b=buf, vo=vb: b[:, vo + kc * 128:vo + (kc + 1) * 128],
                XN[:, 12288:16384], obias, slot, rrow, lv, S[f"lx{u % 2}"])
            px_end.append(S["px"].n)
            if u < 3:
                if u >= 1:
                    P.wait("y", S["px"], px_end[u - 1])
                lv_next = emit_a_loads(u + 1)
            P.wait("y", S["ev"], cast_last)
            for d2 in range(4):
                dd = beta * 4 + d2
                stv = STD(cc_ao_in[slot][dd * 128:(dd + 1) * 128, :],
                          XN[:, 12288 + d2 * 1024:12288 + (d2 + 1) * 1024])
                ostore.append(stv)
            P.wait("v", S["st"], ostore[-1])
            if u == 1:
                P.add("g", lambda g, v=ostore[-1]: g.wait_ge(S["st"].h, v))
                cc2a = COLL("AllToAll", cc_ao_in[0], cc_ao_out[0])
        P.add("g", lambda g: g.wait_ge(S["st"].h, ostore[-1]))
        cc2b = COLL("AllToAll", cc_ao_in[1], cc_ao_out[1])

        P.wait("y", S["cc"], cc2a)
        olv = None
        oev = {}
        P.wait("y", S["px"], px_end[2])
        for fc in range(NFC):
            if fc == 8:
                P.wait("y", S["ld"], olv)
                P.wait("v", S["ld"], olv)
                oev[0] = VINC()
                P.wait("y", S["cc"], cc2b)
            olv = SYD(XN[:, fc * 1024:(fc + 1) * 1024],
                      cc_ao_out[fc // 8][(fc % 8) * 128:(fc % 8 + 1) * 128, :])
        P.wait("v", S["ld"], olv)
        oev[1] = VINC()
        xev_pa, pa_mm = emit_proj_residual2(wproj, XN, MC_BPROJ, oev[0], oev[1])

        ln2_ev, gate_pb2 = emit_ln(with_gate=True, xfc_ev=xev_pa,
                                   xn_guard_mm=pa_mm)
        topk_ev = emit_topk(gate_pb2)

        def emit_ffn(ln_ev, tk_ev, out_cb=None):
            hmul_ev = {0: 0, 1: 0}
            for j in range(22):
                sv, wv = SLAB(sgw[:, j * 256:(j + 1) * 256], 16, 256)
                P.wait("t", wv[0], wv[1])
                P.wait("t", S["ev"], ln_ev)
                pp = j % 2
                for half, pr in ((0, 0), (1, 1)):
                    pe_acq(pr * 2)
                    pe_acq(pr * 2 + 1)
                    for th in range(2):
                        for fc in range(NFC):
                            MM(PP[pr][:, th * 512:th * 512 + 512],
                               sv[:, fc * 256 + half * 128: fc * 256 + half * 128 + 128],
                               XN[:, fc * 1024 + th * 512: fc * 1024 + th * 512 + 512],
                               fc == 0, fc == NFC - 1)
                    TINC()
                mvy = S["mm"].n - 1
                mvg = S["mm"].n
                P.wait("v", S["mm"], mvy)
                V("tensor_copy", WS[:, j * 1024:j * 1024 + 512], PP[0][:, 0:512])
                REL("v", 0)
                V("tensor_copy", WS[:, j * 1024 + 512:(j + 1) * 1024], PP[0][:, 512:1024])
                REL("v", 1)
                P.wait("s", S["mm"], mvg)
                if hmul_ev[pp] > 0:
                    P.wait("s", S["ev"], hmul_ev[pp])
                SACT(WS[:, 22528 + pp * 1024:22528 + pp * 1024 + 512],
                     PP[1][:, 0:512], AF.Silu)
                REL("s", 2)
                SACT(WS[:, 22528 + pp * 1024 + 512:22528 + (pp + 1) * 1024],
                     PP[1][:, 512:1024], AF.Silu)
                REL("s", 3)
                gse_pb = pb[3].n
                P.wait("v", pb[3], gse_pb)
                VD("tensor_mul", WS[:, j * 1024:(j + 1) * 1024],
                   WS[:, j * 1024:(j + 1) * 1024],
                   WS[:, 22528 + pp * 1024:22528 + (pp + 1) * 1024])
                hmul_ev[pp] = VINC()
                SLAB_DONE()
            hsh_ev = VINC()
            for cs in range(8):
                svs = []
                for part in range(2):
                    sv, wv = SLAB(sdw[part * 1408:(part + 1) * 1408,
                                      cs * 256:(cs + 1) * 256], 11, 256)
                    svs.append((sv, wv))
                P.wait("t", svs[0][1][0], svs[0][1][1])
                P.wait("t", svs[1][1][0], svs[1][1][1])
                P.wait("t", S["ev"], hsh_ev)
                for oc2 in range(2):
                    oc = cs * 2 + oc2
                    for th in range(2):
                        hp = (oc2 * 2 + th) % 4
                        pe_acq(hp)
                        for part in range(2):
                            for l in range(11):
                                j = part * 11 + l
                                MM(ph(hp),
                                   svs[part][0][:, l * 256 + oc2 * 128: l * 256 + oc2 * 128 + 128],
                                   WS[:, j * 1024 + th * 512: j * 1024 + th * 512 + 512],
                                   part == 0 and l == 0, part == 1 and l == 10)
                        mv = TINC()
                        P.wait("v", S["mm"], mv)
                        xsl = oc * 1024 + th * 512
                        V("tensor_add", X_[:, xsl:xsl + 512], X_[:, xsl:xsl + 512], ph(hp))
                        REL("v", hp)
                SLAB_DONE()
                SLAB_DONE()
            for e in range(8):
                pe_acq(6)
                pe_acq(7)
                P.wait("t", S["ev"], tk_ev)
                MM(ph(6), onehot[0:8, e * 128:(e + 1) * 128], w8sb[0:8, 0:512], True, True)
                MM(ph(7), onehot[0:8, e * 128:(e + 1) * 128], w8sb[0:8, 512:1024], True, True)
                mv = TINC()
                P.wait("v", S["mm"], mv)
                V("tensor_copy", weB[:, 0:512], ph(6))
                REL("v", 6)
                V("tensor_copy", weB[:, 512:1024], ph(7))
                REL("v", 7)
                hmul_ev = {0: 0, 1: 0}
                xev = {}
                for j in range(11):
                    sv, wv = SLAB(upw[e, :, j * 256:(j + 1) * 256], 16, 256)
                    P.wait("t", wv[0], wv[1])
                    pp = j % 2
                    for half, pr in ((0, 0), (1, 1)):
                        pe_acq(pr * 2)
                        pe_acq(pr * 2 + 1)
                        for th in range(2):
                            for fc in range(NFC):
                                MM(PP[pr][:, th * 512:th * 512 + 512],
                                   sv[:, fc * 256 + half * 128: fc * 256 + half * 128 + 128],
                                   XN[:, fc * 1024 + th * 512: fc * 1024 + th * 512 + 512],
                                   fc == 0, fc == NFC - 1)
                        TINC()
                    mvg = S["mm"].n - 1
                    mvu = S["mm"].n
                    P.wait("s", S["mm"], mvg)
                    if hmul_ev[pp] > 0:
                        P.wait("s", S["ev"], hmul_ev[pp])
                    SACT(WS[:, 22528 + pp * 1024:22528 + pp * 1024 + 512],
                         PP[0][:, 0:512], AF.Silu)
                    REL("s", 0)
                    SACT(WS[:, 22528 + pp * 1024 + 512:22528 + (pp + 1) * 1024],
                         PP[0][:, 512:1024], AF.Silu)
                    REL("s", 1)
                    gse_pb = pb[1].n
                    P.wait("v", S["mm"], mvu)
                    VD("tensor_mul", WS[:, j * 1024:j * 1024 + 512],
                       PP[1][:, 0:512], weB[:, 0:512])
                    REL("v", 2)
                    V("tensor_mul", WS[:, j * 1024 + 512:(j + 1) * 1024],
                      PP[1][:, 512:1024], weB[:, 512:1024])
                    REL("v", 3)
                    P.wait("v", pb[1], gse_pb)
                    VD("tensor_mul", WS[:, j * 1024:(j + 1) * 1024],
                       WS[:, j * 1024:(j + 1) * 1024],
                       WS[:, 22528 + pp * 1024:22528 + (pp + 1) * 1024])
                    hmul_ev[pp] = VINC()
                    SLAB_DONE()
                he_ev = VINC()
                xn_free_mm = S["mm"].n
                for cs in range(8):
                    sv, wv = SLAB(dnw[e, :, cs * 256:(cs + 1) * 256], 11, 256)
                    P.wait("t", wv[0], wv[1])
                    P.wait("t", S["ev"], he_ev)
                    for oc2 in range(2):
                        oc = cs * 2 + oc2
                        for th in range(2):
                            hp = (oc2 * 2 + th) % 4
                            pe_acq(hp)
                            for l in range(11):
                                MM(ph(hp),
                                   sv[:, l * 256 + oc2 * 128: l * 256 + oc2 * 128 + 128],
                                   WS[:, l * 1024 + th * 512: l * 1024 + th * 512 + 512],
                                   l == 0, l == 10)
                            mv = TINC()
                            P.wait("v", S["mm"], mv)
                            xsl = oc * 1024 + th * 512
                            V("tensor_add", X_[:, xsl:xsl + 512],
                              X_[:, xsl:xsl + 512], ph(hp))
                            REL("v", hp)
                            if e == 7 and th == 1:
                                xev[oc] = VINC()
                    SLAB_DONE()
                    if e == 7 and out_cb is not None and cs in (3, 7):
                        out_cb(cs // 4, xev[cs * 2 + 1])
            return (VINC(), [xev[oc] for oc in range(16)], xn_free_mm,
                    S["mm"].n)

        ffn1_ev, xev_f1, f1_xn_mm, f1_all_mm = emit_ffn(ln2_ev, topk_ev)

        ln3_ev, _ = emit_ln(with_gate=False, xfc_ev=xev_f1,
                            xn_guard_mm=f1_xn_mm, sq_guard_mm=f1_all_mm)
        lat_st = []
        slab_st = {}
        for sl in range(4):
            src = wkd if sl < 2 else wqd
            c0 = (sl % 2) * 256
            sv, wv = SLAB(src[:, c0:c0 + 256], 16, 256)
            pbv = WS[:, (sl % 2) * 2048:(sl % 2) * 2048 + 2048]
            if sl >= 2:
                P.wait("v", S["st"], slab_st[sl - 2])
            P.wait("t", wv[0], wv[1])
            P.wait("t", S["ev"], ln3_ev)
            for half in range(2):
                for tc2 in range(2):
                    hp = (half * 2 + tc2) % 4
                    pe_acq(hp)
                    for fc in range(NFC):
                        MM(ph(hp),
                           sv[:, fc * 256 + half * 128: fc * 256 + half * 128 + 128],
                           XN[:, fc * 1024 + tc2 * 512: fc * 1024 + tc2 * 512 + 512],
                           fc == 0, fc == NFC - 1)
                    mv = TINC()
                    P.wait("v", S["mm"], mv)
                    V("tensor_copy",
                      pbv[:, half * 1024 + tc2 * 512: half * 1024 + tc2 * 512 + 512],
                      ph(hp))
                    REL("v", hp)
            pev = VINC()
            SLAB_DONE()
            P.wait("y", S["ev"], pev)
            dst_t = cc_bkv_in if sl < 2 else cc_bq_in
            stv = None
            for half in range(2):
                r0 = c0 + half * 128
                stv = STD(dst_t[r0:r0 + 128, :],
                          pbv[:, half * 1024:half * 1024 + 1024],
                          chain=lat_st[-1] if lat_st else 0)
                lat_st.append(stv)
            slab_st[sl] = stv
            if sl == 2:
                P.add("g", lambda g, v=slab_st[1]: g.wait_ge(S["st"].h, v))
                cc3 = COLL("AllGather", cc_bkv_in, cc_bkv_out)
        P.add("g", lambda g: g.wait_ge(S["st"].h, lat_st[-1]))
        cc3q = COLL("AllGather", cc_bq_in, cc_bq_out)
        lat_mm_end = S["mm"].n

        P.wait("y", S["cc"], cc3)
        P.wait("y", S["mm"], lat_mm_end)
        P.wait("v", S["st"], lat_st[-1])
        bostore = []
        LATB = [XN[:, 0:8192], WS[:, 16384:24576]]
        grp_mm = {}
        grp_ev = {}
        grp_ld = {}
        GROUPS = []
        for u in range(4):
            if u // 2 == 0:
                GROUPS += [(u, side, lh) for side in ("kv", "q") for lh in range(2)]
            else:
                GROUPS += [(u, side, lh) for lh in range(2) for side in ("kv", "q")]

        def emit_b_loads(gi):
            u, side, lh = GROUPS[gi]
            beta = u % 2
            latb = LATB[gi % 2]
            if gi >= 2:
                P.wait("y", S["mm"], grp_mm[gi - 2])
            src_t = cc_bkv_out if side == "kv" else cc_bq_out
            if side == "q" and gi == 2:
                P.wait("y", S["cc"], cc3q)
            lvl = None
            for l in range(4):
                for j in range(2):
                    r = (beta * 4 + lh * 2 + j) * 512 + l * 128
                    lvl = SYD(latb[:, l * 2048 + j * 1024: l * 2048 + j * 1024 + 1024],
                              src_t[r:r + 128, :], sem=S[f"lx{gi % 2}"])
            grp_ld[gi] = lvl

        emit_b_loads(0)
        emit_b_loads(1)
        for u in range(4):
            beta, slot = u % 2, u // 2
            for k in range(4):
                gi = u * 4 + k
                side, lh = GROUPS[gi][1], GROUPS[gi][2]
                latb = LATB[gi % 2]
                if slot == 1 and side == "kv":
                    if gi >= 1:
                        P.wait("g", S["ev"], grp_ev[gi - 1])
                    lvp = GPD(WS[:, 12288:14336], pa_in[:, lh * 2048:(lh + 1) * 2048])
                    lvp = GPD(WS[:, 14336:16384], pb_in[:, lh * 2048:(lh + 1) * 2048])
                else:
                    lvp = 0
                P.wait("t", S[f"lx{gi % 2}"], grp_ld[gi])
                if lvp:
                    P.wait("t", S["gp"], lvp)
                wof = {"q": 0, "kv": 1024}[side]
                dst = 8192 if side == "q" else 0
                beffc = MC_BEFFQ if side == "q" else MC_BEFFK
                bia = misc[:, beffc:beffc + 1]
                bia1 = misc[:, beffc + 1:beffc + 2]
                ncomp = 1 if slot == 0 else 2
                for ci in range(ncomp):
                    outv = (WS[:, dst + lh * 2048:dst + (lh + 1) * 2048] if slot == 0
                            else XN[:, 8192 + ci * 2048:8192 + (ci + 1) * 2048])
                    for tt in range(4):
                        hp = tt % 2
                        pe_acq(hp)
                        for l in range(4):
                            MM(ph(hp),
                               upslc[:, wof + l * 256 + ci * 128: wof + l * 256 + ci * 128 + 128],
                               latb[:, l * 2048 + tt * 512: l * 2048 + tt * 512 + 512],
                               l == 0, l == 3)
                        mv = TINC()
                        P.wait("v", S["mm"], mv)
                        bb = bia if ci == 0 else bia1
                        sca = SCALE_B if side == "q" else 1.0
                        V("tensor_scalar", outv[:, tt * 512:(tt + 1) * 512],
                          ph(hp), sca, bb, ALU.mult, ALU.add)
                        REL("v", hp)
                if slot == 1:
                    VD("tensor_mul", WS[:, dst + lh * 2048:dst + (lh + 1) * 2048],
                       XN[:, 8192:10240], WS[:, 12288:14336])
                    VD("tensor_mul", XN[:, 8192:10240],
                       XN[:, 10240:12288], WS[:, 14336:16384])
                    VD("tensor_add", WS[:, dst + lh * 2048:dst + (lh + 1) * 2048],
                       WS[:, dst + lh * 2048:dst + (lh + 1) * 2048],
                       XN[:, 8192:10240])
                if side == "kv":
                    for tq in range(4):
                        hp = 2 + (tq % 2)
                        pe_acq(hp)
                        for k in range(4):
                            tt = tq * 4 + k
                            for l in range(4):
                                MM(ph(hp)[:, k * 128:(k + 1) * 128],
                                   latb[:, l * 2048 + tt * 128: l * 2048 + tt * 128 + 128],
                                   upslc[:, 2048 + l * 256 + slot * 128:
                                         2048 + l * 256 + slot * 128 + 128],
                                   l == 0, l == 3)
                        mv = TINC()
                        P.wait("v", S["mm"], mv)
                        kc0 = lh * 16 + tq * 4
                        V("tensor_copy",
                          WS[:, 4096 + kc0 * 128:4096 + kc0 * 128 + 512],
                          ph(hp)[:, 0:512])
                        REL("v", hp)
                grp_mm[gi] = S["mm"].n
                grp_ev[gi] = VINC()
                if gi + 2 < 16:
                    emit_b_loads(gi + 2)
            bqk_ev = VINC()
            last_bqk = bqk_ev
            P.wait("t", S["ev"], bqk_ev)
            if u > 0:
                P.wait("v", S["st"], bostore[-1])
            obias = misc[:, MC_BEFFV + slot:MC_BEFFV + slot + 1]
            cast_last = emit_attn_b_unit(
                WS[:, 8192:12288], WS[:, 0:4096],
                lambda kc: WS[:, 4096 + kc * 128:4096 + (kc + 1) * 128],
                XN[:, 12288:16384], obias, None, None)
            P.wait("y", S["ev"], cast_last)
            for d2 in range(4):
                dd = beta * 4 + d2
                stv = STD(cc_bo_in[slot][dd * 128:(dd + 1) * 128, :],
                          XN[:, 12288 + d2 * 1024:12288 + (d2 + 1) * 1024])
                bostore.append(stv)
            if u == 1:
                P.add("g", lambda g, v=bostore[-1]: g.wait_ge(S["st"].h, v))
                cc4a = COLL("AllToAll", cc_bo_in[0], cc_bo_out[0])
        P.add("g", lambda g: g.wait_ge(S["st"].h, bostore[-1]))
        cc4b = COLL("AllToAll", cc_bo_in[1], cc_bo_out[1])

        P.wait("y", S["cc"], cc4a)
        olv = None
        oevb = {}
        P.wait("y", S["mm"], grp_mm[14])
        P.wait("y", S["ev"], last_bqk)
        for fc in range(NFC):
            if fc == 8:
                P.wait("y", S["ld"], olv)
                P.wait("v", S["ld"], olv)
                oevb[0] = VINC()
                P.wait("y", S["cc"], cc4b)
            olv = SYD(XN[:, fc * 1024:(fc + 1) * 1024],
                      cc_bo_out[fc // 8][(fc % 8) * 128:(fc % 8 + 1) * 128, :])
        P.wait("v", S["ld"], olv)
        oevb[1] = VINC()
        xev_pb, pb_mm = emit_proj_residual2(wo, XN, MC_BO, oevb[0], oevb[1])

        ln4_ev, gate_pb4 = emit_ln(with_gate=True, xfc_ev=xev_pb,
                                   xn_guard_mm=pb_mm)
        topk4_ev = emit_topk(gate_pb4)

        out_st = []

        def emit_out_half(hf, gate_ev):
            for tc in range(NTC):
                stg = xstgA if len(out_st) % 2 == 0 else xstgB
                if len(out_st) >= 2:
                    P.wait("v", S["st"], out_st[-2])
                for fh in range(2):
                    hp = 4 + fh
                    pe_acq(hp)
                    if tc == 0 and fh == 0:
                        P.wait("t", S["ev"], gate_ev)
                    for k in range(4):
                        fc = hf * 8 + fh * 4 + k
                        TTR(ph(hp)[:, k * 128:(k + 1) * 128],
                            X_[:, fc * 1024 + tc * 128: fc * 1024 + tc * 128 + 128])
                    mv = TINC()
                    P.wait("v", S["mm"], mv)
                    V("tensor_copy", stg[:, fh * 512:(fh + 1) * 512], ph(hp)[:, 0:512])
                    REL("v", hp)
                sev = VINC()
                P.wait("y", S["ev"], sev)
                stv = STD(out_t[tc * 128:(tc + 1) * 128, hf * 1024:(hf + 1) * 1024],
                          stg, chain=out_st[-1] if out_st else 0)
                out_st.append(stv)

        ffn2_ev, _, _, _ = emit_ffn(ln4_ev, topk4_ev, out_cb=emit_out_half)
        P.add("y", lambda y: y.wait_ge(S["st"].h, out_st[-1]))

        block.tensor(lambda e: [f(e) for f in P.q["t"]])
        block.vector(lambda e: [f(e) for f in P.q["v"]])
        block.scalar(lambda e: [f(e) for f in P.q["s"]])
        block.gpsimd(lambda e: [f(e) for f in P.q["g"]])
        block.sync(lambda e: [f(e) for f in P.q["y"]])

    return nc


def _prep_inputs(ins):
    f32 = np.float32
    xfull = np.ascontiguousarray(ins["L0_attn_a_hidden_states"], dtype=f32)

    perm = []
    for d in range(8):
        for part in range(3):
            for hh in (d, d + 8):
                base = part * H + hh * 128
                perm.extend(range(base, base + 128))
    perm = np.array(perm)
    wqkv_p = np.ascontiguousarray(ins["L0_attn_a_w_qkv"][:, perm], dtype=f32)
    bqkv_p = np.ascontiguousarray(ins["L0_attn_a_b_qkv"][perm], dtype=f32)

    operm = np.arange(2048)
    wproj_p = np.ascontiguousarray(ins["L0_attn_a_w_proj"][operm, :], f32)
    wo_p = np.ascontiguousarray(ins["L0_attn_b_Wo"].T[operm, :], f32)

    wqd_t = np.ascontiguousarray(ins["L0_attn_b_Wq_down"].T, f32)
    wkd_t = np.ascontiguousarray(ins["L0_attn_b_Wkv_down"].T, f32)

    sperm = []
    for j in range(22):
        sperm.extend(range(j * 128, (j + 1) * 128))
        sperm.extend(range(2816 + j * 128, 2816 + (j + 1) * 128))
    sperm = np.array(sperm)
    sgw_ti = np.ascontiguousarray(ins["L0_ffn_shared_gw"].T[:, sperm], f32)
    sdw_t = np.ascontiguousarray(ins["L0_ffn_shared_dw"].T, f32)
    uperm = []
    for j in range(11):
        uperm.extend(range(j * 128, (j + 1) * 128))
        uperm.extend(range(1408 + j * 128, 1408 + (j + 1) * 128))
    uperm = np.array(uperm)
    upw_ti = np.ascontiguousarray(ins["L0_ffn_up_w"].transpose(0, 2, 1)[:, :, uperm], f32)
    dnw_t = np.ascontiguousarray(ins["L0_ffn_down_w"].transpose(0, 2, 1), f32)
    gwt_t = np.ascontiguousarray(ins["L0_ffn_gate_w"].T, f32)

    ident = np.eye(128, dtype=f32)
    ones_row = np.ones((1, 128), f32)
    ones_colb = np.ones((128, 1), bfnp)
    onehot8 = np.zeros((8, 1024), bfnp)
    for e in range(8):
        onehot8[e, e * 128:(e + 1) * 128] = 1
    qpos_bf = np.arange(SEQ, dtype=f32)[None, :].astype(bfnp)
    slopes = _alibi_slopes(NH)
    cos64, sin64 = _rope_cos_sin(SEQ, H // 4)

    dm = np.zeros((128, 2048), np.float64)
    for dvar in range(4):
        pcol = dvar * 128 + np.arange(128)[:, None]
        fcol = np.arange(512)[None, :]
        dm[:, dvar * 512:(dvar + 1) * 512] = np.where(pcol > fcol, -30000.0, 0.0)
    diag_masks = dm.astype(bfnp)

    gate_bias_p = np.asarray(ins["L0_ffn_gate_bias"], f32).reshape(8, 1)

    base = dict(
        wqkv_p=wqkv_p, wproj_p=wproj_p, wo_p=wo_p, wqd_t=wqd_t, wkd_t=wkd_t,
        shared_gw_ti=sgw_ti, shared_dw_t=sdw_t, up_w_ti=upw_ti, down_w_t=dnw_t,
        gate_w_t=gwt_t, gate_bias_p=gate_bias_p,
        ident_in=ident, ones_row=ones_row, ones_colb=ones_colb, onehot8=onehot8,
        qpos_bf=qpos_bf, diag_masks=diag_masks,
    )

    in_maps = []
    for c in range(8):
        m = dict(base)
        beta, r = c // 4, c % 4
        m["x_slice"] = np.ascontiguousarray(xfull[beta, r * TOK:(r + 1) * TOK, :], f32)
        hA, hB = c, c + 8
        mm = c % 4
        bm = np.zeros((128, 80), f32)
        bm[:, MCB("BPROJ"):MCB("BPROJ") + 16] = np.asarray(
            ins["L0_attn_a_b_proj"], f32).reshape(16, 128).T
        bm[:, MCB("BO"):MCB("BO") + 16] = np.asarray(
            ins["L0_attn_b_bo"], f32).reshape(16, 128).T
        bq = np.asarray(ins["L0_attn_a_b_qkv"], f32)
        bm[:, MCB("BVA") + 0] = bq[2 * H + hA * 128: 2 * H + (hA + 1) * 128]
        bm[:, MCB("BVA") + 1] = bq[2 * H + hB * 128: 2 * H + (hB + 1) * 128]
        for d in range(8):
            for half in range(2):
                hh = d + 8 * half
                bm[:, 48 + d * 2 + half] = bq[hh * 128:(hh + 1) * 128]
                bm[:, 64 + d * 2 + half] = bq[2048 + hh * 128: 2048 + (hh + 1) * 128]
        bm[:, MCB("BQD"):MCB("BQD") + 4] = np.asarray(
            ins["L0_attn_b_bq_down"], f32).reshape(4, 128).T
        bm[:, MCB("BKD"):MCB("BKD") + 4] = np.asarray(
            ins["L0_attn_b_bkv_down"], f32).reshape(4, 128).T
        blk_c = mm if c < 4 else mm + 4
        blk_o = mm + 4 if c < 4 else mm
        qcols = list(range(blk_c * 128, (blk_c + 1) * 128)) + \
            list(range(blk_o * 128, (blk_o + 1) * 128))
        m["wqu_s"] = np.ascontiguousarray(ins["L0_attn_b_Wq_up"].T[:, qcols], f32)
        m["wku_s"] = np.ascontiguousarray(ins["L0_attn_b_Wk_up"].T[:, qcols], f32)
        vcols = list(range(hA * 128, (hA + 1) * 128)) + \
            list(range(hB * 128, (hB + 1) * 128))
        m["wvu_s"] = np.ascontiguousarray(ins["L0_attn_b_Wv_up"].T[:, vcols], f32)
        bqu = np.asarray(ins["L0_attn_b_bq_up"], f32)
        bku = np.asarray(ins["L0_attn_b_bk_up"], f32)
        bvu = np.asarray(ins["L0_attn_b_bv_up"], f32)
        bm[:, MCB("BQU") + 0] = bqu[blk_c * 128:(blk_c + 1) * 128]
        bm[:, MCB("BQU") + 1] = bqu[blk_o * 128:(blk_o + 1) * 128]
        bm[:, MCB("BKU") + 0] = bku[blk_c * 128:(blk_c + 1) * 128]
        bm[:, MCB("BKU") + 1] = bku[blk_o * 128:(blk_o + 1) * 128]
        bm[:, MCB("BVU") + 0] = bvu[hA * 128:(hA + 1) * 128]
        bm[:, MCB("BVU") + 1] = bvu[hB * 128:(hB + 1) * 128]
        m["bias_misc"] = bm
        sl2 = np.zeros((1, 256), np.float64)
        sl2[0, 0:128] = slopes[hA]
        sl2[0, 128:256] = slopes[hB]
        m["slope_row2"] = sl2.astype(f32)
        kpc = np.zeros((128, 64), np.float64)
        for kc in range(32):
            pos = kc * 128 + np.arange(128)
            kpc[:, kc * 2 + 0] = slopes[hA] * pos
            kpc[:, kc * 2 + 1] = slopes[hB] * pos
        m["kposc"] = kpc.astype(f32)
        pa = cos64[:, mm * 128:(mm + 1) * 128].T
        sgn = -1.0 if c < 4 else 1.0
        pbp = sgn * sin64[:, mm * 128:(mm + 1) * 128].T
        m["ropePA"] = np.ascontiguousarray(pa, f32)
        m["ropePB"] = np.ascontiguousarray(pbp, f32)
        in_maps.append(m)
    return in_maps


def MCB(name):
    return {"BPROJ": 0, "BO": 16, "BVA": 32, "BQD": 34, "BKD": 38,
            "BQU": 42, "BKU": 44, "BVU": 46}[name]


_PROG_CACHE = {}


def kernel(**inputs):
    in_maps = _prep_inputs(inputs)
    if "nc" not in _PROG_CACHE:
        _PROG_CACHE["nc"] = build_program()
    nc = _PROG_CACHE["nc"]
    with allow_ici_all_to_all():
        res = run_bass_kernel_spmd(nc, in_maps, core_ids=list(range(8)))
    out = np.zeros((B, SEQ, H), np.float32)
    for c in range(8):
        beta, r = c // 4, c % 4
        out[beta, r * TOK:(r + 1) * TOK, :] = np.asarray(res.results[c]["out_slice"])
    return out


# version 13
# speedup vs baseline: 1.0099x; 1.0026x over previous
import math
import numpy as np
import ml_dtypes
from contextlib import ExitStack

import concourse.bass as bass
import concourse.mybir as mybir
from concourse.bass_utils import run_bass_kernel_spmd
from concourse.aot_env import allow_ici_all_to_all

F32 = mybir.dt.float32
BF16 = mybir.dt.bfloat16
AF = mybir.ActivationFunctionType
ALU = mybir.AluOpType
AXX = mybir.AxisListType.X

B = 2
SEQ = 4096
H = 2048
NH = 16
HD = 128
TOK = 1024
NTC = 8
NFC = 16
LAT = 512
EPS = 1e-5
SCALE_A = 1.0 / HD
SCALE_B = 1.0 / math.sqrt(HD)
bfnp = ml_dtypes.bfloat16


def _alibi_slopes(n):
    start = 2.0 ** (-2.0 ** (-(math.log2(n) - 3)))
    return np.array([start * start ** i for i in range(n)], np.float64)


def _rope_cos_sin(seq_len, dim, base=10000.0):
    inv_freq = 1.0 / base ** (np.arange(0, dim, 2, dtype=np.float64) / dim)
    t = np.arange(seq_len, dtype=np.float64)
    freqs = np.outer(t, inv_freq)
    emb = np.concatenate([freqs, freqs], -1)
    return np.cos(emb), np.sin(emb)


class Sem:
    __slots__ = ("h", "n")

    def __init__(self, handle):
        self.h = handle
        self.n = 0


class Plan:
    def __init__(self):
        self.q = {e: [] for e in ("t", "v", "s", "g", "y")}

    def add(self, eng, fn, real=False):
        fn.real = real
        self.q[eng].append(fn)

    def wait(self, eng, sem, val):
        if val > 0:
            w = lambda e, _h=sem.h, _v=val: e.wait_ge(_h, _v)
            w.real = False
            self.q[eng].append(w)

    def inc_last(self, eng, sem, k=1):
        if self.q[eng] and getattr(self.q[eng][-1], "real", False):
            prev = self.q[eng].pop()
            def wrapped(e, _p=prev, _h=sem.h, _k=k):
                inst = _p(e)
                inst.then_inc(_h, _k)
            wrapped.real = False
            self.q[eng].append(wrapped)
        else:
            f = lambda e, _h=sem.h, _k=k: e.drain().then_inc(_h, _k)
            f.real = False
            self.q[eng].append(f)
        sem.n += k
        return sem.n


def build_program(upto=99):
    nc = bass.Bass("TRN2", target_bir_lowering=False, debug=False)

    def din(name, shape, dt=F32):
        return nc.dram_tensor(name, shape, dt, kind="ExternalInput").ap()

    x_in = din("x_slice", [TOK, H])
    wqkv = din("wqkv_p", [H, 3 * H])
    wproj = din("wproj_p", [H, H])
    wo = din("wo_p", [H, H])
    wqd = din("wqd_t", [H, LAT])
    wkd = din("wkd_t", [H, LAT])
    sgw = din("shared_gw_ti", [H, 5632])
    sdw = din("shared_dw_t", [2816, H])
    upw = din("up_w_ti", [8, H, 2816])
    dnw = din("down_w_t", [8, 1408, H])
    gwt = din("gate_w_t", [H, 8])
    wqu_s = din("wqu_s", [LAT, 256])
    wku_s = din("wku_s", [LAT, 256])
    wvu_s = din("wvu_s", [LAT, 256])
    bias_misc = din("bias_misc", [128, 80])
    gbias_in = din("gate_bias_p", [8, 1])
    ident_in = din("ident_in", [128, 128])
    onesr_in = din("ones_row", [1, 128])
    onesc_in = din("ones_colb", [128, 1], BF16)
    onehot_in = din("onehot8", [8, 1024], BF16)
    slope_in = din("slope_row2", [1, 256])
    kposc_in = din("kposc", [128, 64])
    qpos_in = din("qpos_bf", [1, SEQ], BF16)
    diag_in = din("diag_masks", [128, 2048], BF16)
    pa_in = din("ropePA", [128, SEQ])
    pb_in = din("ropePB", [128, SEQ])

    out_t = nc.dram_tensor("out_slice", [TOK, H], F32, kind="ExternalOutput").ap()

    cc_ak_in = nc.dram_tensor("cc_ak_in", [2048, TOK], BF16)
    cc_ak_out = nc.dram_tensor("cc_ak_out", [2048, TOK], BF16)
    cc_aq_in = [nc.dram_tensor(f"cc_aq_in{i}", [1024, TOK], BF16) for i in range(2)]
    cc_aq_out = [nc.dram_tensor(f"cc_aq_out{i}", [1024, TOK], BF16) for i in range(2)]
    cc_av_in = nc.dram_tensor("cc_av_in", [8 * TOK, 256], BF16)
    cc_av_out = nc.dram_tensor("cc_av_out", [8 * TOK, 256], BF16)
    cc_ao_in = [nc.dram_tensor(f"cc_ao_in{i}", [8 * 128, TOK], BF16) for i in range(2)]
    cc_ao_out = [nc.dram_tensor(f"cc_ao_out{i}", [8 * 128, TOK], BF16) for i in range(2)]
    cc_bq_in = nc.dram_tensor("cc_bq_in", [512, TOK], BF16)
    cc_bq_out = nc.dram_tensor("cc_bq_out", [8 * 512, TOK], BF16)
    cc_bkv_in = nc.dram_tensor("cc_bkv_in", [512, TOK], BF16)
    cc_bkv_out = nc.dram_tensor("cc_bkv_out", [8 * 512, TOK], BF16)
    cc_bo_in = [nc.dram_tensor(f"cc_bo_in{i}", [8 * 128, TOK], BF16) for i in range(2)]
    cc_bo_out = [nc.dram_tensor(f"cc_bo_out{i}", [8 * 128, TOK], BF16) for i in range(2)]

    MC_BPROJ = 0
    MC_BO = 16
    MC_BVA = 32
    MC_QKB = 48
    MC_BQD = 34
    MC_BKD = 38
    MC_BQU = 42
    MC_BKU = 44
    MC_BVU = 46

    with ExitStack() as stack:
        E = stack.enter_context

        af = E(nc.sbuf_tensor("arena_f32", [128, 22016], F32))
        X_ = af[:, 0:16384]
        dscr = af[:, 16384:17408]
        weB = dscr
        mBh = af[:, 17408:17920]
        rBh = af[:, 17920:18432]
        sumac = af[:, 18432:19456]
        recB = af[:, 19456:19968]
        otmp = af[:, 19968:20992]
        sum2 = af[:, 20992:22016]
        xstgA = af[:, 16384:17408]
        xstgB = af[:, 17408:18432]

        ab = E(nc.sbuf_tensor("arena_b16", [128, 51200], BF16))
        XN = ab[:, 0:16384]
        WS = ab[:, 16384:40960]
        WT = ab[:, 40960:49152]
        PT = ab[:, 49152:51200]

        misc = E(nc.sbuf_tensor("misc_f32", [128, 16], F32))
        MC_ONES = 0
        MC_BEFFQ = 2
        MC_BEFFK = 4
        MC_BEFFV = 6
        gate = E(nc.sbuf_tensor("gate_f32", [8, 2064], F32))
        G_S, G_BS, G_GB = 0, 1024, 2048
        srowA = af[0:1, 16384:16896]
        srowB = af[0:1, 16896:17408]
        srowC = af[0:1, 19456:19968]
        topk = E(nc.sbuf_tensor("topk_f32", [128, 64], F32))
        w8sb = E(nc.sbuf_tensor("w8sb_b16", [8, 1024], BF16))
        onehot = E(nc.sbuf_tensor("onehot_b16", [8, 1024], BF16))
        ident = E(nc.sbuf_tensor("ident_f32", [128, 128], F32))
        onesr = E(nc.sbuf_tensor("onesr_f32", [1, 128], F32))
        onescb = E(nc.sbuf_tensor("onescb_b16", [128, 1], BF16))
        onesrb = E(nc.sbuf_tensor("onesrb_b16", [1, 128], BF16))
        slopes = E(nc.sbuf_tensor("slopes_f32", [1, 256], F32))
        upslc = E(nc.sbuf_tensor("upslc_b16", [128, 3072], BF16))
        gwt_s = E(nc.sbuf_tensor("gwt_b16", [128, 128], BF16))
        bmisc = E(nc.sbuf_tensor("bmisc_f32", [128, 80], F32))
        bqdb = E(nc.sbuf_tensor("bqdb_b16", [128, 8], BF16))
        kposc = E(nc.sbuf_tensor("kposc_f32", [128, 64], F32))

        PP = [E(nc.psum_tensor(f"pp{i}", [128, 1024], F32)) for i in range(4)]

        S = {nm: Sem(E(nc.semaphore(nm))) for nm in
             ["ld", "lx0", "lx1", "w0", "w1", "wf", "mm", "ev", "es", "cc", "st",
              "px", "sum", "gp", "pb0", "pb1", "pb2", "pb3", "pb4", "pb5", "pb6",
              "pb7"]}
        pb = [S[f"pb{i}"] for i in range(8)]
        pbu = [0] * 8

        block = E(nc.Block())
        P = Plan()

        def ph(i):
            return PP[i // 2][:, (i % 2) * 512:(i % 2) * 512 + 512]

        def pe_acq(i):
            if pbu[i] > 0:
                P.wait("t", pb[i], pbu[i])
            pbu[i] += 1

        def REL(eng, i):
            P.inc_last(eng, pb[i], 1)

        def MM(out, lhsT, rhs, start, stop):
            P.add("t", lambda t, o=out, l=lhsT, r=rhs, st=start, sp=stop:
                  t.matmul(o, l, r, start=st, stop=sp), real=True)

        def TTR(out, in_, np_=128):
            P.add("t", lambda t, o=out, i=in_, n=np_: t.transpose(o, i, ident[0:n, 0:n]),
                  real=True)

        def TINC(sem=None, k=1):
            return P.inc_last("t", sem or S["mm"], k)

        def VINC(sem=None, k=1):
            return P.inc_last("v", sem or S["ev"], k)

        def SINC(sem=None, k=1):
            return P.inc_last("s", sem or S["es"], k)

        def V(meth, *args):
            def f(v, m=meth, a=args):
                v.drain()
                return getattr(v, m)(*a)
            P.add("v", f, real=True)

        VD = V

        def SDR():
            P.add("s", lambda s: s.drain())

        def SC(meth, *args, **kw):
            P.add("s", lambda s, m=meth, a=args, k=kw: getattr(s, m)(*a, **k))

        def SACT(out, in_, func, bias=0.0, scale=1.0, accum=None):
            if accum is None:
                def f1(s, o=out, i=in_, fn=func, b=bias, sc=scale):
                    s.drain()
                    return s.activation(out=o, in_=i, func=fn, bias=b, scale=sc)
                P.add("s", f1, real=True)
            else:
                def f2(s, o=out, i=in_, fn=func, b=bias, sc=scale, ac=accum):
                    s.drain()
                    return s.activation(out=o, in_=i, func=fn, bias=b, scale=sc,
                                        accum_out=ac)
                P.add("s", f2, real=True)

        def SACT_ND(out, in_, func, bias=0.0, scale=1.0):
            P.add("s", lambda s, o=out, i=in_, fn=func, b=bias, sc=scale:
                  s.activation(out=o, in_=i, func=fn, bias=b, scale=sc), real=True)

        def SYD(out, in_, transpose=False, sem=None):
            sm = sem or S["ld"]
            P.add("y", lambda y, o=out, i=in_, tr=transpose, _h=sm.h:
                  y.dma_start(out=o, in_=i, transpose=tr).then_inc(_h, 16))
            sm.n += 16
            return sm.n

        def STD(out, in_, chain=0):
            def _f(y, o=out, i=in_, ch=chain):
                if ch > 0:
                    y.wait_ge(S["st"].h, ch)
                y.dma_start(out=o, in_=i).then_inc(S["st"].h, 16)
            P.add("y", _f)
            S["st"].n += 16
            return S["st"].n

        def GPD(out, in_, sem=None):
            sm = sem or S["gp"]
            P.add("g", lambda g, o=out, i=in_, _h=sm.h:
                  g.dma_start(out=o, in_=i).then_inc(_h, 16))
            sm.n += 16
            return sm.n

        def COLL(kind, ins, outs):
            def _do(g, k=kind, i=ins, o=outs):
                with allow_ici_all_to_all():
                    g.collective_compute(k, ALU.bypass,
                                         replica_groups=[list(range(8))],
                                         ins=[i.ap().opt()], outs=[o.ap().opt()]
                                         ).then_inc(S["cc"].h, 1)
            P.add("g", _do)
            S["cc"].n += 1
            return S["cc"].n

        wstate = {"idx": 0}
        slab_free_mm = {}

        def SLAB(ap2d, nf, osz):
            si = wstate["idx"]
            wstate["idx"] += 1
            par = si % 2
            wsem = S[f"w{par}"]
            sv = WT[:, par * 4096:par * 4096 + nf * osz]
            prev = si - 2
            def _dma(g, dr=ap2d, s2=sv, oz=osz, pv=prev, wh=wsem.h):
                if pv >= 0:
                    g.wait_ge(S["mm"].h, slab_free_mm[pv])
                g.dma_start(out=s2.rearrange("p (c o) -> p c o", o=oz),
                            in_=dr.rearrange("(c p) o -> p c o", p=128)
                            ).then_inc(wh, 16)
            P.add("g", _dma)
            wsem.n += 16
            return sv, (wsem, wsem.n)

        def SLAB_DONE():
            slab_free_mm[len(slab_free_mm)] = S["mm"].n

        GPD(ident[:, :], ident_in)
        GPD(onesr[:, :], onesr_in)
        GPD(onescb[:, :], onesc_in)
        GPD(slopes[:, :], slope_in)
        GPD(onehot[:, :], onehot_in)
        GPD(bmisc[:, :], bias_misc)
        GPD(gate[0:8, G_GB:G_GB + 1], gbias_in)
        GPD(WS[0:1, 8192:12288], qpos_in)
        GPD(WS[:, 12288:14336], diag_in)
        GPD(gwt_s[:, :].rearrange("p (c o) -> p c o", o=8),
            gwt.rearrange("(c p) o -> p c o", p=128))
        GPD(upslc[:, 0:1024].rearrange("p (c o) -> p c o", o=256),
            wqu_s.rearrange("(c p) o -> p c o", p=128))
        GPD(upslc[:, 1024:2048].rearrange("p (c o) -> p c o", o=256),
            wku_s.rearrange("(c p) o -> p c o", p=128))
        GPD(upslc[:, 2048:3072].rearrange("p (c o) -> p c o", o=256),
            wvu_s.rearrange("(c p) o -> p c o", p=128))
        GPD(bqdb[:, 0:4], bias_misc[:, MC_BQD:MC_BQD + 4])
        GPD(bqdb[:, 4:8], bias_misc[:, MC_BKD:MC_BKD + 4])
        CONST = GPD(kposc[:, :], kposc_in)

        P.wait("v", S["gp"], CONST)
        V("memset", misc[:, MC_ONES:MC_ONES + 1], 1.0)
        V("memset", onesrb[:, :], 1.0)
        V("tensor_scalar", bmisc[:, MC_QKB:MC_QKB + 16],
          bmisc[:, MC_QKB:MC_QKB + 16], SCALE_A, None, ALU.mult)

        P.wait("t", S["gp"], CONST)
        xmv = {}
        xfc1 = []
        for i in range(NTC * 2):
            hf, tc = i // 8, i % 8
            stg = xstgA if i % 2 == 0 else xstgB
            if i >= 2:
                P.wait("y", S["mm"], xmv[i - 2])
            lv = SYD(stg, x_in[tc * 128:(tc + 1) * 128, hf * 1024:(hf + 1) * 1024],
                     sem=S[f"lx{i % 2}"])
            P.wait("t", S[f"lx{i % 2}"], lv)
            for fh in range(2):
                hp = 4 + fh
                pe_acq(hp)
                for k in range(4):
                    f8 = fh * 4 + k
                    TTR(ph(hp)[:, k * 128:(k + 1) * 128],
                        stg[:, f8 * 128:f8 * 128 + 128])
                mv = TINC()
                P.wait("v", S["mm"], mv)
                V("tensor_copy",
                  X_[:, hf * 8192:(hf + 1) * 8192].rearrange(
                      "p (f o) -> p f o", o=1024)[:, fh * 4:fh * 4 + 4,
                                                  tc * 128:(tc + 1) * 128],
                  ph(hp)[:, 0:512].rearrange("p (f o) -> p f o", o=128))
                REL("v", hp)
            xmv[i] = mv
            if tc == 7:
                hev = VINC()
                xfc1 += [hev] * 8
        X_EV = VINC()

        for woff, bcol, ocol, scl in ((0, MC_BQU, MC_BEFFQ, SCALE_B),
                                      (1024, MC_BKU, MC_BEFFK, 1.0),
                                      (2048, MC_BVU, MC_BEFFV, 1.0)):
            bsrc = 0 if woff == 0 else (4 if woff == 1024 else 4)
            for blk in range(2):
                pe_acq(6)
                P.wait("t", S["gp"], CONST)
                for l in range(4):
                    MM(ph(6)[:, 0:1],
                       upslc[:, woff + l * 256 + blk * 128: woff + l * 256 + blk * 128 + 128],
                       bqdb[:, bsrc + l: bsrc + l + 1], l == 0, l == 3)
                mv = TINC()
                P.wait("v", S["mm"], mv)
                V("scalar_tensor_tensor", misc[:, ocol + blk:ocol + blk + 1],
                  ph(6)[:, 0:1], 1.0, bmisc[:, bcol + blk:bcol + blk + 1],
                  ALU.mult, ALU.add)
                if scl != 1.0:
                    VD("tensor_scalar", misc[:, ocol + blk:ocol + blk + 1],
                       misc[:, ocol + blk:ocol + blk + 1], scl, None, ALU.mult)
                REL("v", 6)
        BEFF_EV = VINC()

        def emit_ln(with_gate, xfc_ev=None, xn_guard_mm=0, sq_guard_mm=0):
            entry_ev = S["ev"].n
            if sq_guard_mm:
                P.wait("s", S["mm"], sq_guard_mm)
            for fc in range(NFC):
                P.wait("s", S["ev"], xfc_ev[fc] if xfc_ev else entry_ev)
                SACT_ND(WS[:, fc * 512:(fc + 1) * 512],
                        X_[:, fc * 1024:fc * 1024 + 512], AF.Square)
                SACT_ND(WS[:, 14336 + fc * 512:14336 + (fc + 1) * 512],
                        X_[:, fc * 1024 + 512:(fc + 1) * 1024], AF.Square)
            cmk = {}
            if xn_guard_mm:
                P.wait("s", S["mm"], xn_guard_mm)
            for fc in range(NFC):
                SACT_ND(XN[:, fc * 1024:(fc + 1) * 1024],
                        X_[:, fc * 1024:(fc + 1) * 1024], AF.Copy)
                cmk[fc] = SINC()
            amk = {}
            for th in range(2):
                sqb = 0 if th == 0 else 14336
                pe_acq(6)
                pe_acq(7)
                for fc in range(NFC):
                    P.wait("t", S["es"], cmk[fc])
                    MM(PP[3][0:1, 0:512], onescb[:, :],
                       XN[:, fc * 1024 + th * 512: fc * 1024 + th * 512 + 512],
                       fc == 0, fc == NFC - 1)
                    MM(PP[3][0:1, 512:1024], onescb[:, :],
                       WS[:, sqb + fc * 512:sqb + (fc + 1) * 512],
                       fc == 0, fc == NFC - 1)
                mv = TINC()
                P.wait("v", S["mm"], mv)
                V("tensor_scalar", srowA, PP[3][0:1, 0:512], 1.0 / H, None, ALU.mult)
                REL("v", 6)
                VD("tensor_mul", srowC, srowA, srowA)
                VD("scalar_tensor_tensor", srowB, PP[3][0:1, 512:1024],
                   1.0 / H, srowC, ALU.mult, ALU.subtract)
                REL("v", 7)
                VD("tensor_scalar", srowB, srowB, EPS, None, ALU.add)
                var_ev = VINC()
                P.wait("s", S["ev"], var_ev)
                SACT(srowB, srowB, AF.Ln)
                SDR()
                SACT(srowB, srowB, AF.Exp, scale=-0.5)
                rstd_ev = SINC()
                pe_acq(6)
                pe_acq(7)
                P.wait("t", S["es"], rstd_ev)
                MM(ph(6), onesr[:, :], srowA, True, True)
                MM(ph(7), onesr[:, :], srowB, True, True)
                mv = TINC()
                P.wait("v", S["mm"], mv)
                V("tensor_copy", mBh[:, :], ph(6))
                REL("v", 6)
                V("tensor_copy", rBh[:, :], ph(7))
                REL("v", 7)
                scrf = af[:, 18432:21504]
                def _sub(fc, th=th):
                    sl0 = fc * 1024 + th * 512
                    (VD if fc == 0 else V)("tensor_sub",
                        scrf[:, (fc % 6) * 512:(fc % 6) * 512 + 512],
                        X_[:, sl0:sl0 + 512], mBh[:, :])
                def _mul(fc, th=th):
                    sl0 = fc * 1024 + th * 512
                    (VD if fc == 0 else V)("tensor_mul", XN[:, sl0:sl0 + 512],
                        scrf[:, (fc % 6) * 512:(fc % 6) * 512 + 512], rBh[:, :])
                for fc in range(NFC):
                    _sub(fc)
                    if fc >= 3:
                        _mul(fc - 3)
                        if th == 1:
                            amk[fc - 3] = VINC()
                for fc in range(NFC - 3, NFC):
                    _mul(fc)
                    if th == 1:
                        amk[fc] = VINC()
            apply_ev = VINC()
            amkl = [amk[fc] for fc in range(NFC)]
            if with_gate:
                pe_acq(6)
                pe_acq(7)
                for fc in range(NFC):
                    P.wait("t", S["ev"], amk[fc])
                    for th in range(2):
                        MM(PP[3][0:8, th * 512:th * 512 + 512],
                           gwt_s[:, fc * 8:(fc + 1) * 8],
                           XN[:, fc * 1024 + th * 512: fc * 1024 + th * 512 + 512],
                           fc == 0, fc == NFC - 1)
                mv = TINC()
                P.wait("s", S["mm"], mv)
                SACT(gate[0:8, G_S:G_S + 512], PP[3][0:8, 0:512], AF.Sigmoid)
                REL("s", 6)
                SACT(gate[0:8, G_S + 512:G_S + 1024], PP[3][0:8, 512:1024], AF.Sigmoid)
                REL("s", 7)
                gate_pb = (pb[6].n, pb[7].n)
                return apply_ev, gate_pb, amkl
            return apply_ev, None, amkl

        def emit_topk(gate_pb):
            P.wait("v", pb[6], gate_pb[0])
            P.wait("v", pb[7], gate_pb[1])
            V("tensor_scalar", gate[0:8, G_BS:G_BS + 1024], gate[0:8, G_S:G_S + 1024],
              gate[0:8, G_GB:G_GB + 1], None, ALU.add)
            bs_ev = VINC()
            for tc in range(NTC):
                pe_acq(6)
                P.wait("t", S["ev"], bs_ev)
                TTR(ph(6)[:, 0:8], gate[0:8, G_BS + tc * 128:G_BS + tc * 128 + 128], np_=8)
                TTR(ph(6)[:, 8:16], gate[0:8, G_S + tc * 128:G_S + tc * 128 + 128], np_=8)
                mv = TINC()
                P.wait("v", S["mm"], mv)
                V("tensor_copy", topk[:, 0:8], ph(6)[:, 0:8])
                V("tensor_copy", topk[:, 8:16], ph(6)[:, 8:16])
                REL("v", 6)
                VD("reduce_max", topk[:, 32:33], topk[:, 0:8], AXX)
                VD("tensor_single_scalar", topk[:, 16:24], topk[:, 0:8],
                   topk[:, 32:33], ALU.is_equal)
                VD("scalar_tensor_tensor", topk[:, 24:32], topk[:, 16:24],
                   -1e30, topk[:, 0:8], ALU.mult, ALU.add)
                VD("reduce_max", topk[:, 33:34], topk[:, 24:32], AXX)
                VD("tensor_single_scalar", topk[:, 24:32], topk[:, 24:32],
                   topk[:, 33:34], ALU.is_equal)
                VD("tensor_add", topk[:, 16:24], topk[:, 16:24], topk[:, 24:32])
                VD("tensor_mul", topk[:, 8:16], topk[:, 8:16], topk[:, 16:24])
                VD("reduce_sum", topk[:, 34:35], topk[:, 8:16], AXX)
                VD("reciprocal", topk[:, 35:36], topk[:, 34:35])
                VD("tensor_single_scalar", topk[:, 40:48], topk[:, 8:16],
                   topk[:, 35:36], ALU.mult)
                we_ev = VINC()
                pe_acq(7)
                P.wait("t", S["ev"], we_ev)
                TTR(ph(7)[0:8, 0:128], topk[:, 40:48])
                mv2 = TINC()
                P.wait("v", S["mm"], mv2)
                V("tensor_copy", w8sb[0:8, tc * 128:(tc + 1) * 128], ph(7)[0:8, 0:128])
                REL("v", 7)
            return VINC()

        def emit_proj_residual2(wdram, rhs16, bias_base, rhs_ev, rhs_ev_b=None):
            xev = {}
            for sl in range(8):
                sv, wv = SLAB(wdram[:, sl * 256:(sl + 1) * 256], 16, 256)
                P.wait("t", wv[0], wv[1])
                P.wait("t", S["ev"], rhs_ev)
                for oc2 in range(2):
                    oc = sl * 2 + oc2
                    for th in range(2):
                        hp = (oc2 * 2 + th) % 4
                        pe_acq(hp)
                        for fc in range(NFC):
                            if fc == 8 and rhs_ev_b is not None and sl == 0:
                                P.wait("t", S["ev"], rhs_ev_b)
                            MM(ph(hp),
                               sv[:, fc * 256 + oc2 * 128: fc * 256 + oc2 * 128 + 128],
                               rhs16[:, fc * 1024 + th * 512: fc * 1024 + th * 512 + 512],
                               fc == 0, fc == NFC - 1)
                        mv = TINC()
                        P.wait("v", S["mm"], mv)
                        xsl = oc * 1024 + th * 512
                        op = ((oc * 2 + th) % 2) * 512
                        V("tensor_scalar", otmp[:, op:op + 512], ph(hp),
                          bmisc[:, bias_base + oc:bias_base + oc + 1], None, ALU.add)
                        REL("v", hp)
                        VD("tensor_add", X_[:, xsl:xsl + 512], X_[:, xsl:xsl + 512],
                           otmp[:, op:op + 512])
                        if th == 1:
                            xev[oc] = VINC()
                SLAB_DONE()
            return [xev[oc] for oc in range(16)], S["mm"].n

        def emit_attn_tail(variant, qt, Ostag, obias_ap, sumv_last):
            oh = 4 + (qt % 2)
            pe_acq(6)
            P.wait("t", S["sum"], sumv_last)
            sq = (qt % 2) * 512
            MM(ph(6)[0:1, 0:512], misc[:, MC_ONES:MC_ONES + 1],
               sumac[:, sq:sq + 512], True, False)
            MM(ph(6)[0:1, 0:512], misc[:, MC_ONES:MC_ONES + 1],
               sum2[:, sq:sq + 512], False, True)
            mv = TINC()
            P.wait("v", S["mm"], mv)
            V("reciprocal", srowA, ph(6)[0:1, 0:512])
            REL("v", 6)
            rec_pb6 = pb[6].n
            pe_acq(7)
            P.wait("t", pb[6], rec_pb6)
            MM(ph(7), onesr[:, :], srowA, True, True)
            mv2 = TINC()
            P.wait("v", S["mm"], mv2)
            V("tensor_copy", recB[:, :], ph(7))
            REL("v", 7)
            ot = (qt % 2) * 512
            VD("tensor_mul", otmp[:, ot:ot + 512], ph(oh), recB[:, :])
            REL("v", oh)
            VD("tensor_scalar", Ostag[:, qt * 512:(qt + 1) * 512],
               otmp[:, ot:ot + 512], obias_ap, None, ALU.add)
            return VINC()

        def emit_attn_a_unit(Qv, Kv, Vtile, Ostag, obias_ap, slot, rrow,
                             load_val, load_sem):
            sched = []
            for qt in range(8):
                nkc = 4 * (qt + 1)
                for kc in range(nkc):
                    sched.append((qt, kc, nkc))
            njj = len(sched)
            expv = {}
            avv = {}
            sumv = {}
            tail_cast = {}
            LAG = 3

            def emit_S(j):
                qt, kc, nkc = sched[j]
                sb = j % 4
                if pbu[sb] > 0:
                    P.wait("t", pb[sb], pbu[sb])
                pbu[sb] += 1
                if j == 0 and load_val is not None:
                    P.wait("t", load_sem, load_val)
                MM(ph(sb), Kv[:, kc * 128:(kc + 1) * 128],
                   Qv[:, qt * 512:(qt + 1) * 512], True, False)
                MM(ph(sb), onesrb[:, :], rrow[0:1, qt * 512:(qt + 1) * 512],
                   False, True)
                smm = TINC()
                if kc >= 4 * qt:
                    dvar = kc - 4 * qt
                    P.wait("v", S["mm"], smm)
                    VD("tensor_add", ph(sb), ph(sb),
                       WS[:, 12288 + dvar * 512:12288 + (dvar + 1) * 512])
                    sev = VINC()
                    P.wait("s", S["ev"], sev)
                else:
                    P.wait("s", S["mm"], smm)
                p_i = j % 4
                if j >= 4:
                    P.wait("s", S["px"], avv[j - 4])
                    P.wait("s", S["sum"], sumv[j - 4])
                SACT_ND(PT[:, p_i * 512:(p_i + 1) * 512], ph(sb), AF.Exp,
                        bias=kposc[:, kc * 2 + slot:kc * 2 + slot + 1])
                expv[j] = P.inc_last("s", pb[sb], 1)
                P.wait("v", pb[sb], expv[j])
                sbuf_acc = sumac if kc % 2 == 0 else sum2
                sq = (qt % 2) * 512
                if kc < 2:
                    V("tensor_copy", sbuf_acc[:, sq:sq + 512],
                      PT[:, p_i * 512:(p_i + 1) * 512])
                else:
                    V("tensor_add", sbuf_acc[:, sq:sq + 512], sbuf_acc[:, sq:sq + 512],
                      PT[:, p_i * 512:(p_i + 1) * 512])
                sumv[j] = P.inc_last("v", S["sum"], 1)

            def emit_AV(j):
                qt, kc, nkc = sched[j]
                p_i = j % 4
                oh = 4 + (qt % 2)
                if kc == 0:
                    pe_acq(oh)
                P.wait("t", pb[j % 4], expv[j])
                MM(ph(oh), Vtile(kc), PT[:, p_i * 512:(p_i + 1) * 512],
                   kc == 0, kc == nkc - 1)
                avv[j] = P.inc_last("t", S["px"], 1)
                if kc == nkc - 1:
                    tail_cast[qt] = emit_attn_tail("a", qt, Ostag, obias_ap, sumv[j])

            for j in range(njj):
                emit_S(j)
                if j >= LAG:
                    emit_AV(j - LAG)
            for j in range(njj - LAG, njj):
                emit_AV(j)
            return tail_cast[7]

        def emit_attn_b_unit(Qv, Kv, Vtile, Ostag, obias_ap, load_val, load_sem):
            pairs = []
            for qt in range(8):
                for kp in range(16):
                    pairs.append((qt, 2 * kp, 32))
            npj = len(pairs)
            pexp = {}
            avv = {}
            sumv = {}
            tail_cast = {}
            PLAG = 1

            def emit_Spair(jp):
                qt, kc0, nkc = pairs[jp]
                pp = jp % 2
                g = pb[2 * pp]
                if pbu[2 * pp] > 0:
                    P.wait("t", g, pbu[2 * pp])
                pbu[2 * pp] += 1
                pbu[2 * pp + 1] += 0
                if jp == 0 and load_val is not None:
                    P.wait("t", load_sem, load_val)
                for h in range(2):
                    kc = kc0 + h
                    MM(ph(2 * pp + h), Kv[:, kc * 128:(kc + 1) * 128],
                       Qv[:, qt * 512:(qt + 1) * 512], True, True)
                smm = TINC()
                P.wait("s", S["mm"], smm)
                if jp >= 2:
                    jlast_old = chunk_index(jp - 2) + 1
                    P.wait("s", S["px"], avv[jlast_old])
                    P.wait("s", S["sum"], sumv[jlast_old])
                SACT_ND(PT[:, (jp % 2) * 1024:(jp % 2) * 1024 + 1024],
                        PP[pp][:, 0:1024], AF.Exp)
                pexp[jp] = P.inc_last("s", g, 1)
                P.wait("v", g, pexp[jp])
                for h in range(2):
                    kc = kc0 + h
                    j = chunk_index(jp) + h
                    sbuf_acc = sumac if kc % 2 == 0 else sum2
                    sq = (qt % 2) * 512
                    pt_off = (jp % 2) * 1024 + h * 512
                    if kc < 2:
                        V("tensor_copy", sbuf_acc[:, sq:sq + 512],
                          PT[:, pt_off:pt_off + 512])
                    else:
                        V("tensor_add", sbuf_acc[:, sq:sq + 512],
                          sbuf_acc[:, sq:sq + 512], PT[:, pt_off:pt_off + 512])
                    sumv[j] = P.inc_last("v", S["sum"], 1)

            def chunk_index(jp):
                return 2 * jp

            def emit_AVpair(jp):
                qt, kc0, nkc = pairs[jp]
                pp = jp % 2
                oh = 4 + (qt % 2)
                for h in range(2):
                    kc = kc0 + h
                    j = chunk_index(jp) + h
                    if kc == 0:
                        pe_acq(oh)
                    P.wait("t", pb[2 * pp], pexp[jp])
                    pt_off = (jp % 2) * 1024 + h * 512
                    MM(ph(oh), Vtile(kc), PT[:, pt_off:pt_off + 512],
                       kc == 0, kc == nkc - 1)
                    avv[j] = P.inc_last("t", S["px"], 1)
                    if kc == nkc - 1:
                        tail_cast[qt] = emit_attn_tail("b", qt, Ostag, obias_ap,
                                                       sumv[j])

            for jp in range(npj):
                emit_Spair(jp)
                if jp >= PLAG:
                    emit_AVpair(jp - PLAG)
            for jp in range(npj - PLAG, npj):
                emit_AVpair(jp)
            return tail_cast[7]

        P.wait("v", S["ev"], X_EV)
        ln1_ev, _, ln1_amk = emit_ln(with_gate=False, xfc_ev=xfc1)

        piece_st = []
        SLORD = [(d, p) for p in (1, 2, 0) for d in range(8)]
        for sl in range(24):
            d, part = SLORD[sl]
            sv, wv = SLAB(wqkv[:, (d * 3 + part) * 256:(d * 3 + part) * 256 + 256],
                          16, 256)
            pbv = WS[:, (sl % 2) * 2048:(sl % 2) * 2048 + 2048]
            if sl >= 2:
                P.wait("v", S["st"], piece_st[sl - 2])
            P.wait("t", wv[0], wv[1])
            if sl > 0:
                P.wait("t", S["ev"], ln1_ev)
            if part != 2:
                mcb = MC_QKB + (16 if part == 1 else 0) + d * 2
                sc = SCALE_A if part == 0 else 1.0
                for half in range(2):
                    for tc2 in range(2):
                        hp = (half * 2 + tc2) % 4
                        pe_acq(hp)
                        for fc in range(NFC):
                            if sl == 0 and half == 0 and tc2 == 0:
                                P.wait("t", S["ev"], ln1_amk[fc])
                            MM(ph(hp),
                               sv[:, fc * 256 + half * 128: fc * 256 + half * 128 + 128],
                               XN[:, fc * 1024 + tc2 * 512: fc * 1024 + tc2 * 512 + 512],
                               fc == 0, fc == NFC - 1)
                        mv = TINC()
                        P.wait("v", S["mm"], mv)
                        V("tensor_scalar",
                          pbv[:, half * 1024 + tc2 * 512: half * 1024 + tc2 * 512 + 512],
                          ph(hp), sc, bmisc[:, mcb + half:mcb + half + 1],
                          ALU.mult, ALU.add)
                        REL("v", hp)
                pev = VINC()
                SLAB_DONE()
                P.wait("y", S["ev"], pev)
                stv = piece_st[-1] if piece_st else 0
                for half in range(2):
                    if part == 1:
                        dst = cc_ak_in[d * 256 + half * 128:d * 256 + half * 128 + 128, :]
                    else:
                        dst = cc_aq_in[half][d * 128:(d + 1) * 128, :]
                    stv = STD(dst, pbv[:, half * 1024:half * 1024 + 1024], chain=stv)
                piece_st.append(stv)
            else:
                for tc in range(NTC):
                    hp = tc % 4
                    pe_acq(hp)
                    for fc in range(NFC):
                        MM(ph(hp)[:, 0:256],
                           XN[:, fc * 1024 + tc * 128: fc * 1024 + tc * 128 + 128],
                           sv[:, fc * 256:(fc + 1) * 256], fc == 0, fc == NFC - 1)
                    mv = TINC()
                    P.wait("v", S["mm"], mv)
                    V("tensor_copy", pbv[:, tc * 256:(tc + 1) * 256], ph(hp)[:, 0:256])
                    REL("v", hp)
                pev = VINC()
                SLAB_DONE()
                P.wait("y", S["ev"], pev)
                stv = STD(cc_av_in[d * 1024:(d + 1) * 1024, 0:256].rearrange(
                              "(tc p) o -> p tc o", p=128),
                          pbv.rearrange("p (tc o) -> p tc o", o=256),
                          chain=piece_st[-1] if piece_st else 0)
                piece_st.append(stv)
            if sl == 9:
                P.add("g", lambda g, v=piece_st[7]: g.wait_ge(S["st"].h, v))
                cc1k = COLL("AllToAll", cc_ak_in, cc_ak_out)
            if sl == 17:
                P.add("g", lambda g, v=piece_st[15]: g.wait_ge(S["st"].h, v))
                cc1v = COLL("AllToAll", cc_av_in, cc_av_out)
        P.add("g", lambda g: g.wait_ge(S["st"].h, piece_st[-1]))
        cc1q = [COLL("AllToAll", cc_aq_in[i], cc_aq_out[i]) for i in range(2)]
        qkv_mm_end = S["mm"].n

        P.wait("y", S["mm"], qkv_mm_end)
        ostore = []
        px_end = []

        def emit_a_loads(u):
            beta, slot = u % 2, u // 2
            buf = XN if u % 2 == 0 else WS
            vb = 8192 if u % 2 == 0 else 20480
            lsem = S[f"lx{u % 2}"]
            if u == 0:
                P.wait("y", S["cc"], cc1k)
            for j in range(4):
                r = (beta * 4 + j) * 256 + slot * 128
                SYD(buf[:, 4096 + j * 1024:4096 + (j + 1) * 1024],
                    cc_ak_out[r:r + 128, :], sem=lsem)
            if u == 0:
                P.wait("y", S["cc"], cc1v)
            SYD(buf[:, vb:vb + 4096].rearrange("p (kc h) -> p kc h", h=128),
                cc_av_out[beta * 4096:(beta + 1) * 4096,
                          slot * 128:slot * 128 + 128].rearrange(
                              "(kc p) h -> p kc h", p=128), sem=lsem)
            P.wait("y", S["cc"], cc1q[slot])
            lv = None
            for j in range(4):
                r = (beta * 4 + j) * 128
                lv = SYD(buf[:, j * 1024:(j + 1) * 1024],
                         cc_aq_out[slot][r:r + 128, :], sem=lsem)
            return lv

        lv_next = emit_a_loads(0)
        for u in range(4):
            beta, slot = u % 2, u // 2
            lv = lv_next
            buf = XN if u % 2 == 0 else WS
            vb = 8192 if u % 2 == 0 else 20480
            obias = bmisc[:, MC_BVA + slot:MC_BVA + slot + 1]
            rrow = WS[0:1, 16384:20480]
            P.wait("v", S["px"], S["px"].n)
            V("tensor_scalar", rrow, WS[0:1, 8192:12288],
              slopes[0:1, slot * 128:slot * 128 + 1], -1.0, ALU.mult, ALU.mult)
            rr_ev = VINC()
            P.wait("t", S["ev"], rr_ev)
            cast_last = emit_attn_a_unit(
                buf[:, 0:4096], buf[:, 4096:8192],
                lambda kc, b=buf, vo=vb: b[:, vo + kc * 128:vo + (kc + 1) * 128],
                XN[:, 12288:16384], obias, slot, rrow, lv, S[f"lx{u % 2}"])
            px_end.append(S["px"].n)
            if u < 3:
                if u >= 1:
                    P.wait("y", S["px"], px_end[u - 1])
                lv_next = emit_a_loads(u + 1)
            P.wait("y", S["ev"], cast_last)
            for d2 in range(4):
                dd = beta * 4 + d2
                stv = STD(cc_ao_in[slot][dd * 128:(dd + 1) * 128, :],
                          XN[:, 12288 + d2 * 1024:12288 + (d2 + 1) * 1024])
                ostore.append(stv)
            P.wait("v", S["st"], ostore[-1])
            if u == 1:
                P.add("g", lambda g, v=ostore[-1]: g.wait_ge(S["st"].h, v))
                cc2a = COLL("AllToAll", cc_ao_in[0], cc_ao_out[0])
        P.add("g", lambda g: g.wait_ge(S["st"].h, ostore[-1]))
        cc2b = COLL("AllToAll", cc_ao_in[1], cc_ao_out[1])

        P.wait("y", S["cc"], cc2a)
        olv = None
        oev = {}
        P.wait("y", S["px"], px_end[2])
        for fc in range(NFC):
            if fc == 8:
                P.wait("y", S["ld"], olv)
                P.wait("v", S["ld"], olv)
                oev[0] = VINC()
                P.wait("y", S["cc"], cc2b)
            olv = SYD(XN[:, fc * 1024:(fc + 1) * 1024],
                      cc_ao_out[fc // 8][(fc % 8) * 128:(fc % 8 + 1) * 128, :])
        P.wait("v", S["ld"], olv)
        oev[1] = VINC()
        xev_pa, pa_mm = emit_proj_residual2(wproj, XN, MC_BPROJ, oev[0], oev[1])

        ln2_ev, gate_pb2, ln2_amk = emit_ln(with_gate=True, xfc_ev=xev_pa,
                                            xn_guard_mm=pa_mm)
        topk_ev = emit_topk(gate_pb2)

        def emit_ffn(ln_ev, tk_ev, out_cb=None, ln_amk=None):
            hmul_ev = {0: 0, 1: 0}
            for j in range(22):
                sv, wv = SLAB(sgw[:, j * 256:(j + 1) * 256], 16, 256)
                P.wait("t", wv[0], wv[1])
                if j > 0 or ln_amk is None:
                    P.wait("t", S["ev"], ln_ev)
                pp = j % 2
                for half, pr in ((0, 0), (1, 1)):
                    pe_acq(pr * 2)
                    pe_acq(pr * 2 + 1)
                    for th in range(2):
                        for fc in range(NFC):
                            if j == 0 and half == 0 and th == 0 and ln_amk:
                                P.wait("t", S["ev"], ln_amk[fc])
                            MM(PP[pr][:, th * 512:th * 512 + 512],
                               sv[:, fc * 256 + half * 128: fc * 256 + half * 128 + 128],
                               XN[:, fc * 1024 + th * 512: fc * 1024 + th * 512 + 512],
                               fc == 0, fc == NFC - 1)
                    TINC()
                mvy = S["mm"].n - 1
                mvg = S["mm"].n
                P.wait("v", S["mm"], mvy)
                V("tensor_copy", WS[:, j * 1024:j * 1024 + 512], PP[0][:, 0:512])
                REL("v", 0)
                V("tensor_copy", WS[:, j * 1024 + 512:(j + 1) * 1024], PP[0][:, 512:1024])
                REL("v", 1)
                P.wait("s", S["mm"], mvg)
                if hmul_ev[pp] > 0:
                    P.wait("s", S["ev"], hmul_ev[pp])
                SACT(WS[:, 22528 + pp * 1024:22528 + pp * 1024 + 512],
                     PP[1][:, 0:512], AF.Silu)
                REL("s", 2)
                SACT(WS[:, 22528 + pp * 1024 + 512:22528 + (pp + 1) * 1024],
                     PP[1][:, 512:1024], AF.Silu)
                REL("s", 3)
                gse_pb = pb[3].n
                P.wait("v", pb[3], gse_pb)
                VD("tensor_mul", WS[:, j * 1024:(j + 1) * 1024],
                   WS[:, j * 1024:(j + 1) * 1024],
                   WS[:, 22528 + pp * 1024:22528 + (pp + 1) * 1024])
                hmul_ev[pp] = VINC()
                SLAB_DONE()
            hsh_ev = VINC()
            for cs in range(8):
                svs = []
                for part in range(2):
                    sv, wv = SLAB(sdw[part * 1408:(part + 1) * 1408,
                                      cs * 256:(cs + 1) * 256], 11, 256)
                    svs.append((sv, wv))
                P.wait("t", svs[0][1][0], svs[0][1][1])
                P.wait("t", svs[1][1][0], svs[1][1][1])
                P.wait("t", S["ev"], hsh_ev)
                for oc2 in range(2):
                    oc = cs * 2 + oc2
                    for th in range(2):
                        hp = (oc2 * 2 + th) % 4
                        pe_acq(hp)
                        for part in range(2):
                            for l in range(11):
                                j = part * 11 + l
                                MM(ph(hp),
                                   svs[part][0][:, l * 256 + oc2 * 128: l * 256 + oc2 * 128 + 128],
                                   WS[:, j * 1024 + th * 512: j * 1024 + th * 512 + 512],
                                   part == 0 and l == 0, part == 1 and l == 10)
                        mv = TINC()
                        P.wait("v", S["mm"], mv)
                        xsl = oc * 1024 + th * 512
                        V("tensor_add", X_[:, xsl:xsl + 512], X_[:, xsl:xsl + 512], ph(hp))
                        REL("v", hp)
                SLAB_DONE()
                SLAB_DONE()
            for e in range(8):
                pe_acq(6)
                pe_acq(7)
                P.wait("t", S["ev"], tk_ev)
                MM(ph(6), onehot[0:8, e * 128:(e + 1) * 128], w8sb[0:8, 0:512], True, True)
                MM(ph(7), onehot[0:8, e * 128:(e + 1) * 128], w8sb[0:8, 512:1024], True, True)
                mv = TINC()
                P.wait("v", S["mm"], mv)
                V("tensor_copy", weB[:, 0:512], ph(6))
                REL("v", 6)
                V("tensor_copy", weB[:, 512:1024], ph(7))
                REL("v", 7)
                hmul_ev = {0: 0, 1: 0}
                xev = {}
                for j in range(11):
                    sv, wv = SLAB(upw[e, :, j * 256:(j + 1) * 256], 16, 256)
                    P.wait("t", wv[0], wv[1])
                    pp = j % 2
                    for half, pr in ((0, 0), (1, 1)):
                        pe_acq(pr * 2)
                        pe_acq(pr * 2 + 1)
                        for th in range(2):
                            for fc in range(NFC):
                                MM(PP[pr][:, th * 512:th * 512 + 512],
                                   sv[:, fc * 256 + half * 128: fc * 256 + half * 128 + 128],
                                   XN[:, fc * 1024 + th * 512: fc * 1024 + th * 512 + 512],
                                   fc == 0, fc == NFC - 1)
                        TINC()
                    mvg = S["mm"].n - 1
                    mvu = S["mm"].n
                    P.wait("s", S["mm"], mvg)
                    if hmul_ev[pp] > 0:
                        P.wait("s", S["ev"], hmul_ev[pp])
                    SACT(WS[:, 22528 + pp * 1024:22528 + pp * 1024 + 512],
                         PP[0][:, 0:512], AF.Silu)
                    REL("s", 0)
                    SACT(WS[:, 22528 + pp * 1024 + 512:22528 + (pp + 1) * 1024],
                         PP[0][:, 512:1024], AF.Silu)
                    REL("s", 1)
                    gse_pb = pb[1].n
                    P.wait("v", S["mm"], mvu)
                    VD("tensor_mul", WS[:, j * 1024:j * 1024 + 512],
                       PP[1][:, 0:512], weB[:, 0:512])
                    REL("v", 2)
                    V("tensor_mul", WS[:, j * 1024 + 512:(j + 1) * 1024],
                      PP[1][:, 512:1024], weB[:, 512:1024])
                    REL("v", 3)
                    P.wait("v", pb[1], gse_pb)
                    VD("tensor_mul", WS[:, j * 1024:(j + 1) * 1024],
                       WS[:, j * 1024:(j + 1) * 1024],
                       WS[:, 22528 + pp * 1024:22528 + (pp + 1) * 1024])
                    hmul_ev[pp] = VINC()
                    SLAB_DONE()
                he_ev = VINC()
                xn_free_mm = S["mm"].n
                for cs in range(8):
                    sv, wv = SLAB(dnw[e, :, cs * 256:(cs + 1) * 256], 11, 256)
                    P.wait("t", wv[0], wv[1])
                    P.wait("t", S["ev"], he_ev)
                    for oc2 in range(2):
                        oc = cs * 2 + oc2
                        for th in range(2):
                            hp = (oc2 * 2 + th) % 4
                            pe_acq(hp)
                            for l in range(11):
                                MM(ph(hp),
                                   sv[:, l * 256 + oc2 * 128: l * 256 + oc2 * 128 + 128],
                                   WS[:, l * 1024 + th * 512: l * 1024 + th * 512 + 512],
                                   l == 0, l == 10)
                            mv = TINC()
                            P.wait("v", S["mm"], mv)
                            xsl = oc * 1024 + th * 512
                            V("tensor_add", X_[:, xsl:xsl + 512],
                              X_[:, xsl:xsl + 512], ph(hp))
                            REL("v", hp)
                            if e == 7 and th == 1:
                                xev[oc] = VINC()
                    SLAB_DONE()
                    if e == 7 and out_cb is not None and cs in (3, 7):
                        out_cb(cs // 4, xev[cs * 2 + 1])
            return (VINC(), [xev[oc] for oc in range(16)], xn_free_mm,
                    S["mm"].n)

        ffn1_ev, xev_f1, f1_xn_mm, f1_all_mm = emit_ffn(
            ln2_ev, topk_ev, ln_amk=ln2_amk)

        ln3_ev, _, ln3_amk = emit_ln(with_gate=False, xfc_ev=xev_f1,
                                     xn_guard_mm=f1_xn_mm, sq_guard_mm=f1_all_mm)
        lat_st = []
        slab_st = {}
        for sl in range(4):
            src = wkd if sl < 2 else wqd
            c0 = (sl % 2) * 256
            sv, wv = SLAB(src[:, c0:c0 + 256], 16, 256)
            pbv = WS[:, (sl % 2) * 2048:(sl % 2) * 2048 + 2048]
            if sl >= 2:
                P.wait("v", S["st"], slab_st[sl - 2])
            P.wait("t", wv[0], wv[1])
            if sl > 0:
                P.wait("t", S["ev"], ln3_ev)
            for half in range(2):
                for tc2 in range(2):
                    hp = (half * 2 + tc2) % 4
                    pe_acq(hp)
                    for fc in range(NFC):
                        if sl == 0 and half == 0 and tc2 == 0:
                            P.wait("t", S["ev"], ln3_amk[fc])
                        MM(ph(hp),
                           sv[:, fc * 256 + half * 128: fc * 256 + half * 128 + 128],
                           XN[:, fc * 1024 + tc2 * 512: fc * 1024 + tc2 * 512 + 512],
                           fc == 0, fc == NFC - 1)
                    mv = TINC()
                    P.wait("v", S["mm"], mv)
                    V("tensor_copy",
                      pbv[:, half * 1024 + tc2 * 512: half * 1024 + tc2 * 512 + 512],
                      ph(hp))
                    REL("v", hp)
            pev = VINC()
            SLAB_DONE()
            P.wait("y", S["ev"], pev)
            dst_t = cc_bkv_in if sl < 2 else cc_bq_in
            stv = None
            for half in range(2):
                r0 = c0 + half * 128
                stv = STD(dst_t[r0:r0 + 128, :],
                          pbv[:, half * 1024:half * 1024 + 1024],
                          chain=lat_st[-1] if lat_st else 0)
                lat_st.append(stv)
            slab_st[sl] = stv
            if sl == 2:
                P.add("g", lambda g, v=slab_st[1]: g.wait_ge(S["st"].h, v))
                cc3 = COLL("AllGather", cc_bkv_in, cc_bkv_out)
        P.add("g", lambda g: g.wait_ge(S["st"].h, lat_st[-1]))
        cc3q = COLL("AllGather", cc_bq_in, cc_bq_out)
        lat_mm_end = S["mm"].n

        P.wait("y", S["cc"], cc3)
        P.wait("y", S["mm"], lat_mm_end)
        P.wait("v", S["st"], lat_st[-1])
        bostore = []
        LATB = [XN[:, 0:8192], WS[:, 16384:24576]]
        grp_mm = {}
        grp_ev = {}
        grp_ld = {}
        GROUPS = []
        for u in range(4):
            if u // 2 == 0:
                GROUPS += [(u, side, lh) for side in ("kv", "q") for lh in range(2)]
            else:
                GROUPS += [(u, side, lh) for lh in range(2) for side in ("kv", "q")]

        def emit_b_loads(gi):
            u, side, lh = GROUPS[gi]
            beta = u % 2
            latb = LATB[gi % 2]
            if gi >= 2:
                P.wait("y", S["mm"], grp_mm[gi - 2])
            src_t = cc_bkv_out if side == "kv" else cc_bq_out
            if side == "q" and gi == 2:
                P.wait("y", S["cc"], cc3q)
            lvl = None
            for l in range(4):
                for j in range(2):
                    r = (beta * 4 + lh * 2 + j) * 512 + l * 128
                    lvl = SYD(latb[:, l * 2048 + j * 1024: l * 2048 + j * 1024 + 1024],
                              src_t[r:r + 128, :], sem=S[f"lx{gi % 2}"])
            grp_ld[gi] = lvl

        emit_b_loads(0)
        emit_b_loads(1)
        for u in range(4):
            beta, slot = u % 2, u // 2
            for k in range(4):
                gi = u * 4 + k
                side, lh = GROUPS[gi][1], GROUPS[gi][2]
                latb = LATB[gi % 2]
                if slot == 1 and side == "kv":
                    if gi >= 1:
                        P.wait("g", S["ev"], grp_ev[gi - 1])
                    lvp = GPD(WS[:, 12288:14336], pa_in[:, lh * 2048:(lh + 1) * 2048])
                    lvp = GPD(WS[:, 14336:16384], pb_in[:, lh * 2048:(lh + 1) * 2048])
                else:
                    lvp = 0
                P.wait("t", S[f"lx{gi % 2}"], grp_ld[gi])
                if lvp:
                    P.wait("t", S["gp"], lvp)
                wof = {"q": 0, "kv": 1024}[side]
                dst = 8192 if side == "q" else 0
                beffc = MC_BEFFQ if side == "q" else MC_BEFFK
                bia = misc[:, beffc:beffc + 1]
                bia1 = misc[:, beffc + 1:beffc + 2]
                ncomp = 1 if slot == 0 else 2
                for ci in range(ncomp):
                    outv = (WS[:, dst + lh * 2048:dst + (lh + 1) * 2048] if slot == 0
                            else XN[:, 8192 + ci * 2048:8192 + (ci + 1) * 2048])
                    for tt in range(4):
                        hp = tt % 2
                        pe_acq(hp)
                        for l in range(4):
                            MM(ph(hp),
                               upslc[:, wof + l * 256 + ci * 128: wof + l * 256 + ci * 128 + 128],
                               latb[:, l * 2048 + tt * 512: l * 2048 + tt * 512 + 512],
                               l == 0, l == 3)
                        mv = TINC()
                        P.wait("v", S["mm"], mv)
                        bb = bia if ci == 0 else bia1
                        sca = SCALE_B if side == "q" else 1.0
                        V("tensor_scalar", outv[:, tt * 512:(tt + 1) * 512],
                          ph(hp), sca, bb, ALU.mult, ALU.add)
                        REL("v", hp)
                if slot == 1:
                    VD("tensor_mul", WS[:, dst + lh * 2048:dst + (lh + 1) * 2048],
                       XN[:, 8192:10240], WS[:, 12288:14336])
                    VD("tensor_mul", XN[:, 8192:10240],
                       XN[:, 10240:12288], WS[:, 14336:16384])
                    VD("tensor_add", WS[:, dst + lh * 2048:dst + (lh + 1) * 2048],
                       WS[:, dst + lh * 2048:dst + (lh + 1) * 2048],
                       XN[:, 8192:10240])
                if side == "kv":
                    for tq in range(4):
                        hp = 2 + (tq % 2)
                        pe_acq(hp)
                        for k in range(4):
                            tt = tq * 4 + k
                            for l in range(4):
                                MM(ph(hp)[:, k * 128:(k + 1) * 128],
                                   latb[:, l * 2048 + tt * 128: l * 2048 + tt * 128 + 128],
                                   upslc[:, 2048 + l * 256 + slot * 128:
                                         2048 + l * 256 + slot * 128 + 128],
                                   l == 0, l == 3)
                        mv = TINC()
                        P.wait("v", S["mm"], mv)
                        kc0 = lh * 16 + tq * 4
                        V("tensor_copy",
                          WS[:, 4096 + kc0 * 128:4096 + kc0 * 128 + 512],
                          ph(hp)[:, 0:512])
                        REL("v", hp)
                grp_mm[gi] = S["mm"].n
                grp_ev[gi] = VINC()
                if gi + 2 < 16:
                    emit_b_loads(gi + 2)
            bqk_ev = VINC()
            last_bqk = bqk_ev
            P.wait("t", S["ev"], bqk_ev)
            if u > 0:
                P.wait("v", S["st"], bostore[-1])
            obias = misc[:, MC_BEFFV + slot:MC_BEFFV + slot + 1]
            cast_last = emit_attn_b_unit(
                WS[:, 8192:12288], WS[:, 0:4096],
                lambda kc: WS[:, 4096 + kc * 128:4096 + (kc + 1) * 128],
                XN[:, 12288:16384], obias, None, None)
            P.wait("y", S["ev"], cast_last)
            for d2 in range(4):
                dd = beta * 4 + d2
                stv = STD(cc_bo_in[slot][dd * 128:(dd + 1) * 128, :],
                          XN[:, 12288 + d2 * 1024:12288 + (d2 + 1) * 1024])
                bostore.append(stv)
            if u == 1:
                P.add("g", lambda g, v=bostore[-1]: g.wait_ge(S["st"].h, v))
                cc4a = COLL("AllToAll", cc_bo_in[0], cc_bo_out[0])
        P.add("g", lambda g: g.wait_ge(S["st"].h, bostore[-1]))
        cc4b = COLL("AllToAll", cc_bo_in[1], cc_bo_out[1])

        P.wait("y", S["cc"], cc4a)
        olv = None
        oevb = {}
        P.wait("y", S["mm"], grp_mm[14])
        P.wait("y", S["ev"], last_bqk)
        for fc in range(NFC):
            if fc == 8:
                P.wait("y", S["ld"], olv)
                P.wait("v", S["ld"], olv)
                oevb[0] = VINC()
                P.wait("y", S["cc"], cc4b)
            olv = SYD(XN[:, fc * 1024:(fc + 1) * 1024],
                      cc_bo_out[fc // 8][(fc % 8) * 128:(fc % 8 + 1) * 128, :])
        P.wait("v", S["ld"], olv)
        oevb[1] = VINC()
        xev_pb, pb_mm = emit_proj_residual2(wo, XN, MC_BO, oevb[0], oevb[1])

        ln4_ev, gate_pb4, ln4_amk = emit_ln(with_gate=True, xfc_ev=xev_pb,
                                            xn_guard_mm=pb_mm)
        topk4_ev = emit_topk(gate_pb4)

        out_st = []

        def emit_out_half(hf, gate_ev):
            for tc in range(NTC):
                stg = xstgA if len(out_st) % 2 == 0 else xstgB
                if len(out_st) >= 2:
                    P.wait("v", S["st"], out_st[-2])
                for fh in range(2):
                    hp = 4 + fh
                    pe_acq(hp)
                    if tc == 0 and fh == 0:
                        P.wait("t", S["ev"], gate_ev)
                    for k in range(4):
                        fc = hf * 8 + fh * 4 + k
                        TTR(ph(hp)[:, k * 128:(k + 1) * 128],
                            X_[:, fc * 1024 + tc * 128: fc * 1024 + tc * 128 + 128])
                    mv = TINC()
                    P.wait("v", S["mm"], mv)
                    V("tensor_copy", stg[:, fh * 512:(fh + 1) * 512], ph(hp)[:, 0:512])
                    REL("v", hp)
                sev = VINC()
                P.wait("y", S["ev"], sev)
                stv = STD(out_t[tc * 128:(tc + 1) * 128, hf * 1024:(hf + 1) * 1024],
                          stg, chain=out_st[-1] if out_st else 0)
                out_st.append(stv)

        ffn2_ev, _, _, _ = emit_ffn(ln4_ev, topk4_ev, out_cb=emit_out_half,
                                    ln_amk=ln4_amk)
        P.add("y", lambda y: y.wait_ge(S["st"].h, out_st[-1]))

        block.tensor(lambda e: [f(e) for f in P.q["t"]])
        block.vector(lambda e: [f(e) for f in P.q["v"]])
        block.scalar(lambda e: [f(e) for f in P.q["s"]])
        block.gpsimd(lambda e: [f(e) for f in P.q["g"]])
        block.sync(lambda e: [f(e) for f in P.q["y"]])

    return nc


def _prep_inputs(ins):
    f32 = np.float32
    xfull = np.ascontiguousarray(ins["L0_attn_a_hidden_states"], dtype=f32)

    perm = []
    for d in range(8):
        for part in range(3):
            for hh in (d, d + 8):
                base = part * H + hh * 128
                perm.extend(range(base, base + 128))
    perm = np.array(perm)
    wqkv_p = np.ascontiguousarray(ins["L0_attn_a_w_qkv"][:, perm], dtype=f32)
    bqkv_p = np.ascontiguousarray(ins["L0_attn_a_b_qkv"][perm], dtype=f32)

    operm = np.arange(2048)
    wproj_p = np.ascontiguousarray(ins["L0_attn_a_w_proj"][operm, :], f32)
    wo_p = np.ascontiguousarray(ins["L0_attn_b_Wo"].T[operm, :], f32)

    wqd_t = np.ascontiguousarray(ins["L0_attn_b_Wq_down"].T, f32)
    wkd_t = np.ascontiguousarray(ins["L0_attn_b_Wkv_down"].T, f32)

    sperm = []
    for j in range(22):
        sperm.extend(range(j * 128, (j + 1) * 128))
        sperm.extend(range(2816 + j * 128, 2816 + (j + 1) * 128))
    sperm = np.array(sperm)
    sgw_ti = np.ascontiguousarray(ins["L0_ffn_shared_gw"].T[:, sperm], f32)
    sdw_t = np.ascontiguousarray(ins["L0_ffn_shared_dw"].T, f32)
    uperm = []
    for j in range(11):
        uperm.extend(range(j * 128, (j + 1) * 128))
        uperm.extend(range(1408 + j * 128, 1408 + (j + 1) * 128))
    uperm = np.array(uperm)
    upw_ti = np.ascontiguousarray(ins["L0_ffn_up_w"].transpose(0, 2, 1)[:, :, uperm], f32)
    dnw_t = np.ascontiguousarray(ins["L0_ffn_down_w"].transpose(0, 2, 1), f32)
    gwt_t = np.ascontiguousarray(ins["L0_ffn_gate_w"].T, f32)

    ident = np.eye(128, dtype=f32)
    ones_row = np.ones((1, 128), f32)
    ones_colb = np.ones((128, 1), bfnp)
    onehot8 = np.zeros((8, 1024), bfnp)
    for e in range(8):
        onehot8[e, e * 128:(e + 1) * 128] = 1
    qpos_bf = np.arange(SEQ, dtype=f32)[None, :].astype(bfnp)
    slopes = _alibi_slopes(NH)
    cos64, sin64 = _rope_cos_sin(SEQ, H // 4)

    dm = np.zeros((128, 2048), np.float64)
    for dvar in range(4):
        pcol = dvar * 128 + np.arange(128)[:, None]
        fcol = np.arange(512)[None, :]
        dm[:, dvar * 512:(dvar + 1) * 512] = np.where(pcol > fcol, -30000.0, 0.0)
    diag_masks = dm.astype(bfnp)

    gate_bias_p = np.asarray(ins["L0_ffn_gate_bias"], f32).reshape(8, 1)

    base = dict(
        wqkv_p=wqkv_p, wproj_p=wproj_p, wo_p=wo_p, wqd_t=wqd_t, wkd_t=wkd_t,
        shared_gw_ti=sgw_ti, shared_dw_t=sdw_t, up_w_ti=upw_ti, down_w_t=dnw_t,
        gate_w_t=gwt_t, gate_bias_p=gate_bias_p,
        ident_in=ident, ones_row=ones_row, ones_colb=ones_colb, onehot8=onehot8,
        qpos_bf=qpos_bf, diag_masks=diag_masks,
    )

    in_maps = []
    for c in range(8):
        m = dict(base)
        beta, r = c // 4, c % 4
        m["x_slice"] = np.ascontiguousarray(xfull[beta, r * TOK:(r + 1) * TOK, :], f32)
        hA, hB = c, c + 8
        mm = c % 4
        bm = np.zeros((128, 80), f32)
        bm[:, MCB("BPROJ"):MCB("BPROJ") + 16] = np.asarray(
            ins["L0_attn_a_b_proj"], f32).reshape(16, 128).T
        bm[:, MCB("BO"):MCB("BO") + 16] = np.asarray(
            ins["L0_attn_b_bo"], f32).reshape(16, 128).T
        bq = np.asarray(ins["L0_attn_a_b_qkv"], f32)
        bm[:, MCB("BVA") + 0] = bq[2 * H + hA * 128: 2 * H + (hA + 1) * 128]
        bm[:, MCB("BVA") + 1] = bq[2 * H + hB * 128: 2 * H + (hB + 1) * 128]
        for d in range(8):
            for half in range(2):
                hh = d + 8 * half
                bm[:, 48 + d * 2 + half] = bq[hh * 128:(hh + 1) * 128]
                bm[:, 64 + d * 2 + half] = bq[2048 + hh * 128: 2048 + (hh + 1) * 128]
        bm[:, MCB("BQD"):MCB("BQD") + 4] = np.asarray(
            ins["L0_attn_b_bq_down"], f32).reshape(4, 128).T
        bm[:, MCB("BKD"):MCB("BKD") + 4] = np.asarray(
            ins["L0_attn_b_bkv_down"], f32).reshape(4, 128).T
        blk_c = mm if c < 4 else mm + 4
        blk_o = mm + 4 if c < 4 else mm
        qcols = list(range(blk_c * 128, (blk_c + 1) * 128)) + \
            list(range(blk_o * 128, (blk_o + 1) * 128))
        m["wqu_s"] = np.ascontiguousarray(ins["L0_attn_b_Wq_up"].T[:, qcols], f32)
        m["wku_s"] = np.ascontiguousarray(ins["L0_attn_b_Wk_up"].T[:, qcols], f32)
        vcols = list(range(hA * 128, (hA + 1) * 128)) + \
            list(range(hB * 128, (hB + 1) * 128))
        m["wvu_s"] = np.ascontiguousarray(ins["L0_attn_b_Wv_up"].T[:, vcols], f32)
        bqu = np.asarray(ins["L0_attn_b_bq_up"], f32)
        bku = np.asarray(ins["L0_attn_b_bk_up"], f32)
        bvu = np.asarray(ins["L0_attn_b_bv_up"], f32)
        bm[:, MCB("BQU") + 0] = bqu[blk_c * 128:(blk_c + 1) * 128]
        bm[:, MCB("BQU") + 1] = bqu[blk_o * 128:(blk_o + 1) * 128]
        bm[:, MCB("BKU") + 0] = bku[blk_c * 128:(blk_c + 1) * 128]
        bm[:, MCB("BKU") + 1] = bku[blk_o * 128:(blk_o + 1) * 128]
        bm[:, MCB("BVU") + 0] = bvu[hA * 128:(hA + 1) * 128]
        bm[:, MCB("BVU") + 1] = bvu[hB * 128:(hB + 1) * 128]
        m["bias_misc"] = bm
        sl2 = np.zeros((1, 256), np.float64)
        sl2[0, 0:128] = slopes[hA]
        sl2[0, 128:256] = slopes[hB]
        m["slope_row2"] = sl2.astype(f32)
        kpc = np.zeros((128, 64), np.float64)
        for kc in range(32):
            pos = kc * 128 + np.arange(128)
            kpc[:, kc * 2 + 0] = slopes[hA] * pos
            kpc[:, kc * 2 + 1] = slopes[hB] * pos
        m["kposc"] = kpc.astype(f32)
        pa = cos64[:, mm * 128:(mm + 1) * 128].T
        sgn = -1.0 if c < 4 else 1.0
        pbp = sgn * sin64[:, mm * 128:(mm + 1) * 128].T
        m["ropePA"] = np.ascontiguousarray(pa, f32)
        m["ropePB"] = np.ascontiguousarray(pbp, f32)
        in_maps.append(m)
    return in_maps


def MCB(name):
    return {"BPROJ": 0, "BO": 16, "BVA": 32, "BQD": 34, "BKD": 38,
            "BQU": 42, "BKU": 44, "BVU": 46}[name]


_PROG_CACHE = {}


def kernel(**inputs):
    in_maps = _prep_inputs(inputs)
    if "nc" not in _PROG_CACHE:
        _PROG_CACHE["nc"] = build_program()
    nc = _PROG_CACHE["nc"]
    with allow_ici_all_to_all():
        res = run_bass_kernel_spmd(nc, in_maps, core_ids=list(range(8)))
    out = np.zeros((B, SEQ, H), np.float32)
    for c in range(8):
        beta, r = c // 4, c % 4
        out[beta, r * TOK:(r + 1) * TOK, :] = np.asarray(res.results[c]["out_slice"])
    return out


# version 14 (final)
# speedup vs baseline: 1.0141x; 1.0041x over previous
import math
import numpy as np
import ml_dtypes
from contextlib import ExitStack

import concourse.bass as bass
import concourse.mybir as mybir
from concourse.bass_utils import run_bass_kernel_spmd
from concourse.aot_env import allow_ici_all_to_all

F32 = mybir.dt.float32
BF16 = mybir.dt.bfloat16
AF = mybir.ActivationFunctionType
ALU = mybir.AluOpType
AXX = mybir.AxisListType.X

B = 2
SEQ = 4096
H = 2048
NH = 16
HD = 128
TOK = 1024
NTC = 8
NFC = 16
LAT = 512
EPS = 1e-5
SCALE_A = 1.0 / HD
SCALE_B = 1.0 / math.sqrt(HD)
bfnp = ml_dtypes.bfloat16


def _alibi_slopes(n):
    start = 2.0 ** (-2.0 ** (-(math.log2(n) - 3)))
    return np.array([start * start ** i for i in range(n)], np.float64)


def _rope_cos_sin(seq_len, dim, base=10000.0):
    inv_freq = 1.0 / base ** (np.arange(0, dim, 2, dtype=np.float64) / dim)
    t = np.arange(seq_len, dtype=np.float64)
    freqs = np.outer(t, inv_freq)
    emb = np.concatenate([freqs, freqs], -1)
    return np.cos(emb), np.sin(emb)


class Sem:
    __slots__ = ("h", "n")

    def __init__(self, handle):
        self.h = handle
        self.n = 0


class Plan:
    def __init__(self):
        self.q = {e: [] for e in ("t", "v", "s", "g", "y")}

    def add(self, eng, fn, real=False):
        fn.real = real
        self.q[eng].append(fn)

    def wait(self, eng, sem, val):
        if val > 0:
            w = lambda e, _h=sem.h, _v=val: e.wait_ge(_h, _v)
            w.real = False
            self.q[eng].append(w)

    def inc_last(self, eng, sem, k=1):
        if self.q[eng] and getattr(self.q[eng][-1], "real", False):
            prev = self.q[eng].pop()
            def wrapped(e, _p=prev, _h=sem.h, _k=k):
                inst = _p(e)
                inst.then_inc(_h, _k)
            wrapped.real = False
            self.q[eng].append(wrapped)
        else:
            f = lambda e, _h=sem.h, _k=k: e.drain().then_inc(_h, _k)
            f.real = False
            self.q[eng].append(f)
        sem.n += k
        return sem.n


def build_program(upto=99):
    nc = bass.Bass("TRN2", target_bir_lowering=False, debug=False)

    def din(name, shape, dt=F32):
        return nc.dram_tensor(name, shape, dt, kind="ExternalInput").ap()

    x_in = din("x_slice", [TOK, H])
    wqkv = din("wqkv_p", [H, 3 * H])
    wproj = din("wproj_p", [H, H])
    wo = din("wo_p", [H, H])
    wqd = din("wqd_t", [H, LAT])
    wkd = din("wkd_t", [H, LAT])
    sgw = din("shared_gw_ti", [H, 5632])
    sdw = din("shared_dw_t", [2816, H])
    upw = din("up_w_ti", [8, H, 2816])
    dnw = din("down_w_t", [8, 1408, H])
    gwt = din("gate_w_t", [H, 8])
    wqu_s = din("wqu_s", [LAT, 256])
    wku_s = din("wku_s", [LAT, 256])
    wvu_s = din("wvu_s", [LAT, 256])
    bias_misc = din("bias_misc", [128, 80])
    gbias_in = din("gate_bias_p", [8, 1])
    ident_in = din("ident_in", [128, 128])
    onesr_in = din("ones_row", [1, 128])
    onesc_in = din("ones_colb", [128, 1], BF16)
    onehot_in = din("onehot8", [8, 1024], BF16)
    slope_in = din("slope_row2", [1, 256])
    kposc_in = din("kposc", [128, 64])
    qpos_in = din("qpos_bf", [1, SEQ], BF16)
    diag_in = din("diag_masks", [128, 2048], BF16)
    pa_in = din("ropePA", [128, SEQ])
    pb_in = din("ropePB", [128, SEQ])

    out_t = nc.dram_tensor("out_slice", [TOK, H], F32, kind="ExternalOutput").ap()

    cc_ak_in = nc.dram_tensor("cc_ak_in", [2048, TOK], BF16)
    cc_ak_out = nc.dram_tensor("cc_ak_out", [2048, TOK], BF16)
    cc_aq_in = [nc.dram_tensor(f"cc_aq_in{i}", [1024, TOK], BF16) for i in range(2)]
    cc_aq_out = [nc.dram_tensor(f"cc_aq_out{i}", [1024, TOK], BF16) for i in range(2)]
    cc_av_in = nc.dram_tensor("cc_av_in", [8 * TOK, 256], BF16)
    cc_av_out = nc.dram_tensor("cc_av_out", [8 * TOK, 256], BF16)
    cc_ao_in = [nc.dram_tensor(f"cc_ao_in{i}", [8 * 128, TOK], BF16) for i in range(2)]
    cc_ao_out = [nc.dram_tensor(f"cc_ao_out{i}", [8 * 128, TOK], BF16) for i in range(2)]
    cc_bq_in = nc.dram_tensor("cc_bq_in", [512, TOK], BF16)
    cc_bq_out = nc.dram_tensor("cc_bq_out", [8 * 512, TOK], BF16,
                               addr_space="Shared")
    cc_bkv_in = nc.dram_tensor("cc_bkv_in", [512, TOK], BF16)
    cc_bkv_out = nc.dram_tensor("cc_bkv_out", [8 * 512, TOK], BF16,
                                addr_space="Shared")
    cc_bo_in = [nc.dram_tensor(f"cc_bo_in{i}", [8 * 128, TOK], BF16) for i in range(2)]
    cc_bo_out = [nc.dram_tensor(f"cc_bo_out{i}", [8 * 128, TOK], BF16) for i in range(2)]

    MC_BPROJ = 0
    MC_BO = 16
    MC_BVA = 32
    MC_QKB = 48
    MC_BQD = 34
    MC_BKD = 38
    MC_BQU = 42
    MC_BKU = 44
    MC_BVU = 46

    with ExitStack() as stack:
        E = stack.enter_context

        af = E(nc.sbuf_tensor("arena_f32", [128, 22016], F32))
        X_ = af[:, 0:16384]
        dscr = af[:, 16384:17408]
        weB = dscr
        mBh = af[:, 17408:17920]
        rBh = af[:, 17920:18432]
        sumac = af[:, 18432:19456]
        recB = af[:, 19456:19968]
        otmp = af[:, 19968:20992]
        sum2 = af[:, 20992:22016]
        xstgA = af[:, 16384:17408]
        xstgB = af[:, 17408:18432]

        ab = E(nc.sbuf_tensor("arena_b16", [128, 51200], BF16))
        XN = ab[:, 0:16384]
        WS = ab[:, 16384:40960]
        WT = ab[:, 40960:49152]
        PT = ab[:, 49152:51200]

        misc = E(nc.sbuf_tensor("misc_f32", [128, 16], F32))
        MC_ONES = 0
        MC_BEFFQ = 2
        MC_BEFFK = 4
        MC_BEFFV = 6
        gate = E(nc.sbuf_tensor("gate_f32", [8, 2064], F32))
        G_S, G_BS, G_GB = 0, 1024, 2048
        srowA = af[0:1, 16384:16896]
        srowB = af[0:1, 16896:17408]
        srowC = af[0:1, 19456:19968]
        topk = E(nc.sbuf_tensor("topk_f32", [128, 64], F32))
        w8sb = E(nc.sbuf_tensor("w8sb_b16", [8, 1024], BF16))
        onehot = E(nc.sbuf_tensor("onehot_b16", [8, 1024], BF16))
        ident = E(nc.sbuf_tensor("ident_f32", [128, 128], F32))
        onesr = E(nc.sbuf_tensor("onesr_f32", [1, 128], F32))
        onescb = E(nc.sbuf_tensor("onescb_b16", [128, 1], BF16))
        onesrb = E(nc.sbuf_tensor("onesrb_b16", [1, 128], BF16))
        slopes = E(nc.sbuf_tensor("slopes_f32", [1, 256], F32))
        upslc = E(nc.sbuf_tensor("upslc_b16", [128, 3072], BF16))
        gwt_s = E(nc.sbuf_tensor("gwt_b16", [128, 128], BF16))
        bmisc = E(nc.sbuf_tensor("bmisc_f32", [128, 80], F32))
        bqdb = E(nc.sbuf_tensor("bqdb_b16", [128, 8], BF16))
        kposc = E(nc.sbuf_tensor("kposc_f32", [128, 64], F32))

        PP = [E(nc.psum_tensor(f"pp{i}", [128, 1024], F32)) for i in range(4)]

        S = {nm: Sem(E(nc.semaphore(nm))) for nm in
             ["ld", "lx0", "lx1", "w0", "w1", "wf", "mm", "ev", "es", "cc", "st",
              "px", "sum", "gp", "pb0", "pb1", "pb2", "pb3", "pb4", "pb5", "pb6",
              "pb7"]}
        pb = [S[f"pb{i}"] for i in range(8)]
        pbu = [0] * 8

        block = E(nc.Block())
        P = Plan()

        def ph(i):
            return PP[i // 2][:, (i % 2) * 512:(i % 2) * 512 + 512]

        def pe_acq(i):
            if pbu[i] > 0:
                P.wait("t", pb[i], pbu[i])
            pbu[i] += 1

        def REL(eng, i):
            P.inc_last(eng, pb[i], 1)

        def MM(out, lhsT, rhs, start, stop):
            P.add("t", lambda t, o=out, l=lhsT, r=rhs, st=start, sp=stop:
                  t.matmul(o, l, r, start=st, stop=sp), real=True)

        def TTR(out, in_, np_=128):
            P.add("t", lambda t, o=out, i=in_, n=np_: t.transpose(o, i, ident[0:n, 0:n]),
                  real=True)

        def TINC(sem=None, k=1):
            return P.inc_last("t", sem or S["mm"], k)

        def VINC(sem=None, k=1):
            return P.inc_last("v", sem or S["ev"], k)

        def SINC(sem=None, k=1):
            return P.inc_last("s", sem or S["es"], k)

        def V(meth, *args):
            def f(v, m=meth, a=args):
                v.drain()
                return getattr(v, m)(*a)
            P.add("v", f, real=True)

        VD = V

        def SDR():
            P.add("s", lambda s: s.drain())

        def SC(meth, *args, **kw):
            P.add("s", lambda s, m=meth, a=args, k=kw: getattr(s, m)(*a, **k))

        def SACT(out, in_, func, bias=0.0, scale=1.0, accum=None):
            if accum is None:
                def f1(s, o=out, i=in_, fn=func, b=bias, sc=scale):
                    s.drain()
                    return s.activation(out=o, in_=i, func=fn, bias=b, scale=sc)
                P.add("s", f1, real=True)
            else:
                def f2(s, o=out, i=in_, fn=func, b=bias, sc=scale, ac=accum):
                    s.drain()
                    return s.activation(out=o, in_=i, func=fn, bias=b, scale=sc,
                                        accum_out=ac)
                P.add("s", f2, real=True)

        def SACT_ND(out, in_, func, bias=0.0, scale=1.0):
            P.add("s", lambda s, o=out, i=in_, fn=func, b=bias, sc=scale:
                  s.activation(out=o, in_=i, func=fn, bias=b, scale=sc), real=True)

        def SYD(out, in_, transpose=False, sem=None):
            sm = sem or S["ld"]
            P.add("y", lambda y, o=out, i=in_, tr=transpose, _h=sm.h:
                  y.dma_start(out=o, in_=i, transpose=tr).then_inc(_h, 16))
            sm.n += 16
            return sm.n

        def STD(out, in_, chain=0):
            def _f(y, o=out, i=in_, ch=chain):
                if ch > 0:
                    y.wait_ge(S["st"].h, ch)
                y.dma_start(out=o, in_=i).then_inc(S["st"].h, 16)
            P.add("y", _f)
            S["st"].n += 16
            return S["st"].n

        def GPD(out, in_, sem=None):
            sm = sem or S["gp"]
            P.add("g", lambda g, o=out, i=in_, _h=sm.h:
                  g.dma_start(out=o, in_=i).then_inc(_h, 16))
            sm.n += 16
            return sm.n

        def COLL(kind, ins, outs):
            def _do(g, k=kind, i=ins, o=outs):
                with allow_ici_all_to_all():
                    g.collective_compute(k, ALU.bypass,
                                         replica_groups=[list(range(8))],
                                         ins=[i.ap().opt()], outs=[o.ap().opt()]
                                         ).then_inc(S["cc"].h, 1)
            P.add("g", _do)
            S["cc"].n += 1
            return S["cc"].n

        wstate = {"idx": 0}
        slab_free_mm = {}

        def SLAB(ap2d, nf, osz):
            si = wstate["idx"]
            wstate["idx"] += 1
            par = si % 2
            wsem = S[f"w{par}"]
            sv = WT[:, par * 4096:par * 4096 + nf * osz]
            prev = si - 2
            def _dma(g, dr=ap2d, s2=sv, oz=osz, pv=prev, wh=wsem.h):
                if pv >= 0:
                    g.wait_ge(S["mm"].h, slab_free_mm[pv])
                g.dma_start(out=s2.rearrange("p (c o) -> p c o", o=oz),
                            in_=dr.rearrange("(c p) o -> p c o", p=128)
                            ).then_inc(wh, 16)
            P.add("g", _dma)
            wsem.n += 16
            return sv, (wsem, wsem.n)

        def SLAB_DONE():
            slab_free_mm[len(slab_free_mm)] = S["mm"].n

        GPD(ident[:, :], ident_in)
        GPD(onesr[:, :], onesr_in)
        GPD(onescb[:, :], onesc_in)
        GPD(slopes[:, :], slope_in)
        GPD(onehot[:, :], onehot_in)
        GPD(bmisc[:, :], bias_misc)
        GPD(gate[0:8, G_GB:G_GB + 1], gbias_in)
        GPD(WS[0:1, 8192:12288], qpos_in)
        GPD(WS[:, 12288:14336], diag_in)
        GPD(gwt_s[:, :].rearrange("p (c o) -> p c o", o=8),
            gwt.rearrange("(c p) o -> p c o", p=128))
        GPD(upslc[:, 0:1024].rearrange("p (c o) -> p c o", o=256),
            wqu_s.rearrange("(c p) o -> p c o", p=128))
        GPD(upslc[:, 1024:2048].rearrange("p (c o) -> p c o", o=256),
            wku_s.rearrange("(c p) o -> p c o", p=128))
        GPD(upslc[:, 2048:3072].rearrange("p (c o) -> p c o", o=256),
            wvu_s.rearrange("(c p) o -> p c o", p=128))
        GPD(bqdb[:, 0:4], bias_misc[:, MC_BQD:MC_BQD + 4])
        GPD(bqdb[:, 4:8], bias_misc[:, MC_BKD:MC_BKD + 4])
        CONST = GPD(kposc[:, :], kposc_in)

        P.wait("v", S["gp"], CONST)
        V("memset", misc[:, MC_ONES:MC_ONES + 1], 1.0)
        V("memset", onesrb[:, :], 1.0)
        V("tensor_scalar", bmisc[:, MC_QKB:MC_QKB + 16],
          bmisc[:, MC_QKB:MC_QKB + 16], SCALE_A, None, ALU.mult)

        P.wait("t", S["gp"], CONST)
        xmv = {}
        xfc1 = []
        for i in range(NTC * 2):
            hf, tc = i // 8, i % 8
            stg = xstgA if i % 2 == 0 else xstgB
            if i >= 2:
                P.wait("y", S["mm"], xmv[i - 2])
            lv = SYD(stg, x_in[tc * 128:(tc + 1) * 128, hf * 1024:(hf + 1) * 1024],
                     sem=S[f"lx{i % 2}"])
            P.wait("t", S[f"lx{i % 2}"], lv)
            for fh in range(2):
                hp = 4 + fh
                pe_acq(hp)
                for k in range(4):
                    f8 = fh * 4 + k
                    TTR(ph(hp)[:, k * 128:(k + 1) * 128],
                        stg[:, f8 * 128:f8 * 128 + 128])
                mv = TINC()
                P.wait("v", S["mm"], mv)
                V("tensor_copy",
                  X_[:, hf * 8192:(hf + 1) * 8192].rearrange(
                      "p (f o) -> p f o", o=1024)[:, fh * 4:fh * 4 + 4,
                                                  tc * 128:(tc + 1) * 128],
                  ph(hp)[:, 0:512].rearrange("p (f o) -> p f o", o=128))
                REL("v", hp)
            xmv[i] = mv
            if tc == 7:
                hev = VINC()
                xfc1 += [hev] * 8
        X_EV = VINC()

        for woff, bcol, ocol, scl in ((0, MC_BQU, MC_BEFFQ, SCALE_B),
                                      (1024, MC_BKU, MC_BEFFK, 1.0),
                                      (2048, MC_BVU, MC_BEFFV, 1.0)):
            bsrc = 0 if woff == 0 else (4 if woff == 1024 else 4)
            for blk in range(2):
                pe_acq(6)
                P.wait("t", S["gp"], CONST)
                for l in range(4):
                    MM(ph(6)[:, 0:1],
                       upslc[:, woff + l * 256 + blk * 128: woff + l * 256 + blk * 128 + 128],
                       bqdb[:, bsrc + l: bsrc + l + 1], l == 0, l == 3)
                mv = TINC()
                P.wait("v", S["mm"], mv)
                V("scalar_tensor_tensor", misc[:, ocol + blk:ocol + blk + 1],
                  ph(6)[:, 0:1], 1.0, bmisc[:, bcol + blk:bcol + blk + 1],
                  ALU.mult, ALU.add)
                if scl != 1.0:
                    VD("tensor_scalar", misc[:, ocol + blk:ocol + blk + 1],
                       misc[:, ocol + blk:ocol + blk + 1], scl, None, ALU.mult)
                REL("v", 6)
        BEFF_EV = VINC()

        def emit_ln(with_gate, xfc_ev=None, xn_guard_mm=0, sq_guard_mm=0):
            entry_ev = S["ev"].n
            if sq_guard_mm:
                P.wait("s", S["mm"], sq_guard_mm)
            for fc in range(NFC):
                P.wait("s", S["ev"], xfc_ev[fc] if xfc_ev else entry_ev)
                SACT_ND(WS[:, fc * 512:(fc + 1) * 512],
                        X_[:, fc * 1024:fc * 1024 + 512], AF.Square)
                SACT_ND(WS[:, 14336 + fc * 512:14336 + (fc + 1) * 512],
                        X_[:, fc * 1024 + 512:(fc + 1) * 1024], AF.Square)
            cmk = {}
            if xn_guard_mm:
                P.wait("s", S["mm"], xn_guard_mm)
            for fc in range(NFC):
                SACT_ND(XN[:, fc * 1024:(fc + 1) * 1024],
                        X_[:, fc * 1024:(fc + 1) * 1024], AF.Copy)
                cmk[fc] = SINC()
            amk = {}
            for th in range(2):
                sqb = 0 if th == 0 else 14336
                pe_acq(6)
                pe_acq(7)
                for fc in range(NFC):
                    P.wait("t", S["es"], cmk[fc])
                    MM(PP[3][0:1, 0:512], onescb[:, :],
                       XN[:, fc * 1024 + th * 512: fc * 1024 + th * 512 + 512],
                       fc == 0, fc == NFC - 1)
                    MM(PP[3][0:1, 512:1024], onescb[:, :],
                       WS[:, sqb + fc * 512:sqb + (fc + 1) * 512],
                       fc == 0, fc == NFC - 1)
                mv = TINC()
                P.wait("v", S["mm"], mv)
                V("tensor_scalar", srowA, PP[3][0:1, 0:512], 1.0 / H, None, ALU.mult)
                REL("v", 6)
                VD("tensor_mul", srowC, srowA, srowA)
                VD("scalar_tensor_tensor", srowB, PP[3][0:1, 512:1024],
                   1.0 / H, srowC, ALU.mult, ALU.subtract)
                REL("v", 7)
                VD("tensor_scalar", srowB, srowB, EPS, None, ALU.add)
                var_ev = VINC()
                P.wait("s", S["ev"], var_ev)
                SACT(srowB, srowB, AF.Ln)
                SDR()
                SACT(srowB, srowB, AF.Exp, scale=-0.5)
                rstd_ev = SINC()
                pe_acq(6)
                pe_acq(7)
                P.wait("t", S["es"], rstd_ev)
                MM(ph(6), onesr[:, :], srowA, True, True)
                MM(ph(7), onesr[:, :], srowB, True, True)
                mv = TINC()
                P.wait("v", S["mm"], mv)
                V("tensor_copy", mBh[:, :], ph(6))
                REL("v", 6)
                V("tensor_copy", rBh[:, :], ph(7))
                REL("v", 7)
                scrf = af[:, 18432:21504]
                def _sub(fc, th=th):
                    sl0 = fc * 1024 + th * 512
                    (VD if fc == 0 else V)("tensor_sub",
                        scrf[:, (fc % 6) * 512:(fc % 6) * 512 + 512],
                        X_[:, sl0:sl0 + 512], mBh[:, :])
                def _mul(fc, th=th):
                    sl0 = fc * 1024 + th * 512
                    (VD if fc == 0 else V)("tensor_mul", XN[:, sl0:sl0 + 512],
                        scrf[:, (fc % 6) * 512:(fc % 6) * 512 + 512], rBh[:, :])
                for fc in range(NFC):
                    _sub(fc)
                    if fc >= 3:
                        _mul(fc - 3)
                        if th == 1:
                            amk[fc - 3] = VINC()
                for fc in range(NFC - 3, NFC):
                    _mul(fc)
                    if th == 1:
                        amk[fc] = VINC()
            apply_ev = VINC()
            amkl = [amk[fc] for fc in range(NFC)]
            if with_gate:
                pe_acq(6)
                pe_acq(7)
                for fc in range(NFC):
                    P.wait("t", S["ev"], amk[fc])
                    for th in range(2):
                        MM(PP[3][0:8, th * 512:th * 512 + 512],
                           gwt_s[:, fc * 8:(fc + 1) * 8],
                           XN[:, fc * 1024 + th * 512: fc * 1024 + th * 512 + 512],
                           fc == 0, fc == NFC - 1)
                mv = TINC()
                P.wait("s", S["mm"], mv)
                SACT(gate[0:8, G_S:G_S + 512], PP[3][0:8, 0:512], AF.Sigmoid)
                REL("s", 6)
                SACT(gate[0:8, G_S + 512:G_S + 1024], PP[3][0:8, 512:1024], AF.Sigmoid)
                REL("s", 7)
                gate_pb = (pb[6].n, pb[7].n)
                return apply_ev, gate_pb, amkl
            return apply_ev, None, amkl

        def emit_topk(gate_pb):
            P.wait("v", pb[6], gate_pb[0])
            P.wait("v", pb[7], gate_pb[1])
            V("tensor_scalar", gate[0:8, G_BS:G_BS + 1024], gate[0:8, G_S:G_S + 1024],
              gate[0:8, G_GB:G_GB + 1], None, ALU.add)
            bs_ev = VINC()
            for tc in range(NTC):
                pe_acq(6)
                P.wait("t", S["ev"], bs_ev)
                TTR(ph(6)[:, 0:8], gate[0:8, G_BS + tc * 128:G_BS + tc * 128 + 128], np_=8)
                TTR(ph(6)[:, 8:16], gate[0:8, G_S + tc * 128:G_S + tc * 128 + 128], np_=8)
                mv = TINC()
                P.wait("v", S["mm"], mv)
                V("tensor_copy", topk[:, 0:8], ph(6)[:, 0:8])
                V("tensor_copy", topk[:, 8:16], ph(6)[:, 8:16])
                REL("v", 6)
                VD("reduce_max", topk[:, 32:33], topk[:, 0:8], AXX)
                VD("tensor_single_scalar", topk[:, 16:24], topk[:, 0:8],
                   topk[:, 32:33], ALU.is_equal)
                VD("scalar_tensor_tensor", topk[:, 24:32], topk[:, 16:24],
                   -1e30, topk[:, 0:8], ALU.mult, ALU.add)
                VD("reduce_max", topk[:, 33:34], topk[:, 24:32], AXX)
                VD("tensor_single_scalar", topk[:, 24:32], topk[:, 24:32],
                   topk[:, 33:34], ALU.is_equal)
                VD("tensor_add", topk[:, 16:24], topk[:, 16:24], topk[:, 24:32])
                VD("tensor_mul", topk[:, 8:16], topk[:, 8:16], topk[:, 16:24])
                VD("reduce_sum", topk[:, 34:35], topk[:, 8:16], AXX)
                VD("reciprocal", topk[:, 35:36], topk[:, 34:35])
                VD("tensor_single_scalar", topk[:, 40:48], topk[:, 8:16],
                   topk[:, 35:36], ALU.mult)
                we_ev = VINC()
                pe_acq(7)
                P.wait("t", S["ev"], we_ev)
                TTR(ph(7)[0:8, 0:128], topk[:, 40:48])
                mv2 = TINC()
                P.wait("v", S["mm"], mv2)
                V("tensor_copy", w8sb[0:8, tc * 128:(tc + 1) * 128], ph(7)[0:8, 0:128])
                REL("v", 7)
            return VINC()

        def emit_proj_residual2(wdram, rhs16, bias_base, rhs_ev, rhs_ev_b=None):
            xev = {}
            for sl in range(8):
                sv, wv = SLAB(wdram[:, sl * 256:(sl + 1) * 256], 16, 256)
                P.wait("t", wv[0], wv[1])
                P.wait("t", S["ev"], rhs_ev)
                for oc2 in range(2):
                    oc = sl * 2 + oc2
                    for th in range(2):
                        hp = (oc2 * 2 + th) % 4
                        pe_acq(hp)
                        for fc in range(NFC):
                            if fc == 8 and rhs_ev_b is not None and sl == 0:
                                P.wait("t", S["ev"], rhs_ev_b)
                            MM(ph(hp),
                               sv[:, fc * 256 + oc2 * 128: fc * 256 + oc2 * 128 + 128],
                               rhs16[:, fc * 1024 + th * 512: fc * 1024 + th * 512 + 512],
                               fc == 0, fc == NFC - 1)
                        mv = TINC()
                        P.wait("v", S["mm"], mv)
                        xsl = oc * 1024 + th * 512
                        op = ((oc * 2 + th) % 2) * 512
                        V("tensor_scalar", otmp[:, op:op + 512], ph(hp),
                          bmisc[:, bias_base + oc:bias_base + oc + 1], None, ALU.add)
                        REL("v", hp)
                        VD("tensor_add", X_[:, xsl:xsl + 512], X_[:, xsl:xsl + 512],
                           otmp[:, op:op + 512])
                        if th == 1:
                            xev[oc] = VINC()
                SLAB_DONE()
            return [xev[oc] for oc in range(16)], S["mm"].n

        def emit_attn_tail(variant, qt, Ostag, obias_ap, sumv_last):
            oh = 4 + (qt % 2)
            pe_acq(6)
            P.wait("t", S["sum"], sumv_last)
            sq = (qt % 2) * 512
            MM(ph(6)[0:1, 0:512], misc[:, MC_ONES:MC_ONES + 1],
               sumac[:, sq:sq + 512], True, False)
            MM(ph(6)[0:1, 0:512], misc[:, MC_ONES:MC_ONES + 1],
               sum2[:, sq:sq + 512], False, True)
            mv = TINC()
            P.wait("v", S["mm"], mv)
            V("reciprocal", srowA, ph(6)[0:1, 0:512])
            REL("v", 6)
            rec_pb6 = pb[6].n
            pe_acq(7)
            P.wait("t", pb[6], rec_pb6)
            MM(ph(7), onesr[:, :], srowA, True, True)
            mv2 = TINC()
            P.wait("v", S["mm"], mv2)
            V("tensor_copy", recB[:, :], ph(7))
            REL("v", 7)
            ot = (qt % 2) * 512
            VD("tensor_mul", otmp[:, ot:ot + 512], ph(oh), recB[:, :])
            REL("v", oh)
            VD("tensor_scalar", Ostag[:, qt * 512:(qt + 1) * 512],
               otmp[:, ot:ot + 512], obias_ap, None, ALU.add)
            return VINC()

        def emit_attn_a_unit(Qv, Kv, Vtile, Ostag, obias_ap, slot, rrow,
                             load_val, load_sem):
            sched = []
            for qt in range(8):
                nkc = 4 * (qt + 1)
                for kc in range(nkc):
                    sched.append((qt, kc, nkc))
            njj = len(sched)
            expv = {}
            avv = {}
            sumv = {}
            tail_cast = {}
            LAG = 3

            def emit_S(j):
                qt, kc, nkc = sched[j]
                sb = j % 4
                if pbu[sb] > 0:
                    P.wait("t", pb[sb], pbu[sb])
                pbu[sb] += 1
                if j == 0 and load_val is not None:
                    P.wait("t", load_sem, load_val)
                MM(ph(sb), Kv[:, kc * 128:(kc + 1) * 128],
                   Qv[:, qt * 512:(qt + 1) * 512], True, False)
                MM(ph(sb), onesrb[:, :], rrow[0:1, qt * 512:(qt + 1) * 512],
                   False, True)
                smm = TINC()
                if kc >= 4 * qt:
                    dvar = kc - 4 * qt
                    P.wait("v", S["mm"], smm)
                    VD("tensor_add", ph(sb), ph(sb),
                       WS[:, 12288 + dvar * 512:12288 + (dvar + 1) * 512])
                    sev = VINC()
                    P.wait("s", S["ev"], sev)
                else:
                    P.wait("s", S["mm"], smm)
                p_i = j % 4
                if j >= 4:
                    P.wait("s", S["px"], avv[j - 4])
                    P.wait("s", S["sum"], sumv[j - 4])
                SACT_ND(PT[:, p_i * 512:(p_i + 1) * 512], ph(sb), AF.Exp,
                        bias=kposc[:, kc * 2 + slot:kc * 2 + slot + 1])
                expv[j] = P.inc_last("s", pb[sb], 1)
                P.wait("v", pb[sb], expv[j])
                sbuf_acc = sumac if kc % 2 == 0 else sum2
                sq = (qt % 2) * 512
                if kc < 2:
                    V("tensor_copy", sbuf_acc[:, sq:sq + 512],
                      PT[:, p_i * 512:(p_i + 1) * 512])
                else:
                    V("tensor_add", sbuf_acc[:, sq:sq + 512], sbuf_acc[:, sq:sq + 512],
                      PT[:, p_i * 512:(p_i + 1) * 512])
                sumv[j] = P.inc_last("v", S["sum"], 1)

            def emit_AV(j):
                qt, kc, nkc = sched[j]
                p_i = j % 4
                oh = 4 + (qt % 2)
                if kc == 0:
                    pe_acq(oh)
                P.wait("t", pb[j % 4], expv[j])
                MM(ph(oh), Vtile(kc), PT[:, p_i * 512:(p_i + 1) * 512],
                   kc == 0, kc == nkc - 1)
                avv[j] = P.inc_last("t", S["px"], 1)
                if kc == nkc - 1:
                    tail_cast[qt] = emit_attn_tail("a", qt, Ostag, obias_ap, sumv[j])

            for j in range(njj):
                emit_S(j)
                if j >= LAG:
                    emit_AV(j - LAG)
            for j in range(njj - LAG, njj):
                emit_AV(j)
            return tail_cast[7]

        def emit_attn_b_unit(Qv, Kv, Vtile, Ostag, obias_ap, load_val, load_sem):
            pairs = []
            for qt in range(8):
                for kp in range(16):
                    pairs.append((qt, 2 * kp, 32))
            npj = len(pairs)
            pexp = {}
            avv = {}
            sumv = {}
            tail_cast = {}
            PLAG = 1

            def emit_Spair(jp):
                qt, kc0, nkc = pairs[jp]
                pp = jp % 2
                g = pb[2 * pp]
                if pbu[2 * pp] > 0:
                    P.wait("t", g, pbu[2 * pp])
                pbu[2 * pp] += 1
                pbu[2 * pp + 1] += 0
                if jp == 0 and load_val is not None:
                    P.wait("t", load_sem, load_val)
                for h in range(2):
                    kc = kc0 + h
                    MM(ph(2 * pp + h), Kv[:, kc * 128:(kc + 1) * 128],
                       Qv[:, qt * 512:(qt + 1) * 512], True, True)
                smm = TINC()
                P.wait("s", S["mm"], smm)
                if jp >= 2:
                    jlast_old = chunk_index(jp - 2) + 1
                    P.wait("s", S["px"], avv[jlast_old])
                    P.wait("s", S["sum"], sumv[jlast_old])
                SACT_ND(PT[:, (jp % 2) * 1024:(jp % 2) * 1024 + 1024],
                        PP[pp][:, 0:1024], AF.Exp)
                pexp[jp] = P.inc_last("s", g, 1)
                P.wait("v", g, pexp[jp])
                for h in range(2):
                    kc = kc0 + h
                    j = chunk_index(jp) + h
                    sbuf_acc = sumac if kc % 2 == 0 else sum2
                    sq = (qt % 2) * 512
                    pt_off = (jp % 2) * 1024 + h * 512
                    if kc < 2:
                        V("tensor_copy", sbuf_acc[:, sq:sq + 512],
                          PT[:, pt_off:pt_off + 512])
                    else:
                        V("tensor_add", sbuf_acc[:, sq:sq + 512],
                          sbuf_acc[:, sq:sq + 512], PT[:, pt_off:pt_off + 512])
                    sumv[j] = P.inc_last("v", S["sum"], 1)

            def chunk_index(jp):
                return 2 * jp

            def emit_AVpair(jp):
                qt, kc0, nkc = pairs[jp]
                pp = jp % 2
                oh = 4 + (qt % 2)
                for h in range(2):
                    kc = kc0 + h
                    j = chunk_index(jp) + h
                    if kc == 0:
                        pe_acq(oh)
                    P.wait("t", pb[2 * pp], pexp[jp])
                    pt_off = (jp % 2) * 1024 + h * 512
                    MM(ph(oh), Vtile(kc), PT[:, pt_off:pt_off + 512],
                       kc == 0, kc == nkc - 1)
                    avv[j] = P.inc_last("t", S["px"], 1)
                    if kc == nkc - 1:
                        tail_cast[qt] = emit_attn_tail("b", qt, Ostag, obias_ap,
                                                       sumv[j])

            for jp in range(npj):
                emit_Spair(jp)
                if jp >= PLAG:
                    emit_AVpair(jp - PLAG)
            for jp in range(npj - PLAG, npj):
                emit_AVpair(jp)
            return tail_cast[7]

        P.wait("v", S["ev"], X_EV)
        ln1_ev, _, ln1_amk = emit_ln(with_gate=False, xfc_ev=xfc1)

        piece_st = []
        SLORD = [(d, p) for p in (1, 2, 0) for d in range(8)]
        for sl in range(24):
            d, part = SLORD[sl]
            sv, wv = SLAB(wqkv[:, (d * 3 + part) * 256:(d * 3 + part) * 256 + 256],
                          16, 256)
            pbv = WS[:, (sl % 2) * 2048:(sl % 2) * 2048 + 2048]
            if sl >= 2:
                P.wait("v", S["st"], piece_st[sl - 2])
            P.wait("t", wv[0], wv[1])
            if sl > 0:
                P.wait("t", S["ev"], ln1_ev)
            if part != 2:
                mcb = MC_QKB + (16 if part == 1 else 0) + d * 2
                sc = SCALE_A if part == 0 else 1.0
                for half in range(2):
                    for tc2 in range(2):
                        hp = (half * 2 + tc2) % 4
                        pe_acq(hp)
                        for fc in range(NFC):
                            if sl == 0 and half == 0 and tc2 == 0:
                                P.wait("t", S["ev"], ln1_amk[fc])
                            MM(ph(hp),
                               sv[:, fc * 256 + half * 128: fc * 256 + half * 128 + 128],
                               XN[:, fc * 1024 + tc2 * 512: fc * 1024 + tc2 * 512 + 512],
                               fc == 0, fc == NFC - 1)
                        mv = TINC()
                        P.wait("v", S["mm"], mv)
                        V("tensor_scalar",
                          pbv[:, half * 1024 + tc2 * 512: half * 1024 + tc2 * 512 + 512],
                          ph(hp), sc, bmisc[:, mcb + half:mcb + half + 1],
                          ALU.mult, ALU.add)
                        REL("v", hp)
                pev = VINC()
                SLAB_DONE()
                P.wait("y", S["ev"], pev)
                stv = piece_st[-1] if piece_st else 0
                for half in range(2):
                    if part == 1:
                        dst = cc_ak_in[d * 256 + half * 128:d * 256 + half * 128 + 128, :]
                    else:
                        dst = cc_aq_in[half][d * 128:(d + 1) * 128, :]
                    stv = STD(dst, pbv[:, half * 1024:half * 1024 + 1024], chain=stv)
                piece_st.append(stv)
            else:
                for tc in range(NTC):
                    hp = tc % 4
                    pe_acq(hp)
                    for fc in range(NFC):
                        MM(ph(hp)[:, 0:256],
                           XN[:, fc * 1024 + tc * 128: fc * 1024 + tc * 128 + 128],
                           sv[:, fc * 256:(fc + 1) * 256], fc == 0, fc == NFC - 1)
                    mv = TINC()
                    P.wait("v", S["mm"], mv)
                    V("tensor_copy", pbv[:, tc * 256:(tc + 1) * 256], ph(hp)[:, 0:256])
                    REL("v", hp)
                pev = VINC()
                SLAB_DONE()
                P.wait("y", S["ev"], pev)
                stv = STD(cc_av_in[d * 1024:(d + 1) * 1024, 0:256].rearrange(
                              "(tc p) o -> p tc o", p=128),
                          pbv.rearrange("p (tc o) -> p tc o", o=256),
                          chain=piece_st[-1] if piece_st else 0)
                piece_st.append(stv)
            if sl == 9:
                P.add("g", lambda g, v=piece_st[7]: g.wait_ge(S["st"].h, v))
                cc1k = COLL("AllToAll", cc_ak_in, cc_ak_out)
            if sl == 17:
                P.add("g", lambda g, v=piece_st[15]: g.wait_ge(S["st"].h, v))
                cc1v = COLL("AllToAll", cc_av_in, cc_av_out)
        P.add("g", lambda g: g.wait_ge(S["st"].h, piece_st[-1]))
        cc1q = [COLL("AllToAll", cc_aq_in[i], cc_aq_out[i]) for i in range(2)]
        qkv_mm_end = S["mm"].n

        P.wait("y", S["mm"], qkv_mm_end)
        ostore = []
        px_end = []

        def emit_a_loads(u):
            beta, slot = u % 2, u // 2
            buf = XN if u % 2 == 0 else WS
            vb = 8192 if u % 2 == 0 else 20480
            lsem = S[f"lx{u % 2}"]
            if u == 0:
                P.wait("y", S["cc"], cc1k)
            for j in range(4):
                r = (beta * 4 + j) * 256 + slot * 128
                SYD(buf[:, 4096 + j * 1024:4096 + (j + 1) * 1024],
                    cc_ak_out[r:r + 128, :], sem=lsem)
            if u == 0:
                P.wait("y", S["cc"], cc1v)
            SYD(buf[:, vb:vb + 4096].rearrange("p (kc h) -> p kc h", h=128),
                cc_av_out[beta * 4096:(beta + 1) * 4096,
                          slot * 128:slot * 128 + 128].rearrange(
                              "(kc p) h -> p kc h", p=128), sem=lsem)
            P.wait("y", S["cc"], cc1q[slot])
            lv = None
            for j in range(4):
                r = (beta * 4 + j) * 128
                lv = SYD(buf[:, j * 1024:(j + 1) * 1024],
                         cc_aq_out[slot][r:r + 128, :], sem=lsem)
            return lv

        lv_next = emit_a_loads(0)
        for u in range(4):
            beta, slot = u % 2, u // 2
            lv = lv_next
            buf = XN if u % 2 == 0 else WS
            vb = 8192 if u % 2 == 0 else 20480
            obias = bmisc[:, MC_BVA + slot:MC_BVA + slot + 1]
            rrow = WS[0:1, 16384:20480]
            P.wait("v", S["px"], S["px"].n)
            V("tensor_scalar", rrow, WS[0:1, 8192:12288],
              slopes[0:1, slot * 128:slot * 128 + 1], -1.0, ALU.mult, ALU.mult)
            rr_ev = VINC()
            P.wait("t", S["ev"], rr_ev)
            cast_last = emit_attn_a_unit(
                buf[:, 0:4096], buf[:, 4096:8192],
                lambda kc, b=buf, vo=vb: b[:, vo + kc * 128:vo + (kc + 1) * 128],
                XN[:, 12288:16384], obias, slot, rrow, lv, S[f"lx{u % 2}"])
            px_end.append(S["px"].n)
            if u < 3:
                if u >= 1:
                    P.wait("y", S["px"], px_end[u - 1])
                lv_next = emit_a_loads(u + 1)
            P.wait("y", S["ev"], cast_last)
            for d2 in range(4):
                dd = beta * 4 + d2
                stv = STD(cc_ao_in[slot][dd * 128:(dd + 1) * 128, :],
                          XN[:, 12288 + d2 * 1024:12288 + (d2 + 1) * 1024])
                ostore.append(stv)
            P.wait("v", S["st"], ostore[-1])
            if u == 1:
                P.add("g", lambda g, v=ostore[-1]: g.wait_ge(S["st"].h, v))
                cc2a = COLL("AllToAll", cc_ao_in[0], cc_ao_out[0])
        P.add("g", lambda g: g.wait_ge(S["st"].h, ostore[-1]))
        cc2b = COLL("AllToAll", cc_ao_in[1], cc_ao_out[1])

        P.wait("y", S["cc"], cc2a)
        olv = None
        oev = {}
        P.wait("y", S["px"], px_end[2])
        for fc in range(NFC):
            if fc == 8:
                P.wait("y", S["ld"], olv)
                P.wait("v", S["ld"], olv)
                oev[0] = VINC()
                P.wait("y", S["cc"], cc2b)
            olv = SYD(XN[:, fc * 1024:(fc + 1) * 1024],
                      cc_ao_out[fc // 8][(fc % 8) * 128:(fc % 8 + 1) * 128, :])
        P.wait("v", S["ld"], olv)
        oev[1] = VINC()
        xev_pa, pa_mm = emit_proj_residual2(wproj, XN, MC_BPROJ, oev[0], oev[1])

        ln2_ev, gate_pb2, ln2_amk = emit_ln(with_gate=True, xfc_ev=xev_pa,
                                            xn_guard_mm=pa_mm)
        topk_ev = emit_topk(gate_pb2)

        def emit_ffn(ln_ev, tk_ev, out_cb=None, ln_amk=None):
            hmul_ev = {0: 0, 1: 0}
            for j in range(22):
                sv, wv = SLAB(sgw[:, j * 256:(j + 1) * 256], 16, 256)
                P.wait("t", wv[0], wv[1])
                if j > 0 or ln_amk is None:
                    P.wait("t", S["ev"], ln_ev)
                pp = j % 2
                for half, pr in ((0, 0), (1, 1)):
                    pe_acq(pr * 2)
                    pe_acq(pr * 2 + 1)
                    for th in range(2):
                        for fc in range(NFC):
                            if j == 0 and half == 0 and th == 0 and ln_amk:
                                P.wait("t", S["ev"], ln_amk[fc])
                            MM(PP[pr][:, th * 512:th * 512 + 512],
                               sv[:, fc * 256 + half * 128: fc * 256 + half * 128 + 128],
                               XN[:, fc * 1024 + th * 512: fc * 1024 + th * 512 + 512],
                               fc == 0, fc == NFC - 1)
                    TINC()
                mvy = S["mm"].n - 1
                mvg = S["mm"].n
                P.wait("v", S["mm"], mvy)
                V("tensor_copy", WS[:, j * 1024:j * 1024 + 512], PP[0][:, 0:512])
                REL("v", 0)
                V("tensor_copy", WS[:, j * 1024 + 512:(j + 1) * 1024], PP[0][:, 512:1024])
                REL("v", 1)
                P.wait("s", S["mm"], mvg)
                if hmul_ev[pp] > 0:
                    P.wait("s", S["ev"], hmul_ev[pp])
                SACT(WS[:, 22528 + pp * 1024:22528 + pp * 1024 + 512],
                     PP[1][:, 0:512], AF.Silu)
                REL("s", 2)
                SACT(WS[:, 22528 + pp * 1024 + 512:22528 + (pp + 1) * 1024],
                     PP[1][:, 512:1024], AF.Silu)
                REL("s", 3)
                gse_pb = pb[3].n
                P.wait("v", pb[3], gse_pb)
                VD("tensor_mul", WS[:, j * 1024:(j + 1) * 1024],
                   WS[:, j * 1024:(j + 1) * 1024],
                   WS[:, 22528 + pp * 1024:22528 + (pp + 1) * 1024])
                hmul_ev[pp] = VINC()
                SLAB_DONE()
            hsh_ev = VINC()
            for cs in range(8):
                svs = []
                for part in range(2):
                    sv, wv = SLAB(sdw[part * 1408:(part + 1) * 1408,
                                      cs * 256:(cs + 1) * 256], 11, 256)
                    svs.append((sv, wv))
                P.wait("t", svs[0][1][0], svs[0][1][1])
                P.wait("t", svs[1][1][0], svs[1][1][1])
                P.wait("t", S["ev"], hsh_ev)
                for oc2 in range(2):
                    oc = cs * 2 + oc2
                    for th in range(2):
                        hp = (oc2 * 2 + th) % 4
                        pe_acq(hp)
                        for part in range(2):
                            for l in range(11):
                                j = part * 11 + l
                                MM(ph(hp),
                                   svs[part][0][:, l * 256 + oc2 * 128: l * 256 + oc2 * 128 + 128],
                                   WS[:, j * 1024 + th * 512: j * 1024 + th * 512 + 512],
                                   part == 0 and l == 0, part == 1 and l == 10)
                        mv = TINC()
                        P.wait("v", S["mm"], mv)
                        xsl = oc * 1024 + th * 512
                        V("tensor_add", X_[:, xsl:xsl + 512], X_[:, xsl:xsl + 512], ph(hp))
                        REL("v", hp)
                SLAB_DONE()
                SLAB_DONE()
            for e in range(8):
                pe_acq(6)
                pe_acq(7)
                P.wait("t", S["ev"], tk_ev)
                MM(ph(6), onehot[0:8, e * 128:(e + 1) * 128], w8sb[0:8, 0:512], True, True)
                MM(ph(7), onehot[0:8, e * 128:(e + 1) * 128], w8sb[0:8, 512:1024], True, True)
                mv = TINC()
                P.wait("v", S["mm"], mv)
                V("tensor_copy", weB[:, 0:512], ph(6))
                REL("v", 6)
                V("tensor_copy", weB[:, 512:1024], ph(7))
                REL("v", 7)
                hmul_ev = {0: 0, 1: 0}
                xev = {}
                for j in range(11):
                    sv, wv = SLAB(upw[e, :, j * 256:(j + 1) * 256], 16, 256)
                    P.wait("t", wv[0], wv[1])
                    pp = j % 2
                    for half, pr in ((0, 0), (1, 1)):
                        pe_acq(pr * 2)
                        pe_acq(pr * 2 + 1)
                        for th in range(2):
                            for fc in range(NFC):
                                MM(PP[pr][:, th * 512:th * 512 + 512],
                                   sv[:, fc * 256 + half * 128: fc * 256 + half * 128 + 128],
                                   XN[:, fc * 1024 + th * 512: fc * 1024 + th * 512 + 512],
                                   fc == 0, fc == NFC - 1)
                        TINC()
                    mvg = S["mm"].n - 1
                    mvu = S["mm"].n
                    P.wait("s", S["mm"], mvg)
                    if hmul_ev[pp] > 0:
                        P.wait("s", S["ev"], hmul_ev[pp])
                    SACT(WS[:, 22528 + pp * 1024:22528 + pp * 1024 + 512],
                         PP[0][:, 0:512], AF.Silu)
                    REL("s", 0)
                    SACT(WS[:, 22528 + pp * 1024 + 512:22528 + (pp + 1) * 1024],
                         PP[0][:, 512:1024], AF.Silu)
                    REL("s", 1)
                    gse_pb = pb[1].n
                    P.wait("v", S["mm"], mvu)
                    VD("tensor_mul", WS[:, j * 1024:j * 1024 + 512],
                       PP[1][:, 0:512], weB[:, 0:512])
                    REL("v", 2)
                    V("tensor_mul", WS[:, j * 1024 + 512:(j + 1) * 1024],
                      PP[1][:, 512:1024], weB[:, 512:1024])
                    REL("v", 3)
                    P.wait("v", pb[1], gse_pb)
                    VD("tensor_mul", WS[:, j * 1024:(j + 1) * 1024],
                       WS[:, j * 1024:(j + 1) * 1024],
                       WS[:, 22528 + pp * 1024:22528 + (pp + 1) * 1024])
                    hmul_ev[pp] = VINC()
                    SLAB_DONE()
                he_ev = VINC()
                xn_free_mm = S["mm"].n
                for cs in range(8):
                    sv, wv = SLAB(dnw[e, :, cs * 256:(cs + 1) * 256], 11, 256)
                    P.wait("t", wv[0], wv[1])
                    P.wait("t", S["ev"], he_ev)
                    for oc2 in range(2):
                        oc = cs * 2 + oc2
                        for th in range(2):
                            hp = (oc2 * 2 + th) % 4
                            pe_acq(hp)
                            for l in range(11):
                                MM(ph(hp),
                                   sv[:, l * 256 + oc2 * 128: l * 256 + oc2 * 128 + 128],
                                   WS[:, l * 1024 + th * 512: l * 1024 + th * 512 + 512],
                                   l == 0, l == 10)
                            mv = TINC()
                            P.wait("v", S["mm"], mv)
                            xsl = oc * 1024 + th * 512
                            V("tensor_add", X_[:, xsl:xsl + 512],
                              X_[:, xsl:xsl + 512], ph(hp))
                            REL("v", hp)
                            if e == 7 and th == 1:
                                xev[oc] = VINC()
                    SLAB_DONE()
                    if e == 7 and out_cb is not None and cs in (3, 7):
                        out_cb(cs // 4, xev[cs * 2 + 1])
            return (VINC(), [xev[oc] for oc in range(16)], xn_free_mm,
                    S["mm"].n)

        ffn1_ev, xev_f1, f1_xn_mm, f1_all_mm = emit_ffn(
            ln2_ev, topk_ev, ln_amk=ln2_amk)

        ln3_ev, _, ln3_amk = emit_ln(with_gate=False, xfc_ev=xev_f1,
                                     xn_guard_mm=f1_xn_mm, sq_guard_mm=f1_all_mm)
        lat_st = []
        slab_st = {}
        for sl in range(4):
            src = wkd if sl < 2 else wqd
            c0 = (sl % 2) * 256
            sv, wv = SLAB(src[:, c0:c0 + 256], 16, 256)
            pbv = WS[:, (sl % 2) * 2048:(sl % 2) * 2048 + 2048]
            if sl >= 2:
                P.wait("v", S["st"], slab_st[sl - 2])
            P.wait("t", wv[0], wv[1])
            if sl > 0:
                P.wait("t", S["ev"], ln3_ev)
            for half in range(2):
                for tc2 in range(2):
                    hp = (half * 2 + tc2) % 4
                    pe_acq(hp)
                    for fc in range(NFC):
                        if sl == 0 and half == 0 and tc2 == 0:
                            P.wait("t", S["ev"], ln3_amk[fc])
                        MM(ph(hp),
                           sv[:, fc * 256 + half * 128: fc * 256 + half * 128 + 128],
                           XN[:, fc * 1024 + tc2 * 512: fc * 1024 + tc2 * 512 + 512],
                           fc == 0, fc == NFC - 1)
                    mv = TINC()
                    P.wait("v", S["mm"], mv)
                    V("tensor_copy",
                      pbv[:, half * 1024 + tc2 * 512: half * 1024 + tc2 * 512 + 512],
                      ph(hp))
                    REL("v", hp)
            pev = VINC()
            SLAB_DONE()
            P.wait("y", S["ev"], pev)
            dst_t = cc_bkv_in if sl < 2 else cc_bq_in
            stv = None
            for half in range(2):
                r0 = c0 + half * 128
                stv = STD(dst_t[r0:r0 + 128, :],
                          pbv[:, half * 1024:half * 1024 + 1024],
                          chain=lat_st[-1] if lat_st else 0)
                lat_st.append(stv)
            slab_st[sl] = stv
            if sl == 2:
                P.add("g", lambda g, v=slab_st[1]: g.wait_ge(S["st"].h, v))
                cc3 = COLL("AllGather", cc_bkv_in, cc_bkv_out)
        P.add("g", lambda g: g.wait_ge(S["st"].h, lat_st[-1]))
        cc3q = COLL("AllGather", cc_bq_in, cc_bq_out)
        lat_mm_end = S["mm"].n

        P.wait("y", S["cc"], cc3)
        P.wait("y", S["mm"], lat_mm_end)
        P.wait("v", S["st"], lat_st[-1])
        bostore = []
        LATB = [XN[:, 0:8192], WS[:, 16384:24576]]
        grp_mm = {}
        grp_ev = {}
        grp_ld = {}
        GROUPS = []
        for u in range(4):
            if u // 2 == 0:
                GROUPS += [(u, side, lh) for side in ("kv", "q") for lh in range(2)]
            else:
                GROUPS += [(u, side, lh) for lh in range(2) for side in ("kv", "q")]

        def emit_b_loads(gi):
            u, side, lh = GROUPS[gi]
            beta = u % 2
            latb = LATB[gi % 2]
            if gi >= 2:
                P.wait("y", S["mm"], grp_mm[gi - 2])
            src_t = cc_bkv_out if side == "kv" else cc_bq_out
            if side == "q" and gi == 2:
                P.wait("y", S["cc"], cc3q)
            lvl = None
            for l in range(4):
                for j in range(2):
                    r = (beta * 4 + lh * 2 + j) * 512 + l * 128
                    lvl = SYD(latb[:, l * 2048 + j * 1024: l * 2048 + j * 1024 + 1024],
                              src_t[r:r + 128, :], sem=S[f"lx{gi % 2}"])
            grp_ld[gi] = lvl

        emit_b_loads(0)
        emit_b_loads(1)
        for u in range(4):
            beta, slot = u % 2, u // 2
            for k in range(4):
                gi = u * 4 + k
                side, lh = GROUPS[gi][1], GROUPS[gi][2]
                latb = LATB[gi % 2]
                if slot == 1 and side == "kv":
                    if gi >= 1:
                        P.wait("g", S["ev"], grp_ev[gi - 1])
                    lvp = GPD(WS[:, 12288:14336], pa_in[:, lh * 2048:(lh + 1) * 2048])
                    lvp = GPD(WS[:, 14336:16384], pb_in[:, lh * 2048:(lh + 1) * 2048])
                else:
                    lvp = 0
                P.wait("t", S[f"lx{gi % 2}"], grp_ld[gi])
                if lvp:
                    P.wait("t", S["gp"], lvp)
                wof = {"q": 0, "kv": 1024}[side]
                dst = 8192 if side == "q" else 0
                beffc = MC_BEFFQ if side == "q" else MC_BEFFK
                bia = misc[:, beffc:beffc + 1]
                bia1 = misc[:, beffc + 1:beffc + 2]
                ncomp = 1 if slot == 0 else 2
                for ci in range(ncomp):
                    outv = (WS[:, dst + lh * 2048:dst + (lh + 1) * 2048] if slot == 0
                            else XN[:, 8192 + ci * 2048:8192 + (ci + 1) * 2048])
                    for tt in range(4):
                        hp = tt % 2
                        pe_acq(hp)
                        for l in range(4):
                            MM(ph(hp),
                               upslc[:, wof + l * 256 + ci * 128: wof + l * 256 + ci * 128 + 128],
                               latb[:, l * 2048 + tt * 512: l * 2048 + tt * 512 + 512],
                               l == 0, l == 3)
                        mv = TINC()
                        P.wait("v", S["mm"], mv)
                        bb = bia if ci == 0 else bia1
                        sca = SCALE_B if side == "q" else 1.0
                        V("tensor_scalar", outv[:, tt * 512:(tt + 1) * 512],
                          ph(hp), sca, bb, ALU.mult, ALU.add)
                        REL("v", hp)
                if slot == 1:
                    VD("tensor_mul", WS[:, dst + lh * 2048:dst + (lh + 1) * 2048],
                       XN[:, 8192:10240], WS[:, 12288:14336])
                    VD("tensor_mul", XN[:, 8192:10240],
                       XN[:, 10240:12288], WS[:, 14336:16384])
                    VD("tensor_add", WS[:, dst + lh * 2048:dst + (lh + 1) * 2048],
                       WS[:, dst + lh * 2048:dst + (lh + 1) * 2048],
                       XN[:, 8192:10240])
                if side == "kv":
                    for tq in range(4):
                        hp = 2 + (tq % 2)
                        pe_acq(hp)
                        for k in range(4):
                            tt = tq * 4 + k
                            for l in range(4):
                                MM(ph(hp)[:, k * 128:(k + 1) * 128],
                                   latb[:, l * 2048 + tt * 128: l * 2048 + tt * 128 + 128],
                                   upslc[:, 2048 + l * 256 + slot * 128:
                                         2048 + l * 256 + slot * 128 + 128],
                                   l == 0, l == 3)
                        mv = TINC()
                        P.wait("v", S["mm"], mv)
                        kc0 = lh * 16 + tq * 4
                        V("tensor_copy",
                          WS[:, 4096 + kc0 * 128:4096 + kc0 * 128 + 512],
                          ph(hp)[:, 0:512])
                        REL("v", hp)
                grp_mm[gi] = S["mm"].n
                grp_ev[gi] = VINC()
                if gi + 2 < 16:
                    emit_b_loads(gi + 2)
            bqk_ev = VINC()
            last_bqk = bqk_ev
            P.wait("t", S["ev"], bqk_ev)
            if u > 0:
                P.wait("v", S["st"], bostore[-1])
            obias = misc[:, MC_BEFFV + slot:MC_BEFFV + slot + 1]
            cast_last = emit_attn_b_unit(
                WS[:, 8192:12288], WS[:, 0:4096],
                lambda kc: WS[:, 4096 + kc * 128:4096 + (kc + 1) * 128],
                XN[:, 12288:16384], obias, None, None)
            P.wait("y", S["ev"], cast_last)
            for d2 in range(4):
                dd = beta * 4 + d2
                stv = STD(cc_bo_in[slot][dd * 128:(dd + 1) * 128, :],
                          XN[:, 12288 + d2 * 1024:12288 + (d2 + 1) * 1024])
                bostore.append(stv)
            if u == 1:
                P.add("g", lambda g, v=bostore[-1]: g.wait_ge(S["st"].h, v))
                cc4a = COLL("AllToAll", cc_bo_in[0], cc_bo_out[0])
        P.add("g", lambda g: g.wait_ge(S["st"].h, bostore[-1]))
        cc4b = COLL("AllToAll", cc_bo_in[1], cc_bo_out[1])

        P.wait("y", S["cc"], cc4a)
        olv = None
        oevb = {}
        P.wait("y", S["mm"], grp_mm[14])
        P.wait("y", S["ev"], last_bqk)
        for fc in range(NFC):
            if fc == 8:
                P.wait("y", S["ld"], olv)
                P.wait("v", S["ld"], olv)
                oevb[0] = VINC()
                P.wait("y", S["cc"], cc4b)
            olv = SYD(XN[:, fc * 1024:(fc + 1) * 1024],
                      cc_bo_out[fc // 8][(fc % 8) * 128:(fc % 8 + 1) * 128, :])
        P.wait("v", S["ld"], olv)
        oevb[1] = VINC()
        xev_pb, pb_mm = emit_proj_residual2(wo, XN, MC_BO, oevb[0], oevb[1])

        ln4_ev, gate_pb4, ln4_amk = emit_ln(with_gate=True, xfc_ev=xev_pb,
                                            xn_guard_mm=pb_mm)
        topk4_ev = emit_topk(gate_pb4)

        out_st = []

        def emit_out_half(hf, gate_ev):
            for tc in range(NTC):
                stg = xstgA if len(out_st) % 2 == 0 else xstgB
                if len(out_st) >= 2:
                    P.wait("v", S["st"], out_st[-2])
                for fh in range(2):
                    hp = 4 + fh
                    pe_acq(hp)
                    if tc == 0 and fh == 0:
                        P.wait("t", S["ev"], gate_ev)
                    for k in range(4):
                        fc = hf * 8 + fh * 4 + k
                        TTR(ph(hp)[:, k * 128:(k + 1) * 128],
                            X_[:, fc * 1024 + tc * 128: fc * 1024 + tc * 128 + 128])
                    mv = TINC()
                    P.wait("v", S["mm"], mv)
                    V("tensor_copy", stg[:, fh * 512:(fh + 1) * 512], ph(hp)[:, 0:512])
                    REL("v", hp)
                sev = VINC()
                P.wait("y", S["ev"], sev)
                stv = STD(out_t[tc * 128:(tc + 1) * 128, hf * 1024:(hf + 1) * 1024],
                          stg, chain=out_st[-1] if out_st else 0)
                out_st.append(stv)

        ffn2_ev, _, _, _ = emit_ffn(ln4_ev, topk4_ev, out_cb=emit_out_half,
                                    ln_amk=ln4_amk)
        P.add("y", lambda y: y.wait_ge(S["st"].h, out_st[-1]))

        block.tensor(lambda e: [f(e) for f in P.q["t"]])
        block.vector(lambda e: [f(e) for f in P.q["v"]])
        block.scalar(lambda e: [f(e) for f in P.q["s"]])
        block.gpsimd(lambda e: [f(e) for f in P.q["g"]])
        block.sync(lambda e: [f(e) for f in P.q["y"]])

    return nc


def _prep_inputs(ins):
    f32 = np.float32
    xfull = np.ascontiguousarray(ins["L0_attn_a_hidden_states"], dtype=f32)

    perm = []
    for d in range(8):
        for part in range(3):
            for hh in (d, d + 8):
                base = part * H + hh * 128
                perm.extend(range(base, base + 128))
    perm = np.array(perm)
    wqkv_p = np.ascontiguousarray(ins["L0_attn_a_w_qkv"][:, perm], dtype=f32)
    bqkv_p = np.ascontiguousarray(ins["L0_attn_a_b_qkv"][perm], dtype=f32)

    operm = np.arange(2048)
    wproj_p = np.ascontiguousarray(ins["L0_attn_a_w_proj"][operm, :], f32)
    wo_p = np.ascontiguousarray(ins["L0_attn_b_Wo"].T[operm, :], f32)

    wqd_t = np.ascontiguousarray(ins["L0_attn_b_Wq_down"].T, f32)
    wkd_t = np.ascontiguousarray(ins["L0_attn_b_Wkv_down"].T, f32)

    sperm = []
    for j in range(22):
        sperm.extend(range(j * 128, (j + 1) * 128))
        sperm.extend(range(2816 + j * 128, 2816 + (j + 1) * 128))
    sperm = np.array(sperm)
    sgw_ti = np.ascontiguousarray(ins["L0_ffn_shared_gw"].T[:, sperm], f32)
    sdw_t = np.ascontiguousarray(ins["L0_ffn_shared_dw"].T, f32)
    uperm = []
    for j in range(11):
        uperm.extend(range(j * 128, (j + 1) * 128))
        uperm.extend(range(1408 + j * 128, 1408 + (j + 1) * 128))
    uperm = np.array(uperm)
    upw_ti = np.ascontiguousarray(ins["L0_ffn_up_w"].transpose(0, 2, 1)[:, :, uperm], f32)
    dnw_t = np.ascontiguousarray(ins["L0_ffn_down_w"].transpose(0, 2, 1), f32)
    gwt_t = np.ascontiguousarray(ins["L0_ffn_gate_w"].T, f32)

    ident = np.eye(128, dtype=f32)
    ones_row = np.ones((1, 128), f32)
    ones_colb = np.ones((128, 1), bfnp)
    onehot8 = np.zeros((8, 1024), bfnp)
    for e in range(8):
        onehot8[e, e * 128:(e + 1) * 128] = 1
    qpos_bf = np.arange(SEQ, dtype=f32)[None, :].astype(bfnp)
    slopes = _alibi_slopes(NH)
    cos64, sin64 = _rope_cos_sin(SEQ, H // 4)

    dm = np.zeros((128, 2048), np.float64)
    for dvar in range(4):
        pcol = dvar * 128 + np.arange(128)[:, None]
        fcol = np.arange(512)[None, :]
        dm[:, dvar * 512:(dvar + 1) * 512] = np.where(pcol > fcol, -30000.0, 0.0)
    diag_masks = dm.astype(bfnp)

    gate_bias_p = np.asarray(ins["L0_ffn_gate_bias"], f32).reshape(8, 1)

    base = dict(
        wqkv_p=wqkv_p, wproj_p=wproj_p, wo_p=wo_p, wqd_t=wqd_t, wkd_t=wkd_t,
        shared_gw_ti=sgw_ti, shared_dw_t=sdw_t, up_w_ti=upw_ti, down_w_t=dnw_t,
        gate_w_t=gwt_t, gate_bias_p=gate_bias_p,
        ident_in=ident, ones_row=ones_row, ones_colb=ones_colb, onehot8=onehot8,
        qpos_bf=qpos_bf, diag_masks=diag_masks,
    )

    in_maps = []
    for c in range(8):
        m = dict(base)
        beta, r = c // 4, c % 4
        m["x_slice"] = np.ascontiguousarray(xfull[beta, r * TOK:(r + 1) * TOK, :], f32)
        hA, hB = c, c + 8
        mm = c % 4
        bm = np.zeros((128, 80), f32)
        bm[:, MCB("BPROJ"):MCB("BPROJ") + 16] = np.asarray(
            ins["L0_attn_a_b_proj"], f32).reshape(16, 128).T
        bm[:, MCB("BO"):MCB("BO") + 16] = np.asarray(
            ins["L0_attn_b_bo"], f32).reshape(16, 128).T
        bq = np.asarray(ins["L0_attn_a_b_qkv"], f32)
        bm[:, MCB("BVA") + 0] = bq[2 * H + hA * 128: 2 * H + (hA + 1) * 128]
        bm[:, MCB("BVA") + 1] = bq[2 * H + hB * 128: 2 * H + (hB + 1) * 128]
        for d in range(8):
            for half in range(2):
                hh = d + 8 * half
                bm[:, 48 + d * 2 + half] = bq[hh * 128:(hh + 1) * 128]
                bm[:, 64 + d * 2 + half] = bq[2048 + hh * 128: 2048 + (hh + 1) * 128]
        bm[:, MCB("BQD"):MCB("BQD") + 4] = np.asarray(
            ins["L0_attn_b_bq_down"], f32).reshape(4, 128).T
        bm[:, MCB("BKD"):MCB("BKD") + 4] = np.asarray(
            ins["L0_attn_b_bkv_down"], f32).reshape(4, 128).T
        blk_c = mm if c < 4 else mm + 4
        blk_o = mm + 4 if c < 4 else mm
        qcols = list(range(blk_c * 128, (blk_c + 1) * 128)) + \
            list(range(blk_o * 128, (blk_o + 1) * 128))
        m["wqu_s"] = np.ascontiguousarray(ins["L0_attn_b_Wq_up"].T[:, qcols], f32)
        m["wku_s"] = np.ascontiguousarray(ins["L0_attn_b_Wk_up"].T[:, qcols], f32)
        vcols = list(range(hA * 128, (hA + 1) * 128)) + \
            list(range(hB * 128, (hB + 1) * 128))
        m["wvu_s"] = np.ascontiguousarray(ins["L0_attn_b_Wv_up"].T[:, vcols], f32)
        bqu = np.asarray(ins["L0_attn_b_bq_up"], f32)
        bku = np.asarray(ins["L0_attn_b_bk_up"], f32)
        bvu = np.asarray(ins["L0_attn_b_bv_up"], f32)
        bm[:, MCB("BQU") + 0] = bqu[blk_c * 128:(blk_c + 1) * 128]
        bm[:, MCB("BQU") + 1] = bqu[blk_o * 128:(blk_o + 1) * 128]
        bm[:, MCB("BKU") + 0] = bku[blk_c * 128:(blk_c + 1) * 128]
        bm[:, MCB("BKU") + 1] = bku[blk_o * 128:(blk_o + 1) * 128]
        bm[:, MCB("BVU") + 0] = bvu[hA * 128:(hA + 1) * 128]
        bm[:, MCB("BVU") + 1] = bvu[hB * 128:(hB + 1) * 128]
        m["bias_misc"] = bm
        sl2 = np.zeros((1, 256), np.float64)
        sl2[0, 0:128] = slopes[hA]
        sl2[0, 128:256] = slopes[hB]
        m["slope_row2"] = sl2.astype(f32)
        kpc = np.zeros((128, 64), np.float64)
        for kc in range(32):
            pos = kc * 128 + np.arange(128)
            kpc[:, kc * 2 + 0] = slopes[hA] * pos
            kpc[:, kc * 2 + 1] = slopes[hB] * pos
        m["kposc"] = kpc.astype(f32)
        pa = cos64[:, mm * 128:(mm + 1) * 128].T
        sgn = -1.0 if c < 4 else 1.0
        pbp = sgn * sin64[:, mm * 128:(mm + 1) * 128].T
        m["ropePA"] = np.ascontiguousarray(pa, f32)
        m["ropePB"] = np.ascontiguousarray(pbp, f32)
        in_maps.append(m)
    return in_maps


def MCB(name):
    return {"BPROJ": 0, "BO": 16, "BVA": 32, "BQD": 34, "BKD": 38,
            "BQU": 42, "BKU": 44, "BVU": 46}[name]


_PROG_CACHE = {}


def kernel(**inputs):
    in_maps = _prep_inputs(inputs)
    if "nc" not in _PROG_CACHE:
        _PROG_CACHE["nc"] = build_program()
    nc = _PROG_CACHE["nc"]
    with allow_ici_all_to_all():
        res = run_bass_kernel_spmd(nc, in_maps, core_ids=list(range(8)))
    out = np.zeros((B, SEQ, H), np.float32)
    for c in range(8):
        beta, r = c // 4, c % 4
        out[beta, r * TOK:(r + 1) * TOK, :] = np.asarray(res.results[c]["out_slice"])
    return out
